# version 14
# speedup vs baseline: 1.1780x; 1.0005x over previous
import numpy as np
import ml_dtypes
import concourse.bass as bass
import concourse.mybir as mybir
from concourse.bass_utils import run_bass_kernel_spmd

F32 = mybir.dt.float32
BF16 = mybir.dt.bfloat16
I32 = mybir.dt.int32
AF = mybir.ActivationFunctionType
ALU = mybir.AluOpType
AX = mybir.AxisListType
BF16NP = ml_dtypes.bfloat16

T = 4096
D = 1024
NEG = -1e30

ENGINES = ("tensor", "vector", "scalar", "sync", "gpsimd")
DMA_ENGINES = ("sync", "gpsimd", "scalar")



class Op:
    __slots__ = ("idx", "engine", "emit", "reads", "writes", "is_dma", "deps",
                 "need_inc", "inc_idx", "gid", "waits", "snap", "tag", "lowp")

    def __init__(self):
        self.need_inc = False
        self.inc_idx = 0
        self.gid = -1
        self.waits = None
        self.snap = None
        self.tag = ""
        self.lowp = False


class Prog:

    def __init__(self, nc, serial=False):
        self.nc = nc
        self.ops = []
        self.key_state = {}
        self.serial = serial
        self.groups = {}
        self.group_batch = {}
        self.group_consumed = {}

    def op(self, engine, emit, reads=(), writes=(), is_dma=False, tag="",
           group=None, lowp=False):
        idx = len(self.ops)
        o = Op()
        o.lowp = lowp
        o.idx = idx
        o.engine = engine
        o.emit = emit
        o.reads = tuple(reads)
        o.writes = tuple(writes)
        o.is_dma = is_dma
        o.tag = tag
        if is_dma:
            gname = group if group is not None else (
                o.writes[0] if o.writes else f"anon{idx}")
            gname = f"{gname}@{engine}"
            if gname not in self.groups:
                self.groups[gname] = len(self.groups)
            o.gid = self.groups[gname]
        deps = set()
        ops = self.ops

        def keep(d):
            od = ops[d]
            return not (od.engine == engine == "tensor"
                        and not od.is_dma and not is_dma)

        for k in o.reads:
            st = self.key_state.get(k)
            if st is None:
                self.key_state[k] = [None, [idx]]
            else:
                lw = st[0]
                if lw is not None and keep(lw):
                    deps.add(lw)
                st[1].append(idx)
        for k in o.writes:
            st = self.key_state.get(k)
            if st is not None:
                lw = st[0]
                if lw is not None and keep(lw):
                    deps.add(lw)
                for r in st[1]:
                    if r != idx and keep(r):
                        deps.add(r)
            self.key_state[k] = [idx, []]
        if self.serial and idx > 0:
            deps.add(idx - 1)
        o.deps = deps
        ops.append(o)
        return idx

    def dma(self, engine, emit, reads=(), writes=(), tag="", group=None):
        return self.op(engine, emit, reads, writes, is_dma=True, tag=tag,
                       group=group)

    def _schedule(self):
        ops = self.ops
        ngroups = len(self.groups)
        inc_count = {e: 0 for e in ENGINES}
        for o in ops:
            for d in o.deps:
                ops[d].need_inc = True
        for o in ops:
            if not o.is_dma and o.need_inc:
                inc_count[o.engine] += 1
                o.inc_idx = inc_count[o.engine]

        know = {e: {} for e in ENGINES}
        cum_dma = {}
        gcount = [0] * ngroups
        cum_inc = {e: {} for e in ENGINES}
        pos_inc = {e: 0 for e in ENGINES}

        def merge(dst, src):
            for k, v in src.items():
                if dst.get(k, -1) < v:
                    dst[k] = v

        for o in ops:
            e = o.engine
            kn = know[e]
            waits = []
            for d in sorted(o.deps):
                od = ops[d]
                if od.is_dma:
                    g = od.gid
                    need = gcount[g]
                    key = f"d:{g}"
                    if kn.get(key, 0) < need:
                        waits.append(("dma", g, need))
                        kn[key] = need
                        merge(kn, cum_dma.get(g, {}))
                    self.group_consumed[g] = True
                else:
                    key = "i:" + od.engine
                    if kn.get(key, 0) < od.inc_idx:
                        waits.append(("inc", od.engine, od.inc_idx))
                        kn[key] = od.inc_idx
                        merge(kn, od.snap)
            o.waits = waits
            if o.is_dma:
                g = o.gid
                gcount[g] += 1
                snap = dict(kn)
                snap[f"d:{g}"] = gcount[g]
                cs = cum_dma.setdefault(g, {})
                merge(cs, snap)
                o.snap = cs
            else:
                if o.need_inc:
                    pos_inc[e] = o.inc_idx
                snap = dict(kn)
                snap["i:" + e] = pos_inc[e]
                if o.need_inc:
                    merge(cum_inc[e], snap)
                    o.snap = dict(cum_inc[e])
                else:
                    o.snap = snap

    def build(self):
        self._schedule()
        nc = self.nc
        per_engine = {e: [] for e in ENGINES}
        for o in self.ops:
            per_engine[o.engine].append(o)

        sem_ctxs = []
        prog_sem = {}
        dma_sem = {}
        used_eng = {o.engine for o in self.ops if not o.is_dma and o.need_inc}
        for e in ENGINES:
            if e not in used_eng:
                continue
            cm = nc.semaphore(f"p_{e}")
            prog_sem[e] = cm.__enter__()
            sem_ctxs.append(cm)
        ngroups = len(self.groups)
        if ngroups > 230:
            raise AssertionError(f"too many dma groups: {ngroups}")
        for g in range(ngroups):
            cm = nc.semaphore(f"g{g}")
            dma_sem[g] = cm.__enter__()
            sem_ctxs.append(cm)

        block_cm = nc.Block()
        block = block_cm.__enter__()

        def make_engine_fn(ename):
            def fn(eng):
                for o in per_engine[ename]:
                    for kind, de, val in o.waits:
                        if kind == "dma":
                            eng.wait_ge(dma_sem[de], 16 * val)
                        else:
                            eng.wait_ge(prog_sem[de], val)
                    if o.lowp:
                        with nc.allow_low_precision(reason="intended bf16 combine"):
                            ins = o.emit(eng)
                    else:
                        ins = o.emit(eng)
                    if o.is_dma:
                        ins.then_inc(dma_sem[o.gid], 16)
                    elif o.need_inc:
                        ins.then_inc(prog_sem[ename], 1)
            return fn

        block.tensor(make_engine_fn("tensor"))
        block.vector(make_engine_fn("vector"))
        block.scalar(make_engine_fn("scalar"))
        block.sync(make_engine_fn("sync"))
        block.gpsimd(make_engine_fn("gpsimd"))
        block_cm.__exit__(None, None, None)
        for cm in sem_ctxs:
            cm.__exit__(None, None, None)

    def t_matmul(self, out, lhsT, rhs, start, stop, reads, writes, tag=""):
        return self.op(
            "tensor",
            lambda t, out=out, lhsT=lhsT, rhs=rhs, start=start, stop=stop:
                t.matmul(out, lhsT, rhs, start=start, stop=stop),
            reads=reads, writes=writes, tag=tag)

    def t_transpose(self, out, in_, ident, reads, writes, tag=""):
        return self.op(
            "tensor",
            lambda t, out=out, in_=in_, ident=ident: t.transpose(out, in_, ident),
            reads=reads, writes=writes, tag=tag)

    def v(self, fname, *args, reads=(), writes=(), tag="", lowp=False, **kw):
        def emit(eng, fname=fname, args=args, kw=kw):
            return getattr(eng, fname)(*args, **kw)
        return self.op("vector", emit, reads=reads, writes=writes, tag=tag,
                       lowp=lowp)

    def s(self, fname, *args, reads=(), writes=(), tag="", lowp=False, **kw):
        def emit(eng, fname=fname, args=args, kw=kw):
            return getattr(eng, fname)(*args, **kw)
        return self.op("scalar", emit, reads=reads, writes=writes, tag=tag,
                       lowp=lowp)

    def load(self, dst, src, engine="sync", reads=(), writes=(), tag="",
             group=None):
        return self.dma(engine, lambda e, dst=dst, src=src: e.dma_start(out=dst, in_=src),
                        reads=reads, writes=writes, tag=tag, group=group)



class Sb:

    def __init__(self, nc):
        self.nc = nc
        self.tensors = {}

    def get(self, name, parts, cols, dtype):
        key = (name, parts, cols, dtype)
        if key not in self.tensors:
            for (n2, p2, c2, d2) in self.tensors:
                assert n2 != name, (
                    f"sbuf name '{name}' reused with different shape: "
                    f"({parts},{cols},{dtype}) vs ({p2},{c2},{d2})")
            self.tensors[key] = self.nc.alloc_sbuf_tensor(
                name, [parts, cols], dtype)
        return self.tensors[key]


class FM:

    def __init__(self, sb, name, d, t, dtype, backing=None):
        assert d % 128 == 0
        self.name, self.d, self.t, self.dtype = name, d, t, dtype
        self.nchunk = d // 128
        if backing is None:
            tens = [sb.get(f"{name}_c{i}", 128, t, dtype)
                    for i in range(self.nchunk)]
            self.chunks = [tt[:, 0:t] for tt in tens]
            self._keys = [tt.name for tt in tens]
        else:
            self.chunks = []
            self._keys = []
            for tt, off in backing:
                ap = tt[:, off:off + (t if dtype == tt.dtype else 2 * t)]
                if dtype != tt.dtype:
                    ap = ap.bitcast(dtype)
                self.chunks.append(ap)
                self._keys.append(tt.name)

    def key(self, i):
        return self._keys[i]

    def keys(self):
        return list(dict.fromkeys(self._keys))


class Pool:

    COLS = 4096

    def __init__(self, sb, n, name="gp"):
        self.tensors = [sb.get(f"{name}{i}", 128, self.COLS, BF16)
                        for i in range(n)]
        self.cursor = [0] * n

    def reset(self):
        self.cursor = [0] * len(self.tensors)

    def mark(self):
        return list(self.cursor)

    def release(self, m):
        self.cursor = list(m)

    def alloc_fm(self, name, d, t, dtype):
        need = t if dtype == BF16 else 2 * t
        nchunk = d // 128
        backing = []
        for _ in range(nchunk):
            for i, c in enumerate(self.cursor):
                if self.COLS - c >= need:
                    backing.append((self.tensors[i], c))
                    self.cursor[i] = c + need
                    break
            else:
                raise AssertionError(
                    f"pool exhausted allocating {name} ({d}x{t} {dtype})")
        return FM(None, name, d, t, dtype, backing=backing)

    def alloc_cols(self, ncols):
        for i, c in enumerate(self.cursor):
            if self.COLS - c >= ncols:
                self.cursor[i] = c + ncols
                return self.tensors[i][:, c:c + ncols], self.tensors[i].name
        raise AssertionError(f"pool exhausted allocating {ncols} cols")



class Ctx:
    def __init__(self, nc, serial=False, npool=16):
        self.nc = nc
        self.p = Prog(nc, serial=serial)
        self.sb = Sb(nc)
        self.host_inputs = {}
        self.dram = {}
        self.const_sbuf = {}
        self.const_log = []
        self.const_group = None
        self.pool = Pool(self.sb, npool)
        self._ws_rot = {}
        self.psf = [nc.alloc_psum_tensor(f"psf{i}", [128, 512], F32) for i in range(6)]
        self.psb = [nc.alloc_psum_tensor(f"psb{i}", [128, 1024], BF16) for i in range(2)]
        self._psf_i = 0
        self._psf_n = 4
        self._psb_i = 0

    def new_phase(self):
        self.pool.reset()
        self._ws_rot = {}

    def ws_get(self, cols):
        i = self._ws_rot.get(cols, 0)
        self._ws_rot[cols] = i + 1
        return self.sb.get(f"ws{cols}_{i}", 128, cols, BF16)

    def psum_f(self):
        i = self._psf_i % self._psf_n
        self._psf_i += 1
        return self.psf[i], f"psf{i}"

    def psum_b(self):
        i = self._psb_i % 2
        self._psb_i += 1
        return self.psb[i], f"psb{i}"

    def din(self, name, arr, dtype=None):
        if name in self.dram:
            return self.dram[name]
        if dtype is None:
            dtype = {np.dtype(np.float32): F32, np.dtype(np.int32): I32,
                     np.dtype(BF16NP): BF16}[arr.dtype]
        ap = self.nc.dram_tensor(name, list(arr.shape), dtype, kind="ExternalInput").ap()
        self.dram[name] = ap
        self.host_inputs[name] = arr
        return ap

    def dout(self, name, shape, dtype):
        ap = self.nc.dram_tensor(name, list(shape), dtype, kind="ExternalOutput").ap()
        self.dram[name] = ap
        return ap

    def dscratch(self, name, shape, dtype):
        if name in self.dram:
            return self.dram[name]
        ap = self.nc.dram_tensor(name, list(shape), dtype, kind="Internal").ap()
        self.dram[name] = ap
        return ap

    def const(self, name, arr, dtype=None):
        if name in self.const_sbuf:
            return self.const_sbuf[name]
        self.const_log.append(("const", name, arr, dtype))
        ap = self.din("c_" + name, arr, dtype)
        parts = arr.shape[0] if arr.ndim > 1 else 1
        cols = arr.shape[-1] if arr.ndim > 1 else arr.shape[0]
        sdt = {np.dtype(np.float32): F32, np.dtype(BF16NP): BF16,
               np.dtype(np.int32): I32}[arr.dtype] if dtype is None else dtype
        t = self.sb.get("cs_" + name, parts, cols, sdt)
        key = "cs_" + name
        src = ap if arr.ndim > 1 else ap[None, :]
        self.p.load(t[:, :], src, engine="sync", writes=[key], tag="const",
                    group=self.const_group)
        self.const_sbuf[name] = (t, key)
        return t, key

    def ident_bf(self):
        return self.const("identb", np.eye(128).astype(BF16NP))

    def ident_f32(self):
        return self.const("identf", np.eye(128, dtype=np.float32))

    def ones_row_bf(self):
        return self.const("ones1", np.ones((1, 128), BF16NP))


def pack_w(W, bw):
    din, dout = W.shape
    assert din % 128 == 0 and dout % bw == 0
    nk, nob = din // 128, dout // bw
    Wr = W.reshape(nk, 128, nob, bw).transpose(2, 1, 0, 3).reshape(nob, 128, nk * bw)
    return np.ascontiguousarray(Wr)



class WStream:

    def __init__(self, ctx, name, W_np, bw, nbuf=2, dtype_override=None,
                 share=None):
        self.ctx = ctx
        self.bw = bw
        self.din, self.dout = W_np.shape
        self.nk = self.din // 128
        self.nob = self.dout // bw
        packed = pack_w(W_np, bw)
        self.dram = ctx.din("w_" + name, packed, dtype_override)
        if share is not None:
            assert share.nk * share.bw >= self.nk * bw
            self.bufs = [ap[:, 0:self.nk * bw] for ap in share.bufs]
            self.buf_keys = list(share.buf_keys)
            nbuf = share.nbuf
        else:
            pairs = [ctx.pool.alloc_cols(self.nk * bw) for i in range(nbuf)]
            self.bufs = [ap for ap, _ in pairs]
            self.buf_keys = [k for _, k in pairs]
        self.nbuf = nbuf
        self.name = name

    def load_block(self, ob):
        b = ob % self.nbuf
        t = self.bufs[b]
        key = self.buf_keys[b]
        self.ctx.p.load(t[:, :], self.dram[ob], engine="gpsimd",
                        writes=[key], tag=f"w:{self.name}:{ob}")
        return t, key

    def tile(self, ob, k, oc):
        t = self.bufs[ob % self.nbuf]
        return t[:, k * self.bw + oc * 128: k * self.bw + (oc + 1) * 128]

    def buf_key(self, ob):
        return self.buf_keys[ob % self.nbuf]


def gemm_fm(ctx, A, ws, epilogue, t0=0, tn=None, tw=512, tag="", toff=0,
            blocks=None):
    p = ctx.p
    tn = tn if tn is not None else A.t
    ntt = (tn - t0) // tw
    old_psf_n = ctx._psf_n
    ctx._psf_n = 6
    for ob in (blocks if blocks is not None else range(ws.nob)):
        wt, wkey = ws.load_block(ob)
        tts = list(range(ntt))
        pairs = [tts[i:i + 4] for i in range(0, ntt, 4)]
        for pr in pairs:
            for oc in range(ws.bw // 128):
                ocg = ob * (ws.bw // 128) + oc
                pss = [ctx.psum_f() for _ in pr]
                for k in range(ws.nk):
                    for j, tt in enumerate(pr):
                        tt0 = t0 + tt * tw
                        p.t_matmul(pss[j][0][:, :tw], ws.tile(ob, k, oc),
                                   A.chunks[k][:, tt0:tt0 + tw],
                                   start=(k == 0), stop=(k == ws.nk - 1),
                                   reads=[wkey, A.key(k)],
                                   writes=[pss[j][1]],
                                   tag=f"mm:{tag}:{ob}:{tt}:{oc}")
                for j, tt in enumerate(pr):
                    tt0 = t0 + tt * tw
                    epilogue(ctx, pss[j][0], pss[j][1], ocg, toff + tt0, tw)
    ctx._psf_n = old_psf_n


def ep_to_fm(out_fm, func=None, scale=1.0, bias_tile=None, bias_key=None,
             engine="scalar"):
    def ep(ctx, ps, pskey, ocg, tt0, tw):
        p = ctx.p
        dst = out_fm.chunks[ocg][:, tt0:tt0 + tw]
        reads = [pskey] + ([bias_key] if bias_key else [])
        writes = [out_fm.key(ocg)]
        if func is None and bias_tile is None and scale == 1.0:
            p.s("copy", dst, ps[:, :tw], reads=reads, writes=writes, tag="ep_copy")
        else:
            kw = {}
            if bias_tile is not None:
                kw["bias"] = bias_tile[:, ocg:ocg + 1]
            if scale != 1.0:
                kw["scale"] = float(scale)
            p.s("activation", dst, ps[:, :tw], func if func else AF.Copy,
                reads=reads, writes=writes, tag="ep_act", **kw)
    return ep


def ep_mult_fm(gate_fm, out_fm):
    def ep(ctx, ps, pskey, ocg, tt0, tw):
        p = ctx.p
        dst = out_fm.chunks[ocg][:, tt0:tt0 + tw]
        p.v("tensor_tensor", dst, ps[:, :tw], gate_fm.chunks[ocg][:, tt0:tt0 + tw],
            ALU.mult, reads=[pskey, gate_fm.key(ocg)], writes=[out_fm.key(ocg)],
            tag="ep_mult")
    return ep



def ln_pass(ctx, x_dram, out_bf, out_f32=None, eps=1e-5, t0=0, nt=T, tag="ln",
            gamma=None, beta=None, out_dram=None, src_key="xres", dst_key=None):
    p, sb = ctx.p, ctx.sb
    if out_bf is not None:
        identb, identb_k = ctx.ident_bf()
    if out_f32 is not None:
        identf, identf_k = ctx.ident_f32()
    ntile = nt // 128
    for ii in range(ntile):
        i = t0 // 128 + ii
        r = i % 2
        xt = sb.get(f"lnx{r}", 128, D, F32)
        xk = f"lnx{r}"
        st = sb.get(f"lnst{r}", 128, 8, F32)
        stk = f"lnst{r}"
        lt = sb.get(f"lnt{r}", 128, D, BF16)
        ltk = f"lnt{r}"
        if out_f32 is not None or out_dram is not None:
            lf = sb.get(f"resx{r}", 128, D, F32)
            lfk = f"resx{r}"
        p.load(xt[:, :], x_dram[i * 128:(i + 1) * 128, :], engine="sync",
               reads=[f"{src_key}:{i}"], writes=[xk], tag=f"{tag}:load{i}")
        p.v("tensor_reduce", st[:, 0:1], xt[:, :], AX.X, ALU.add,
            reads=[xk], writes=[stk], tag=f"{tag}:sum")
        p.v("tensor_scalar", st[:, 0:1], st[:, 0:1], 1.0 / D, None, ALU.mult,
            reads=[stk], writes=[stk], tag=f"{tag}:mean")
        p.v("tensor_scalar", xt[:, :], xt[:, :], st[:, 0:1], None, ALU.subtract,
            reads=[xk, stk], writes=[xk], tag=f"{tag}:sub")
        p.s("activation", lt[:, :], xt[:, :], AF.Square,
            reads=[xk], writes=[ltk, stk], accum_out=st[:, 1:2], tag=f"{tag}:sq")
        p.v("tensor_scalar", st[:, 2:3], st[:, 1:2], 1.0 / D, float(eps),
            ALU.mult, ALU.add, reads=[stk], writes=[stk], tag=f"{tag}:var")
        p.s("activation", st[:, 3:4], st[:, 2:3], AF.Sqrt,
            reads=[stk], writes=[stk], tag=f"{tag}:sd")
        p.v("reciprocal", st[:, 4:5], st[:, 3:4],
            reads=[stk], writes=[stk], tag=f"{tag}:rs")
        if out_bf is not None:
            p.v("tensor_scalar", lt[:, :], xt[:, :], st[:, 4:5], None, ALU.mult,
                reads=[xk, stk], writes=[ltk], tag=f"{tag}:nrm")
        if out_f32 is not None or out_dram is not None:
            p.v("tensor_scalar", lf[:, :], xt[:, :], st[:, 4:5], None, ALU.mult,
                reads=[xk, stk], writes=[lfk], tag=f"{tag}:nrmf")
        if out_dram is not None:
            p.dma("sync", lambda e, lf=lf, i=i: e.dma_start(
                out=out_dram[i * 128:(i + 1) * 128, :], in_=lf[:, :]),
                reads=[lfk], writes=[f"{dst_key}:{i}"], group=lfk,
                tag=f"{tag}:zst")
        for j in range(0 if out_bf is not None or out_f32 is not None else D // 128,
                       D // 128):
            if out_bf is not None:
                pb, pbk = ctx.psum_b()
                p.t_transpose(pb[:, :128], lt[:, j * 128:(j + 1) * 128],
                              identb[:, :], reads=[ltk, identb_k], writes=[pbk],
                              tag=f"{tag}:tr")
                dst = out_bf.chunks[j][:, ii * 128:(ii + 1) * 128]
                if gamma is not None:
                    p.v("tensor_scalar", dst, pb[:, :128], gamma[0][:, j:j + 1],
                        beta[0][:, j:j + 1], ALU.mult, ALU.add,
                        reads=[pbk, gamma[1], beta[1]], writes=[out_bf.key(j)],
                        tag=f"{tag}:trgb")
                else:
                    p.v("tensor_copy", dst, pb[:, :128], reads=[pbk],
                        writes=[out_bf.key(j)], tag=f"{tag}:trc")
            if out_f32 is not None:
                pf, pfk = ctx.psum_f()
                p.t_transpose(pf[:, :128], lf[:, j * 128:(j + 1) * 128],
                              identf[:, :], reads=[lfk, identf_k], writes=[pfk],
                              tag=f"{tag}:trf")
                p.v("tensor_copy", out_f32.chunks[j][:, ii * 128:(ii + 1) * 128],
                    pf[:, :128], reads=[pfk], writes=[out_f32.key(j)],
                    tag=f"{tag}:trfc")



def residual_add(ctx, x_src, x_dst, y_fm, t0, tw, scale=1.0, y_t0=0, tag="res",
                 res_key="xres"):
    p, sb = ctx.p, ctx.sb
    if y_fm.dtype == BF16:
        ident, identk = ctx.ident_bf()
    else:
        ident, identk = ctx.ident_f32()
    ntile = tw // 128
    for i in range(ntile):
        r = i % 2
        xt = sb.get(f"resx{r}", 128, D, F32)
        xk = f"resx{r}"
        g0 = t0 + i * 128
        p.load(xt[:, :], x_src[g0:g0 + 128, :], engine="sync",
               reads=[f"{res_key}:{g0 // 128}"], writes=[xk], tag=f"{tag}:load")
        for j in range(y_fm.nchunk):
            if y_fm.dtype == BF16:
                pb, pbk = ctx.psum_b()
                pbv = pb[:, :128]
            else:
                pb, pbk = ctx.psum_f()
                pbv = pb[:, :128]
            p.t_transpose(pbv, y_fm.chunks[j][:, y_t0 + i * 128:y_t0 + (i + 1) * 128],
                          ident[:, :], reads=[y_fm.key(j), identk], writes=[pbk],
                          tag=f"{tag}:tr")
            p.v("scalar_tensor_tensor", xt[:, j * 128:(j + 1) * 128], pbv,
                float(scale), xt[:, j * 128:(j + 1) * 128], ALU.mult, ALU.add,
                reads=[pbk, xk], writes=[xk], tag=f"{tag}:acc")
        p.dma("sync", lambda e, xt=xt, g0=g0: e.dma_start(
            out=x_dst[g0:g0 + 128, :], in_=xt[:, :]),
            reads=[xk], writes=[f"{res_key}:{g0 // 128}"], group=xk,
            tag=f"{tag}:store")



def emit_l0_ffn(ctx, x_src, x_dst, weights, tag):
    p, sb = ctx.p, ctx.sb
    HID = 3072
    ctx.new_phase()

    ws_g = WStream(ctx, "l0wg", weights["wgT"], 512)
    ws_u = WStream(ctx, "l0wu", weights["wuT"], 512)
    ws_d = WStream(ctx, "l0wd", weights["wdT"], 128)

    ln_bf = ctx.pool.alloc_fm("l0f_ln", D, 512, BF16)
    gate = ctx.pool.alloc_fm("l0f_gate", HID, 512, BF16)
    h = ctx.pool.alloc_fm("l0f_h", HID, 512, BF16)
    y = ctx.pool.alloc_fm("ytt", D, 512, BF16)

    SC = 1.0 / 127.0

    def ep_gate(ctx, ps, pskey, ocg, gt0, tw):
        p.s("activation", gate.chunks[ocg][:, 0:tw], ps[:, :tw], AF.Silu,
            scale=SC, reads=[pskey], writes=[gate.key(ocg)], tag="l0:gsilu")

    def ep_up(ctx, ps, pskey, ocg, gt0, tw):
        p.v("scalar_tensor_tensor", h.chunks[ocg][:, 0:tw], ps[:, :tw], SC,
            gate.chunks[ocg][:, 0:tw], ALU.mult, ALU.mult,
            reads=[pskey, gate.key(ocg)], writes=[h.key(ocg)], tag="l0:hup")

    def ep_down(ctx, ps, pskey, ocg, gt0, tw):
        p.s("activation", y.chunks[ocg][:, 0:tw], ps[:, :tw], AF.Copy,
            scale=SC, reads=[pskey], writes=[y.key(ocg)], tag="l0:down")

    for tt in range(T // 512):
        tt0 = tt * 512
        ln_pass(ctx, x_src, ln_bf, t0=tt0, nt=512, tag=f"{tag}:ln")
        gemm_fm(ctx, ln_bf, ws_g, ep_gate, t0=0, tn=512, toff=tt0,
                tag=f"{tag}:g")
        gemm_fm(ctx, ln_bf, ws_u, ep_up, t0=0, tn=512, toff=tt0,
                tag=f"{tag}:u")
        gemm_fm(ctx, h, ws_d, ep_down, t0=0, tn=512, toff=tt0, tag=f"{tag}:d")
        residual_add(ctx, x_src, x_dst, y, tt0, 512, scale=0.5, tag=f"{tag}:res")



def load_fm_from_dram(ctx, fm, dram, dname, t0, tw, tag="lfm"):
    p = ctx.p
    for c in range(fm.nchunk):
        p.load(fm.chunks[c][:, 0:tw], dram[c, :, t0:t0 + tw], engine="sync",
               reads=[f"{dname}:{c}"], writes=[fm.key(c)], tag=f"{tag}:{c}")


def ep_store_dram(ctx_, dram, dname, group_unused=None, func=None, scale_tile=None,
                  scale_key=None, mult_fm=None, dst_dtype=BF16, t_base=0,
                  bias_tile=None, bias_key=None):
    def ep(ctx, ps, pskey, ocg, tt0, tw):
        p, sb = ctx.p, ctx.sb
        r = (ocg + tt0 // 512) % 2 if dst_dtype == BF16 else 0
        nm = "spill" if dst_dtype == BF16 else "spillf"
        st = sb.get(f"{nm}{r}", 128, 512, dst_dtype)
        stk = st.name
        kw = {}
        if scale_tile is not None:
            kw["scale"] = scale_tile[:, ocg:ocg + 1]
        if bias_tile is not None:
            p.v("tensor_scalar", st[:, :tw], ps[:, :tw],
                bias_tile[:, ocg:ocg + 1], None, ALU.add,
                reads=[pskey, bias_key], writes=[stk], tag="eps_bias")
        elif mult_fm is None:
            p.s("activation", st[:, :tw], ps[:, :tw], func if func else AF.Copy,
                reads=[pskey] + ([scale_key] if scale_key else []),
                writes=[stk], tag="eps_act", **kw)
        else:
            p.v("scalar_tensor_tensor", st[:, :tw], ps[:, :tw],
                scale_tile[:, ocg:ocg + 1] if scale_tile is not None else 1.0,
                mult_fm.chunks[ocg][:, tt0 - t_base:tt0 - t_base + tw],
                ALU.mult, ALU.mult,
                reads=[pskey, mult_fm.key(ocg)] + ([scale_key] if scale_key else []),
                writes=[stk], tag="eps_stt")
        p.dma("sync", lambda e, st=st, ocg=ocg, tt0=tt0, tw=tw: e.dma_start(
            out=dram[ocg, :, tt0:tt0 + tw], in_=st[:, :tw]),
            reads=[stk], writes=[f"{dname}:{ocg}"], group=stk, tag="eps_dma")
    return ep


def col_tile(ctx, name, vec_2d):
    if name in ctx.const_sbuf:
        return ctx.const_sbuf[name]
    ctx.const_log.append(("col", name, vec_2d, None))
    ap = ctx.din("t_" + name, vec_2d)
    t = ctx.sb.get("ts_" + name, 128, vec_2d.shape[1],
                   F32 if vec_2d.dtype == np.float32 else BF16)
    ctx.p.load(t[:, :], ap, engine="sync", writes=[t.name], tag=f"ct:{name}",
               group=ctx.const_group)
    ctx.const_sbuf[name] = (t, t.name)
    return t, t.name



def emit_l3_ffn(ctx, x_src, x_dst, weights, tag):
    p, sb = ctx.p, ctx.sb
    HID = 4096
    NCH = HID // 128
    ctx.new_phase()

    s1t, s1k = col_tile(ctx, "l3s1", weights["s1c"])
    s2t, s2k = col_tile(ctx, "l3s2", weights["s2c"])
    sh1t, sh1k = col_tile(ctx, "l3sh1", weights["sh1c"])
    sh2t, sh2k = col_tile(ctx, "l3sh2", weights["sh2c"])
    s3t, s3k = col_tile(ctx, "l3s3", weights["s3c"])

    a_dram = ctx.dscratch("l3_a", [NCH, 128, T], BF16)
    b_dram = ctx.dscratch("l3_b", [NCH, 128, T], BF16)

    ln_bf = ctx.pool.alloc_fm("l3f_ln", D, T, BF16)
    ln_pass(ctx, x_src, ln_bf, tag=f"{tag}:ln")
    ws_1 = WStream(ctx, "l3w1", weights["w1T"], 512)
    ws_2 = WStream(ctx, "l3w2", weights["w2T"], 512)
    gate = ctx.pool.alloc_fm("l3f_gate", 512, T, BF16)

    for ob in range(HID // 512):
        def ep_gate(ctx, ps, pskey, ocg, gt0, tw, ob=ob):
            loc = ocg - ob * 4
            p.s("activation", gate.chunks[loc][:, gt0:gt0 + tw], ps[:, :tw],
                AF.Silu, scale=s1t[:, ocg:ocg + 1], reads=[pskey, s1k],
                writes=[gate.key(loc)], tag="l3:gsilu")

        def ep_a(ctx, ps, pskey, ocg, gt0, tw, ob=ob):
            loc = ocg - ob * 4
            r = (ocg + gt0 // 512) % 2
            st = sb.get(f"spill{r}", 128, 512, BF16)
            p.v("scalar_tensor_tensor", st[:, :tw], ps[:, :tw],
                s2t[:, ocg:ocg + 1], gate.chunks[loc][:, gt0:gt0 + tw],
                ALU.mult, ALU.mult, reads=[pskey, gate.key(loc), s2k],
                writes=[st.name], tag="l3:amul")
            p.dma("sync", lambda e, st=st, ocg=ocg, gt0=gt0, tw=tw: e.dma_start(
                out=a_dram[ocg, :, gt0:gt0 + tw], in_=st[:, :tw]),
                reads=[st.name], writes=[f"l3_a:{ocg}"], group=st.name,
                tag="l3:adma")

        gemm_fm(ctx, ln_bf, ws_1, ep_gate, blocks=[ob], tag=f"{tag}:w1")
        gemm_fm(ctx, ln_bf, ws_2, ep_a, blocks=[ob], tag=f"{tag}:w2")

    ctx.pool.reset()
    ws_h1 = WStream(ctx, "l3h1", weights["h1T"], 128)
    ws_h2 = WStream(ctx, "l3h2", weights["h2T"], 128)
    ws_3 = WStream(ctx, "l3w3", weights["w3T"], 128)
    y = ctx.pool.alloc_fm("ytt", D, 512, BF16)
    pm = ctx.pool.mark()

    for q in range(4):
        th0 = q * 1024
        ctx.pool.release(pm)
        ah = ctx.pool.alloc_fm("l3ah", HID, 1024, BF16)
        load_fm_from_dram(ctx, ah, a_dram, "l3_a", th0, 1024, tag=f"{tag}:lda")
        gemm_fm(ctx, ah, ws_h1,
                ep_store_dram(ctx, b_dram, "l3_b", f"{tag}:bst{q}",
                              func=AF.Gelu, scale_tile=sh1t, scale_key=sh1k),
                t0=0, tn=1024, toff=th0, tag=f"{tag}:h1")
        ctx.pool.release(pm)
        bh = ctx.pool.alloc_fm("l3bh", HID, 1024, BF16)
        load_fm_from_dram(ctx, bh, b_dram, "l3_b", th0, 1024, tag=f"{tag}:ldb")
        gemm_fm(ctx, bh, ws_h2,
                ep_store_dram(ctx, a_dram, "l3_a", f"{tag}:cst{q}",
                              func=AF.Gelu, scale_tile=sh2t, scale_key=sh2k),
                t0=0, tn=1024, toff=th0, tag=f"{tag}:h2")
        ctx.pool.release(pm)
        ch = ctx.pool.alloc_fm("l3ch", HID, 1024, BF16)
        load_fm_from_dram(ctx, ch, a_dram, "l3_a", th0, 1024, tag=f"{tag}:ldc")

        def ep_y(ctx, ps, pskey, ocg, gt0, tw):
            p.s("activation", y.chunks[ocg][:, 0:tw], ps[:, :tw], AF.Copy,
                scale=s3t[:, ocg:ocg + 1], reads=[pskey, s3k],
                writes=[y.key(ocg)], tag="l3:y")

        for tt in range(2):
            lt0 = tt * 512
            gemm_fm(ctx, ch, ws_3, ep_y, t0=lt0, tn=lt0 + 512, tag=f"{tag}:w3")
            residual_add(ctx, x_src, x_dst, y, th0 + lt0, 512, scale=0.5,
                         tag=f"{tag}:res")


def prep_l3_ffn(inp):
    def colpack(v):
        return np.ascontiguousarray(v.reshape(-1, 128).T.astype(np.float32))
    return {
        "w1T": np.ascontiguousarray(inp["L3_ffn_w1_w"].T),
        "w2T": np.ascontiguousarray(inp["L3_ffn_w2_w"].T),
        "h1T": np.ascontiguousarray(inp["L3_ffn_h1_w"].T),
        "h2T": np.ascontiguousarray(inp["L3_ffn_h2_w"].T),
        "w3T": np.ascontiguousarray(inp["L3_ffn_w3_w"].T),
        "s1c": colpack(inp["L3_ffn_w1_s"]),
        "s2c": colpack(inp["L3_ffn_w2_s"]),
        "sh1c": colpack(inp["L3_ffn_h1_s"]),
        "sh2c": colpack(inp["L3_ffn_h2_s"]),
        "s3c": colpack(inp["L3_ffn_w3_s"]),
    }



def emit_l1_ffn(ctx, x_src, x_dst, weights, tag):
    p, sb = ctx.p, ctx.sb
    E = 20
    ctx.new_phase()
    ln_bf = ctx.pool.alloc_fm("l1f_ln", D, T, BF16)
    ln_pass(ctx, x_src, ln_bf, tag=f"{tag}:ln")
    pm0 = ctx.pool.mark()

    hfm = ctx.pool.alloc_fm("l1f_h", 256, T, BF16)
    logit = ctx.pool.alloc_fm("l1f_lg", 128, T, BF16)
    ws_g1 = WStream(ctx, "l1wg1", weights["wg1"], 256)
    ws_g2 = WStream(ctx, "l1wg2", weights["wg2p"], 128)
    bg1t, bg1k = col_tile(ctx, "l1bg1", weights["bg1c"])
    bg2t, bg2k = col_tile(ctx, "l1bg2", weights["bg2c"])
    identf, identf_k = ctx.ident_f32()

    def ep_h(ctx, ps, pskey, ocg, tt0, tw):
        p.s("activation", hfm.chunks[ocg][:, tt0:tt0 + tw], ps[:, :tw], AF.Gelu,
            bias=bg1t[:, ocg:ocg + 1], reads=[pskey, bg1k],
            writes=[hfm.key(ocg)], tag="l1:h")

    def ep_lg(ctx, ps, pskey, ocg, tt0, tw):
        p.v("tensor_scalar", logit.chunks[0][:, tt0:tt0 + tw], ps[:, :tw],
            bg2t[:, 0:1], None, ALU.add, reads=[pskey, bg2k],
            writes=[logit.key(0)], tag="l1:lg")

    gemm_fm(ctx, ln_bf, ws_g1, ep_h, tag=f"{tag}:g1")
    gemm_fm(ctx, hfm, ws_g2, ep_lg, tag=f"{tag}:g2")

    gates = sb.get("l2mb", 128, 32 * 32, BF16)
    gk = gates.name
    p.v("memset", gates[:, :], 0.0, writes=[gk], tag="l1:gz")
    identb, identb_k = ctx.ident_bf()
    for i in range(T // 128):
        pb, pbk = ctx.psum_b()
        lgT = sb.get(f"l1lgT{i % 2}", 128, 32, F32)
        lk = lgT.name
        st = sb.get(f"l1st{i % 2}", 128, 8, F32)
        sk = st.name
        p.t_transpose(pb[:, :128], logit.chunks[0][:, i * 128:(i + 1) * 128],
                      identb[:, :], reads=[logit.key(0), identb_k],
                      writes=[pbk], tag="l1:lgT")
        p.v("tensor_copy", lgT[:, 0:20], pb[:, 0:20], reads=[pbk], writes=[lk],
            tag="l1:lgTc")
        p.v("reduce_max", st[:, 0:1], lgT[:, 0:20], AX.X, reads=[lk],
            writes=[sk], tag="l1:max")
        p.v("tensor_scalar", lgT[:, 0:20], lgT[:, 0:20], st[:, 0:1], None,
            ALU.subtract, reads=[lk, sk], writes=[lk], tag="l1:sub")
        p.s("activation", lgT[:, 0:20], lgT[:, 0:20], AF.Exp,
            accum_out=st[:, 1:2], reads=[lk], writes=[lk, sk], tag="l1:exp")
        p.v("reciprocal", st[:, 2:3], st[:, 1:2], reads=[sk], writes=[sk],
            tag="l1:rcp")
        p.v("tensor_scalar", lgT[:, 0:20], lgT[:, 0:20], st[:, 2:3], None,
            ALU.mult, reads=[lk, sk], writes=[lk], tag="l1:prob")
        p.v("reduce_max", st[:, 3:4], lgT[:, 0:20], AX.X, reads=[lk],
            writes=[sk], tag="l1:m1")
        m1col = st[:, 3:4]
        mask = sb.get(f"l1msk{i % 2}", 128, 20, F32)
        mk = mask.name
        p.v("tensor_scalar", mask[:, 0:20], lgT[:, 0:20], m1col, None,
            ALU.is_equal, reads=[lk, sk], writes=[mk], tag="l1:eq1")
        pm_t = sb.get(f"l1pm{i % 2}", 128, 20, F32)
        pmk = pm_t.name
        p.v("tensor_scalar", pm_t[:, 0:20], mask[:, 0:20], -1.0, 1.0, ALU.mult,
            ALU.add, reads=[mk], writes=[pmk], tag="l1:inv")
        p.v("tensor_mul", pm_t[:, 0:20], pm_t[:, 0:20], lgT[:, 0:20],
            reads=[pmk, lk], writes=[pmk], tag="l1:pm")
        p.v("reduce_max", st[:, 4:5], pm_t[:, 0:20], AX.X, reads=[pmk],
            writes=[sk], tag="l1:m2")
        p.v("tensor_scalar", pm_t[:, 0:20], pm_t[:, 0:20], st[:, 4:5], None,
            ALU.is_equal, reads=[pmk, sk], writes=[pmk], tag="l1:eq2")
        p.v("tensor_add", mask[:, 0:20], mask[:, 0:20], pm_t[:, 0:20],
            reads=[mk, pmk], writes=[mk], tag="l1:msum")
        p.v("tensor_mul", gates[:, i * 32:i * 32 + 20], mask[:, 0:20],
            lgT[:, 0:20], reads=[mk, lk], writes=[gk], tag="l1:gates")

    ctx.pool.release(pm0)
    web = [ctx.pool.alloc_cols(4096) for _ in range(4)]
    we_dram = ctx.din("w_l1we", weights["wep"])
    be_sb = sb.get("l1be", 20, 1024, BF16)
    p.load(be_sb[:, :], ctx.din("w_l1be", weights["be"]), engine="gpsimd",
           writes=[be_sb.name], tag="l1:be")
    acc = [ctx.pool.alloc_cols(1024) for j in range(8)]
    gT = sb.get("l1gT", 32, 128, BF16)

    for q in range(4):
        q0 = q * 1024
        for j in range(8):
            i = q * 8 + j
            pg, pgk = ctx.psum_b()
            p.t_transpose(pg[0:32, 0:128], gates[:, i * 32:(i + 1) * 32],
                          identb[:, :], reads=[gk, identb_k], writes=[pgk],
                          tag="l1:gT")
            p.v("tensor_copy", gT[:, :], pg[0:32, 0:128], reads=[pgk],
                writes=[gT.name], tag="l1:gTc")
            pf, pfk = ctx.psum_f()
            p.t_matmul(pf[:, 0:512], gT[0:20, :], be_sb[:, 0:512], True, True,
                       reads=[gT.name, be_sb.name], writes=[pfk], tag="l1:be1")
            p.v("tensor_copy", acc[j][0][:, 0:512], pf[:, 0:512], reads=[pfk],
                writes=[acc[j][1]], tag="l1:bec1", lowp=True)
            pf2, pfk2 = ctx.psum_f()
            p.t_matmul(pf2[:, 0:512], gT[0:20, :], be_sb[:, 512:1024], True,
                       True, reads=[gT.name, be_sb.name], writes=[pfk2],
                       tag="l1:be2")
            p.v("tensor_copy", acc[j][0][:, 512:1024], pf2[:, 0:512],
                reads=[pfk2], writes=[acc[j][1]], tag="l1:bec2", lowp=True)
        for e in range(E):
            b0, b0k = web[(e % 2) * 2]
            b1, b1k = web[(e % 2) * 2 + 1]
            p.load(b0[:, :], we_dram[e, :, 0:4096], engine="gpsimd",
                   writes=[b0k], tag=f"l1:we{e}a")
            p.load(b1[:, :], we_dram[e, :, 4096:8192], engine="gpsimd",
                   writes=[b1k], tag=f"l1:we{e}b")

            def wetile(k, db):
                c = k * 1024 + db * 512
                if c < 4096:
                    return b0[:, c:c + 512], b0k
                return b1[:, c - 4096:c - 4096 + 512], b1k

            for j in range(8):
                i = q * 8 + j
                for db in range(2):
                    pf, pfk = ctx.psum_f()
                    for k in range(8):
                        wt, wkk = wetile(k, db)
                        p.t_matmul(pf[:, 0:512], ln_bf.chunks[k][:, i * 128:(i + 1) * 128],
                                   wt, start=(k == 0), stop=(k == 7),
                                   reads=[ln_bf.key(k), wkk], writes=[pfk],
                                   tag=f"l1:mm{e}")
                    p.v("scalar_tensor_tensor", acc[j][0][:, db * 512:(db + 1) * 512],
                        pf[:, 0:512], gates[:, i * 32 + e:i * 32 + e + 1],
                        acc[j][0][:, db * 512:(db + 1) * 512], ALU.mult, ALU.add,
                        reads=[pfk, gk, acc[j][1]], writes=[acc[j][1]],
                        tag=f"l1:acc{e}", lowp=True)
        for j in range(8):
            i = q * 8 + j
            r = i % 2
            xt = sb.get(f"resx{r}", 128, D, F32)
            xk = xt.name
            p.load(xt[:, :], x_src[i * 128:(i + 1) * 128, :], engine="sync",
                   reads=[f"xres:{i}"], writes=[xk], tag="l1:resl")
            p.v("scalar_tensor_tensor", xt[:, :], acc[j][0][:, :], 0.5, xt[:, :],
                ALU.mult, ALU.add, reads=[acc[j][1], xk], writes=[xk],
                tag="l1:resa", lowp=True)
            p.dma("sync", lambda e, xt=xt, i=i: e.dma_start(
                out=x_dst[i * 128:(i + 1) * 128, :], in_=xt[:, :]),
                reads=[xk], writes=[f"xres:{i}"], group=xk, tag="l1:ress")


def prep_l1_ffn(inp):
    wg2 = np.zeros((256, 128), np.float32)
    wg2[:, :40] = inp["L1_ffn_wg2"]
    bg2 = np.zeros((128,), np.float32)
    bg2[:40] = inp["L1_ffn_bg2"]
    We = inp["L1_ffn_We"]
    wep = np.stack([pack_w(We[e], 1024)[0] for e in range(20)], 0)
    return {
        "wg1": np.ascontiguousarray(inp["L1_ffn_wg1"]),
        "wg2p": np.ascontiguousarray(wg2),
        "bg1c": np.ascontiguousarray(inp["L1_ffn_bg1"].reshape(-1, 128).T),
        "bg2c": np.ascontiguousarray(bg2.reshape(-1, 128).T),
        "wep": np.ascontiguousarray(wep),
        "be": np.ascontiguousarray(inp["L1_ffn_be"]),
    }



def emit_l2_ffn(ctx, x_src, x_dst, weights, tag):
    p, sb = ctx.p, ctx.sb
    NH, DH = 16, 64
    ctx.new_phase()
    identb, identb_k = ctx.ident_bf()
    identf, identf_k = ctx.ident_f32()
    onescol, onescol_k = ctx.const("onescol", np.ones((128, 1), BF16NP))
    onesrow, onesrow_k = ctx.ones_row_bf()

    z_dram = ctx.dscratch("l2_z", [T, D], F32)
    zk = f"l2z{tag}"
    qkv_dram = ctx.dscratch("l2_qkv", [24, 128, T], BF16)

    ln1t, ln1k = col_tile(ctx, "l2ln1w", weights["ln1wc"])
    ln1bt, ln1bk = col_tile(ctx, "l2ln1b", weights["ln1bc"])
    ln2t, ln2k = col_tile(ctx, "l2ln2w", weights["ln2wc"])
    ln2bt, ln2bk = col_tile(ctx, "l2ln2b", weights["ln2bc"])
    cw0, cw0k = col_tile(ctx, "l2cw0", weights["cw0c"])
    cw1, cw1k = col_tile(ctx, "l2cw1", weights["cw1c"])
    cw2, cw2k = col_tile(ctx, "l2cw2", weights["cw2c"])

    hc = ctx.pool.alloc_fm("l2hc", D, T, BF16)
    pm1 = ctx.pool.mark()
    h = ctx.pool.alloc_fm("l2h", D, T, BF16)
    ln_pass(ctx, x_src, None, out_dram=z_dram, dst_key=zk, tag=f"{tag}:lnz")
    ln_pass(ctx, z_dram, h, gamma=(ln1t, ln1k), beta=(ln1bt, ln1bk),
            src_key=zk, tag=f"{tag}:ln1")
    for j in range(D // 128):
        hck, hk = hc.key(j), h.key(j)
        hcj, hj = hc.chunks[j], h.chunks[j]
        p.v("tensor_scalar", hcj[:, 2:T], hj[:, 0:T - 2], cw0[:, j:j + 1], None,
            ALU.mult, reads=[hk, cw0k], writes=[hck], tag="l2:c0")
        p.v("scalar_tensor_tensor", hcj[:, 2:T], hj[:, 1:T - 1],
            cw1[:, j:j + 1], hcj[:, 2:T], ALU.mult, ALU.add,
            reads=[hk, cw1k, hck], writes=[hck], tag="l2:c1", lowp=True)
        p.v("scalar_tensor_tensor", hcj[:, 2:T], hj[:, 2:T],
            cw2[:, j:j + 1], hcj[:, 2:T], ALU.mult, ALU.add,
            reads=[hk, cw2k, hck], writes=[hck], tag="l2:c2", lowp=True)
        p.v("tensor_scalar", hcj[:, 0:1], hj[:, 0:1], cw2[:, j:j + 1], None,
            ALU.mult, reads=[hk, cw2k], writes=[hck], tag="l2:cb0")
        p.v("tensor_scalar", hcj[:, 1:2], hj[:, 0:1], cw1[:, j:j + 1], None,
            ALU.mult, reads=[hk, cw1k], writes=[hck], tag="l2:cb1")
        p.v("scalar_tensor_tensor", hcj[:, 1:2], hj[:, 1:2], cw2[:, j:j + 1],
            hcj[:, 1:2], ALU.mult, ALU.add, reads=[hk, cw2k, hck],
            writes=[hck], tag="l2:cb2", lowp=True)

    ctx.pool.release(pm1)
    ws_qkv = WStream(ctx, "l2aw", weights["attn_w"], 512)
    qsc, qsck = col_tile(ctx, "l2qsc", weights["qscale"])
    gemm_fm(ctx, hc, ws_qkv, ep_store_dram(ctx, qkv_dram, "l2qkv",
                                           scale_tile=qsc, scale_key=qsck),
            tag=f"{tag}:qkv")

    ctx.pool.reset()
    attn_fm = ctx.pool.alloc_fm("l2attn", D, T, BF16)
    pm_attn = ctx.pool.mark()
    qp_t, qp_k = ctx.pool.alloc_cols(T)
    kp_t, kp_k = ctx.pool.alloc_cols(T)
    vp_t, vp_k = ctx.pool.alloc_cols(T)
    vTs = [ctx.pool.alloc_cols(16 * 130) for _ in range(2)]
    cos_dram = ctx.din("w_l2cos", weights["cosp"])
    sin_dram = ctx.din("w_l2sin", weights["sinp"])
    ropwc = sb.get("ropwc", 128, 512, BF16)
    ropws = sb.get("ropws", 128, 512, BF16)
    rotP, rotP_k = ctx.const("l2rotP", weights["rotP"])
    mext, mext_k = ctx.const("l2mske", weights["maskext"])
    sqt, sqt_key = ctx.pool.alloc_cols(T)
    kst = sb.get("l2kst", 1, 8, F32)
    chb = sb.get("l2chb", 1, 2, BF16)
    chcol = [sb.get(f"l2chc{i}", 128, 1, F32) for i in range(2)]
    normr = sb.get("l2normr", 1, 512, BF16)
    rbcs = sb.get("l2rbcs", 128, 512, BF16)
    pv_ps, pv_psk = ctx.psf[4], "psf4"
    nm_ps, nm_psk = ctx.psf[5], "psf5"

    def rope_pair(dst_t, dst_k):
        for b in range(T // 512):
            c0 = b * 512
            p.load(ropwc[:, :], cos_dram[:, c0:c0 + 512], engine="sync",
                   writes=["ropwc"], tag="l2:cw")
            p.load(ropws[:, :], sin_dram[:, c0:c0 + 512], engine="sync",
                   writes=["ropws"], tag="l2:sw")
            pf, pfk = ctx.psum_f()
            p.t_matmul(pf[:, 0:512], rotP[:, :], dst_t[:, c0:c0 + 512],
                       True, True, reads=[rotP_k, dst_k], writes=[pfk],
                       tag="l2:rot")
            qc = sb.get("l2ropetmp", 128, 512, BF16)
            p.v("tensor_mul", qc[:, :], dst_t[:, c0:c0 + 512],
                ropwc[:, :], reads=[dst_k, "ropwc"], writes=[qc.name],
                tag="l2:ropec")
            p.v("tensor_mul", pf[:, 0:512], pf[:, 0:512], ropws[:, :],
                reads=[pfk, "ropws"], writes=[pfk], tag="l2:ropes")
            p.v("tensor_add", dst_t[:, c0:c0 + 512], qc[:, :], pf[:, 0:512],
                reads=[qc.name, pfk], writes=[dst_k], tag="l2:ropea", lowp=True)

    for pr in range(8):
        p.load(qp_t[:, :], qkv_dram[pr, :, :], engine="sync",
               reads=["l2qkv:%d" % pr], writes=[qp_k], tag="l2:ldq")
        p.load(kp_t[:, :], qkv_dram[8 + pr, :, :], engine="sync",
               reads=["l2qkv:%d" % (8 + pr)], writes=[kp_k], tag="l2:ldk")
        p.load(vp_t[:, :], qkv_dram[16 + pr, :, :], engine="sync",
               reads=["l2qkv:%d" % (16 + pr)], writes=[vp_k], tag="l2:ldv")
        rope_pair(qp_t, qp_k)
        rope_pair(kp_t, kp_k)
        for half in range(2):
            p.v("memset", vTs[half][0][:, :], 1.0, writes=[vTs[half][1]],
                tag="l2:vT1")
        for i in range(T // 128):
            vb, vbk = vTs[i // 16]
            base = (i % 16) * 130
            pb, pbk = ctx.psum_b()
            p.t_transpose(pb[:, 0:128], vp_t[:, i * 128:(i + 1) * 128],
                          identb[:, :], reads=[vp_k, identb_k], writes=[pbk],
                          tag="l2:vT")
            p.v("tensor_copy", vb[:, base:base + 64], pb[:, 0:64],
                reads=[pbk], writes=[vbk], tag="l2:vTc")
            p.v("tensor_copy", vb[:, base + 65:base + 129], pb[:, 64:128],
                reads=[pbk], writes=[vbk], tag="l2:vTc2")
        for hh in range(2):
            rb = hh * 64
            rb2 = hh * 65
            head = pr * 2 + hh
            kmax_parts = sb.get("l2kmx", 1, 16, F32)
            p.s("activation", sqt[rb:rb + 64, :], qp_t[rb:rb + 64, :],
                AF.Square, reads=[qp_k], writes=[sqt_key], tag="l2:sqq")
            for b in range(T // 512):
                pf, pfk = ctx.psum_f()
                p.t_matmul(pf[0:1, 0:512], onescol[rb:rb + 64, 0:1],
                           sqt[rb:rb + 64, b * 512:(b + 1) * 512], True, True,
                           reads=[onescol_k, sqt_key], writes=[pfk],
                           tag="l2:qn")
                p.v("reduce_max", kmax_parts[0:1, 8 + b:9 + b], pf[0:1, 0:512],
                    AX.X, reads=[pfk], writes=[kmax_parts.name], tag="l2:qnm")
            p.v("reduce_max", kst[0:1, 3:4], kmax_parts[0:1, 8:16], AX.X,
                reads=[kmax_parts.name], writes=[kst.name], tag="l2:qnm2")
            p.s("activation", kst[0:1, 4:5], kst[0:1, 3:4], AF.Sqrt,
                reads=[kst.name], writes=[kst.name], tag="l2:qns")
            p.s("activation", sqt[rb:rb + 64, :], kp_t[rb:rb + 64, :],
                AF.Square, reads=[kp_k], writes=[sqt_key], tag="l2:sqk")
            for b in range(T // 512):
                pf2, pfk2 = ctx.psum_f()
                p.t_matmul(pf2[0:1, 0:512], onescol[rb:rb + 64, 0:1],
                           sqt[rb:rb + 64, b * 512:(b + 1) * 512], True, True,
                           reads=[onescol_k, sqt_key], writes=[pfk2],
                           tag="l2:kn")
                p.v("reduce_max", kmax_parts[0:1, b:b + 1], pf2[0:1, 0:512],
                    AX.X, reads=[pfk2], writes=[kmax_parts.name], tag="l2:knm")
            p.v("reduce_max", kst[0:1, 0:1], kmax_parts[0:1, 0:8], AX.X,
                reads=[kmax_parts.name], writes=[kst.name], tag="l2:knm2")
            p.s("activation", kst[0:1, 1:2], kst[0:1, 0:1], AF.Sqrt,
                reads=[kst.name], writes=[kst.name], tag="l2:kns")
            p.v("tensor_scalar", kst[0:1, 5:6], kst[0:1, 4:5], kst[0:1, 1:2],
                -1.0, ALU.mult, ALU.mult, reads=[kst.name],
                writes=[kst.name], tag="l2:chm")
            p.v("tensor_copy", chb[0:1, 0:1], kst[0:1, 5:6],
                reads=[kst.name], writes=[chb.name], tag="l2:chb", lowp=True)
            ch = chcol[hh]
            pf3, pfk3 = ctx.psum_f()
            p.t_matmul(pf3[:, 0:1], onesrow[0:1, :], chb[0:1, 0:1], True,
                       True, reads=[onesrow_k, chb.name], writes=[pfk3],
                       tag="l2:chbc")
            p.v("tensor_copy", ch[:, 0:1], pf3[:, 0:1], reads=[pfk3],
                writes=[ch.name], tag="l2:chc")


        pvb = [(pv_ps, pv_psk), (nm_ps, nm_psk)]
        for qb in range(8):
            q0 = qb * 512
            nkt = 4 * (qb + 1)
            pend = []

            def flush_pv(pend=None, nkt=nkt):
                for kt_, hh_, pt_ in pend:
                    rb2 = hh_ * 65
                    pvp, pvk = pvb[hh_]
                    vb, vbk = vTs[kt_ // 16]
                    base = (kt_ % 16) * 130
                    p.t_matmul(pvp[0:65, 0:512],
                               vb[:, base + rb2:base + rb2 + 65],
                               pt_[:, :], kt_ == 0, kt_ == nkt - 1,
                               reads=[vbk, pt_.name], writes=[pvk],
                               tag="l2:pv")
                pend.clear()

            for kt in range(nkt):
                new = []
                for hh in range(2):
                    rb = hh * 64
                    ch = chcol[hh]
                    ps, psk = ctx.psum_f()
                    p.t_matmul(ps[:, 0:512],
                               kp_t[rb:rb + 64, kt * 128:(kt + 1) * 128],
                               qp_t[rb:rb + 64, q0:q0 + 512], True, True,
                               reads=[kp_k, qp_k], writes=[psk], tag="l2:sc")
                    pt = sb.get(f"l2pt{(kt % 2) * 2 + hh}", 128, 512, BF16)
                    p.s("activation", pt[:, :], ps[:, 0:512], AF.Exp,
                        bias=ch[:, 0:1], reads=[psk, ch.name],
                        writes=[pt.name], tag="l2:exp")
                    di = kt - 4 * qb
                    if di >= 0:
                        off = 384 - 128 * di
                        p.v("tensor_mul", pt[:, :], pt[:, :],
                            mext[:, off:off + 512],
                            reads=[pt.name, mext_k], writes=[pt.name],
                            tag="l2:msk")
                    new.append((kt, hh, pt))
                flush_pv(pend)
                pend.extend(new)
            flush_pv(pend)
            for hh in range(2):
                rb = hh * 64
                pvp, pvk = pvb[hh]
                nrh = sb.get("l2ropetmp", 128, 512, BF16) if hh == 0 \
                    else sb.get("l3qk", 128, 512, BF16)
                nr = nrh
                rbc = sb.get(f"l2gt{hh}", 128, 512, BF16)
                p.v("reciprocal", nr[0:1, 0:512], pvp[64:65, 0:512],
                    reads=[pvk], writes=[nr.name], tag="l2:rcp", lowp=True)
                pf, pfk = ctx.psum_f()
                p.t_matmul(pf[0:64, 0:512], onesrow[0:1, 0:64],
                           nr[0:1, 0:512], True, True,
                           reads=[onesrow_k, nr.name], writes=[pfk],
                           tag="l2:rbc")
                p.v("tensor_copy", rbc[0:64, 0:512], pf[0:64, 0:512],
                    reads=[pfk], writes=[rbc.name], tag="l2:rbcc")
                p.v("tensor_mul", attn_fm.chunks[pr][rb:rb + 64, q0:q0 + 512],
                    pvp[0:64, 0:512], rbc[0:64, 0:512],
                    reads=[pvk, rbc.name], writes=[attn_fm.key(pr)],
                    tag="l2:onrm", lowp=True)

    ctx.pool.release(pm_attn)
    ws_proj = WStream(ctx, "l2pw", weights["proj_w"], 512)
    y = ctx.pool.alloc_fm("ytt", D, 512, BF16)

    def ep_proj(ctx, ps, pskey, ocg, tt0, tw):
        p.s("copy", y.chunks[ocg][:, 0:tw], ps[:, :tw], reads=[pskey],
            writes=[y.key(ocg)], tag="l2:proj")

    for tt in range(T // 512):
        tt0 = tt * 512
        gemm_fm(ctx, attn_fm, ws_proj, ep_proj, t0=tt0, tn=tt0 + 512,
                tag=f"{tag}:proj")
        residual_add(ctx, z_dram, z_dram, y, tt0, 512, scale=1.0,
                     tag=f"{tag}:zres", res_key=zk)

    ctx.pool.reset()
    ws_rt = WStream(ctx, "l2rt", weights["router_p"], 128, nbuf=1)
    mr_t, mr_key = ctx.pool.alloc_cols(4096)
    maskb = sb.get("l2mb", 128, 1024, BF16)
    wv_dram = ctx.din("w_l2wv", weights["wvp"])
    proj_dram = ctx.din("w_l2proj", weights["projp"])
    h2 = ctx.pool.alloc_fm("l2h2", D, 1024, BF16)
    accfm = ctx.pool.alloc_fm("l2accfm", D, 1024, BF16)
    hid = ctx.pool.alloc_fm("l2hid", 4096, 1024, BF16)
    wvb = [ctx.pool.alloc_cols(2048) for _ in range(1)]
    prb = [ctx.pool.alloc_cols(4096) for _ in range(1)]
    rl_t, rl_key = ctx.pool.alloc_cols(1024)

    for q in range(4):
        q0 = q * 1024
        ln_pass(ctx, z_dram, h2, gamma=(ln2t, ln2k), beta=(ln2bt, ln2bk),
                src_key=zk, t0=q0, nt=1024, tag=f"{tag}:ln2")

        def ep_rl(ctx, ps, pskey, ocg, tt0, tw):
            p.s("copy", rl_t[:, tt0:tt0 + tw], ps[:, :tw],
                reads=[pskey], writes=[rl_key], tag="l2:rl")

        gemm_fm(ctx, h2, ws_rt, ep_rl, t0=0, tn=1024, tag=f"{tag}:rt")
        for jj in range(8):
            pb, pbk = ctx.psum_b()
            p.t_transpose(pb[:, 0:128], rl_t[:, jj * 128:(jj + 1) * 128],
                          identb[:, :], reads=[rl_key, identb_k],
                          writes=[pbk], tag="l2:rlT")
            lgT = sb.get("l2rlT", 128, 8, F32)
            p.v("tensor_copy", lgT[:, 0:4], pb[:, 0:4], reads=[pbk],
                writes=[lgT.name], tag="l2:rlTc")
            p.v("reduce_max", lgT[:, 4:5], lgT[:, 0:4], AX.X,
                reads=[lgT.name], writes=[lgT.name], tag="l2:rmx")
            p.v("tensor_scalar", lgT[:, 0:4], lgT[:, 0:4], lgT[:, 4:5], None,
                ALU.is_equal, reads=[lgT.name], writes=[lgT.name],
                tag="l2:req")
            for e in range(4):
                pg, pgk = ctx.psum_f()
                p.t_transpose(pg[0:1, 0:128], lgT[:, e:e + 1], identf[:, :],
                              reads=[lgT.name, identf_k], writes=[pgk],
                              tag="l2:mT")
                p.v("tensor_copy", mr_t[0:1, e * 1024 + jj * 128:e * 1024 + (jj + 1) * 128],
                    pg[0:1, 0:128], reads=[pgk], writes=[mr_key],
                    tag="l2:mr")

        for e in range(4):
            for hb in range(2):
                pf, pfk = ctx.psum_f()
                p.t_matmul(pf[:, 0:512], onesrow[0:1, :],
                           mr_t[0:1, e * 1024 + hb * 512:e * 1024 + (hb + 1) * 512],
                           True, True, reads=[onesrow_k, mr_key],
                           writes=[pfk], tag="l2:mb")
                p.v("tensor_copy", maskb[:, hb * 512:(hb + 1) * 512],
                    pf[:, 0:512], reads=[pfk], writes=[maskb.name],
                    tag="l2:mbc")
            for gc in range(32):
                bufap, bufkey = wvb[0]
                p.load(bufap[:, :], wv_dram[e, gc, :, :], engine="gpsimd",
                       writes=[bufkey], tag=f"l2:wv{e}")
                for th in range(2):
                    t0l = th * 512
                    psg, psgk = ctx.psum_f()
                    for k in range(8):
                        p.t_matmul(psg[:, 0:512], bufap[:, k * 256:k * 256 + 128],
                                   h2.chunks[k][:, t0l:t0l + 512],
                                   k == 0, k == 7, reads=[bufkey, h2.key(k)],
                                   writes=[psgk], tag="l2:hg")
                    psv, psvk = ctx.psum_f()
                    for k in range(8):
                        p.t_matmul(psv[:, 0:512], bufap[:, k * 256 + 128:k * 256 + 256],
                                   h2.chunks[k][:, t0l:t0l + 512],
                                   k == 0, k == 7, reads=[bufkey, h2.key(k)],
                                   writes=[psvk], tag="l2:hv")
                    gt = sb.get(f"l2gt{th}", 128, 512, BF16)
                    p.s("activation", gt[:, :], psg[:, 0:512], AF.Silu,
                        reads=[psgk], writes=[gt.name], tag="l2:hsilu")
                    p.v("tensor_mul", gt[:, :], gt[:, :], psv[:, 0:512],
                        reads=[gt.name, psvk], writes=[gt.name], tag="l2:hmul", lowp=True)
                    p.v("tensor_mul", hid.chunks[gc][:, t0l:t0l + 512],
                        gt[:, :], maskb[:, t0l:t0l + 512],
                        reads=[gt.name, maskb.name], writes=[hid.key(gc)],
                        tag="l2:hmsk", lowp=True)
            for ob in range(8):
                prab, prk = prb[0]
                p.load(prab[:, :], proj_dram[e, ob, :, :], engine="gpsimd",
                       writes=[prk], tag=f"l2:pr{e}")
                for th in range(2):
                    t0l = th * 512
                    pf, pfk = ctx.psum_f()
                    for k in range(32):
                        p.t_matmul(pf[:, 0:512], prab[:, k * 128:(k + 1) * 128],
                                   hid.chunks[k][:, t0l:t0l + 512],
                                   k == 0, k == 31, reads=[prk, hid.key(k)],
                                   writes=[pfk], tag="l2:prm")
                    if e == 0:
                        p.v("tensor_copy", accfm.chunks[ob][:, t0l:t0l + 512],
                            pf[:, 0:512], reads=[pfk], writes=[accfm.key(ob)],
                            tag="l2:acc0")
                    else:
                        p.v("tensor_add", accfm.chunks[ob][:, t0l:t0l + 512],
                            accfm.chunks[ob][:, t0l:t0l + 512], pf[:, 0:512],
                            reads=[accfm.key(ob), pfk], writes=[accfm.key(ob)],
                            tag="l2:acca", lowp=True)

        for jj in range(8):
            i = q * 8 + jj
            r = i % 2
            xt = sb.get(f"resx{r}", 128, D, F32)
            zt = sb.get(f"lnx{r}", 128, D, F32)
            p.load(xt[:, :], x_src[i * 128:(i + 1) * 128, :], engine="sync",
                   reads=[f"xres:{i}"], writes=[xt.name], tag="l2:cx")
            p.load(zt[:, :], z_dram[i * 128:(i + 1) * 128, :], engine="sync",
                   reads=[f"{zk}:{i}"], writes=[zt.name], tag="l2:cz")
            p.v("scalar_tensor_tensor", xt[:, :], zt[:, :], 0.5, xt[:, :],
                ALU.mult, ALU.add, reads=[zt.name, xt.name], writes=[xt.name],
                tag="l2:cza")
            for j in range(8):
                pf, pfk = ctx.psum_b()
                p.t_transpose(pf[:, 0:128],
                              accfm.chunks[j][:, jj * 128:(jj + 1) * 128],
                              identb[:, :], reads=[accfm.key(j), identb_k],
                              writes=[pfk], tag="l2:accT")
                p.v("scalar_tensor_tensor", xt[:, j * 128:(j + 1) * 128],
                    pf[:, 0:128], 0.5, xt[:, j * 128:(j + 1) * 128],
                    ALU.mult, ALU.add, reads=[pfk, xt.name], writes=[xt.name],
                    tag="l2:cacc")
            p.dma("sync", lambda e2, xt=xt, i=i: e2.dma_start(
                out=x_dst[i * 128:(i + 1) * 128, :], in_=xt[:, :]),
                reads=[xt.name], writes=[f"xres:{i}"], group=xt.name,
                tag="l2:cst")


def weights_negrow(ctx):
    t, k = ctx.const("negrow", np.full((1, 128), -1.0, BF16NP))
    return t[0:1, :]


def prep_l2_ffn(inp):
    C, NH, DH = 1024, 16, 64
    cw = inp["L2_ffn_conv_w"][:, 0, :]
    inv_freq = 1.0 / 10000.0 ** (np.arange(0, DH, 2) / DH)
    tpos = np.arange(T)
    fr = tpos[:, None] * inv_freq
    emb = np.concatenate([fr, fr], -1)
    cos = np.cos(emb).T.astype(np.float32)
    sin = np.sin(emb).T.astype(np.float32)
    cosp = np.concatenate([cos, cos], 0)
    sinp = np.concatenate([sin, sin], 0)
    sc = 1.0 / np.sqrt(DH)
    P = np.zeros((128, 128), np.float32)
    for base in (0, 64):
        for j in range(32):
            P[base + 32 + j, base + j] = -1.0
            P[base + j, base + 32 + j] = 1.0
    kl = np.arange(128)[:, None]
    cl = np.arange(896)[None, :]
    d = {"maskext": (kl <= cl - 384).astype(BF16NP)}
    wv = inp["L2_ffn_expert_wv"]
    wvp = np.zeros((4, 32, 128, 2048), np.float32)
    for e in range(4):
        for gc in range(32):
            pair = np.concatenate([wv[e][:, gc * 128:(gc + 1) * 128],
                                   wv[e][:, 4096 + gc * 128:4096 + (gc + 1) * 128]],
                                  1)
            wvp[e, gc] = pack_w(pair, 256)[0]
    router = np.zeros((1024, 128), np.float32)
    router[:, :4] = inp["L2_ffn_router_w"]
    out = {
        "ln1wc": np.ascontiguousarray(inp["L2_ffn_ln1_w"].reshape(-1, 128).T),
        "ln1bc": np.ascontiguousarray(inp["L2_ffn_ln1_b"].reshape(-1, 128).T),
        "ln2wc": np.ascontiguousarray(inp["L2_ffn_ln2_w"].reshape(-1, 128).T),
        "ln2bc": np.ascontiguousarray(inp["L2_ffn_ln2_b"].reshape(-1, 128).T),
        "cw0c": np.ascontiguousarray(cw[:, 0].reshape(-1, 128).T),
        "cw1c": np.ascontiguousarray(cw[:, 1].reshape(-1, 128).T),
        "cw2c": np.ascontiguousarray(cw[:, 2].reshape(-1, 128).T),
        "attn_w": np.ascontiguousarray(inp["L2_ffn_attn_w"]),
        "proj_w": np.ascontiguousarray(inp["L2_ffn_proj_w"]),
        "router_p": router,
        "cosp": cosp.astype(BF16NP),
        "sinp": sinp.astype(BF16NP),
        "rotP": P.astype(BF16NP),
        "wvp": wvp,
        "qscale": np.ascontiguousarray(
            np.concatenate([np.full(1024, sc), np.ones(2048)])
            .astype(np.float32).reshape(-1, 128).T),
        "projp": np.stack([pack_w(inp["L2_ffn_expert_proj"][e], 128)
                           for e in range(4)], 0),
    }
    out.update(d)
    return out



def emit_l3_attn(ctx, x_src, x_dst, weights, tag):
    p, sb = ctx.p, ctx.sb
    ctx.new_phase()
    identb, identb_k = ctx.ident_bf()
    identf, identf_k = ctx.ident_f32()
    onesrow, onesrow_k = ctx.ones_row_bf()
    TH = T // 2

    qs_dram = ctx.dscratch("l3a_qs", [4, 128, T], BF16)

    ln_bf = ctx.pool.alloc_fm("l3a_ln", D, T, BF16)
    ln_pass(ctx, x_src, ln_bf, tag=f"{tag}:ln")

    ws_q = WStream(ctx, "l3aq", weights["WqT"], 512)
    ws_k = WStream(ctx, "l3ak", weights["WkT"], 512, share=ws_q)
    ws_v = WStream(ctx, "l3av", weights["WvT"], 512, share=ws_q)
    bqt, bqk = col_tile(ctx, "l3abq", weights["bqc"])
    bkt, bkk = col_tile(ctx, "l3abk", weights["bkc"])
    bvt, bvk = col_tile(ctx, "l3abv", weights["bvc"])
    alphat, alphak = col_tile(ctx, "l3aal", weights["alphac"])
    augt, augk = col_tile(ctx, "l3aau", weights["augc"])
    ind = [ctx.const(f"l3hind{c}", weights[f"hind{c}"]) for c in range(4)]
    ind2 = [ctx.const(f"l3hbc{c}", weights[f"hbc{c}"]) for c in range(4)]

    dec = sb.get("l3dec", 128, 4, F32)
    dm1 = sb.get("l3dm1", 128, 4, F32)
    p.s("activation", dec[:, :], alphat[:, :], AF.Sigmoid, reads=[alphak],
        writes=[dec.name], tag="l3a:sig")
    p.v("tensor_scalar", dm1[:, :], dec[:, :], -1.0, 1.0, ALU.mult, ALU.add,
        reads=[dec.name], writes=[dm1.name], tag="l3a:dm1")

    attnT = ctx.pool.alloc_cols(T)
    attnT_t, attnT_k = attnT
    scanh = sb.get("l3shand", 128, 8, F32)
    p.v("memset", scanh[:, :], 0.0, writes=[scanh.name], tag="l3a:sh0")
    pmh = ctx.pool.mark()

    qk = sb.get("l3qk", 128, 512, BF16)
    _sf = sb.get("spillf0", 128, 512, F32)
    class _SfmView:
        name = _sf.name
        def __getitem__(self, idx):
            r, c = idx
            assert r == slice(None)
            return _sf[0:16, c]
    sfm = _SfmView()
    stt = sb.get("l3stok", 128, 16, F32)
    stt2 = sb.get("l3stok2", 128, 8, F32)

    def mk_ep(fm, bt, bk, loc0):
        def ep(ctx, ps, pskey, ocg, tt0, tw):
            p.v("tensor_scalar", fm.chunks[ocg - loc0][:, tt0:tt0 + tw],
                ps[:, :tw], bt[:, ocg:ocg + 1], None, ALU.add,
                reads=[pskey, bk], writes=[fm.key(ocg - loc0)], tag="l3a:qkv")
        return ep

    for hf in range(2):
        h0 = hf * TH
        ctx.pool.release(pmh)
        qfm = ctx.pool.alloc_fm("l3a_q", 512, TH, BF16)
        kfm = ctx.pool.alloc_fm("l3a_k", 512, TH, BF16)

        gemm_fm(ctx, ln_bf, ws_q, mk_ep(qfm, bqt, bqk, 0), t0=h0,
                tn=h0 + TH, toff=-h0, tag=f"{tag}:q")
        gemm_fm(ctx, ln_bf, ws_k, mk_ep(kfm, bkt, bkk, 0), t0=h0,
                tn=h0 + TH, toff=-h0, tag=f"{tag}:k")

        for tb in range(TH // 512):
            b0 = tb * 512
            ps, psk = ctx.psum_f()
            for c in range(4):
                p.v("tensor_mul", qk[:, :], qfm.chunks[c][:, b0:b0 + 512],
                    kfm.chunks[c][:, b0:b0 + 512],
                    reads=[qfm.key(c), kfm.key(c)], writes=[qk.name],
                    tag="l3a:qk")
                p.t_matmul(ps[0:16, 0:512], ind[c][0][:, :], qk[:, :],
                           c == 0, c == 3, reads=[ind[c][1], qk.name],
                           writes=[psk], tag="l3a:ssum")
            p.v("tensor_scalar", sfm[:, 0:512], ps[0:16, 0:512], 0.125, None,
                ALU.mult, reads=[psk], writes=[sfm.name], tag="l3a:ssc")
            for j in range(4):
                pf, pfk = ctx.psum_f()
                p.t_transpose(pf[0:128, 0:16], sfm[:, j * 128:(j + 1) * 128],
                              identf[0:16, 0:16], reads=[sfm.name, identf_k],
                              writes=[pfk], tag="l3a:sT")
                p.v("tensor_copy", stt[:, 0:16], pf[0:128, 0:16], reads=[pfk],
                    writes=[stt.name], tag="l3a:sTc")
                p.v("reduce_max", stt2[:, 0:1], stt[:, 0:8], AX.X,
                    reads=[stt.name], writes=[stt2.name], tag="l3a:smx")
                p.v("tensor_scalar", stt[:, 0:8], stt[:, 0:8], stt2[:, 0:1],
                    None, ALU.subtract, reads=[stt.name, stt2.name],
                    writes=[stt.name], tag="l3a:ssub")
                p.s("activation", stt[:, 0:8], stt[:, 0:8], AF.Exp,
                    accum_out=stt2[:, 1:2], reads=[stt.name],
                    writes=[stt.name, stt2.name], tag="l3a:sexp")
                p.v("reciprocal", stt2[:, 2:3], stt2[:, 1:2],
                    reads=[stt2.name], writes=[stt2.name], tag="l3a:srcp")
                p.v("tensor_scalar", stt[:, 0:8], stt[:, 0:8], stt2[:, 2:3],
                    None, ALU.mult, reads=[stt.name, stt2.name],
                    writes=[stt.name], tag="l3a:snrm", lowp=True)
                pb, pbk = ctx.psum_f()
                p.t_transpose(pb[0:16, 0:128], stt[:, 0:16], identf[:, :],
                              reads=[stt.name, identf_k], writes=[pbk],
                              tag="l3a:sTb")
                p.v("tensor_copy",
                    attnT_t[0:16, h0 + b0 + j * 128:h0 + b0 + (j + 1) * 128],
                    pb[0:16, 0:128], reads=[pbk], writes=[attnT_k],
                    tag="l3a:aTc")

        ctx.pool.release(pmh)
        qfm2 = ctx.pool.alloc_fm("l3a_q", 512, TH, BF16)
        vfm = ctx.pool.alloc_fm("l3a_v", 512, TH, BF16)
        gemm_fm(ctx, ln_bf, ws_v, mk_ep(vfm, bvt, bvk, 0), t0=h0,
                tn=h0 + TH, toff=-h0, tag=f"{tag}:v")
        for c in range(4):
            for tb in range(TH // 512):
                b0 = tb * 512
                pf, pfk = ctx.psum_f()
                p.t_matmul(pf[:, 0:512], ind2[c][0][:, :],
                           attnT_t[0:16, h0 + b0:h0 + b0 + 512], True, True,
                           reads=[ind2[c][1], attnT_k], writes=[pfk],
                           tag="l3a:abc")
                p.v("scalar_tensor_tensor", vfm.chunks[c][:, b0:b0 + 512],
                    pf[:, 0:512], dm1[:, c:c + 1], vfm.chunks[c][:, b0:b0 + 512],
                    ALU.mult, ALU.mult, reads=[pfk, dm1.name, vfm.key(c)],
                    writes=[vfm.key(c)], tag="l3a:u", lowp=True)
            init = 0.0 if hf == 0 else scanh[:, c:c + 1]
            rds = [dec.name, vfm.key(c)] + ([] if hf == 0 else [scanh.name])
            p.v("tensor_tensor_scan", vfm.chunks[c][:, 0:TH],
                dec[:, c:c + 1].broadcast_to((128, TH)),
                vfm.chunks[c][:, 0:TH], init, ALU.mult, ALU.add, reads=rds,
                writes=[vfm.key(c)], tag="l3a:scan", lowp=True)
            if hf == 0:
                p.v("tensor_copy", scanh[:, c:c + 1],
                    vfm.chunks[c][:, TH - 1:TH], reads=[vfm.key(c)],
                    writes=[scanh.name], tag="l3a:shc")
            for tb in range(TH // 512):
                b0 = tb * 512
                p.s("activation", qk[:, :], qfm2.chunks[c][:, b0:b0 + 512],
                    AF.Sigmoid, scale=augt[:, c:c + 1],
                    reads=[qfm2.key(c), augk], writes=[qk.name], tag="l3a:aug")
                p.v("tensor_mul", qfm2.chunks[c][:, b0:b0 + 512],
                    qfm2.chunks[c][:, b0:b0 + 512],
                    vfm.chunks[c][:, b0:b0 + 512],
                    reads=[qfm2.key(c), vfm.key(c)], writes=[qfm2.key(c)],
                    tag="l3a:qs", lowp=True)
                p.v("tensor_add", qfm2.chunks[c][:, b0:b0 + 512],
                    qfm2.chunks[c][:, b0:b0 + 512], qk[:, :],
                    reads=[qfm2.key(c), qk.name], writes=[qfm2.key(c)],
                    tag="l3a:qsa", lowp=True)
            p.dma("sync", lambda e, c=c, hf=hf, ap=qfm2.chunks[c]: e.dma_start(
                out=qs_dram[c, :, hf * TH:(hf + 1) * TH], in_=ap[:, 0:TH]),
                reads=[qfm2.key(c)], writes=[f"l3a_qs:{c}:{hf}"],
                group=qfm2.key(c), tag="l3a:qst")

    ctx.pool.release(pmh)
    gbp, gbp_k = ctx.pool.alloc_cols(2 * D)
    lrow = sb.get("spill0", 128, 512, BF16)
    lnsc = ctx.din("w_l3alns", weights["ln_scale_row"])
    lnbi = ctx.din("w_l3alnb", weights["ln_bias_row"])
    for src, off in ((lnsc, 0), (lnbi, D)):
        for hb in range(2):
            p.load(lrow[0:1, :], src[0:1, hb * 512:(hb + 1) * 512],
                   engine="gpsimd", writes=[lrow.name], tag="l3a:lr")
            pf, pfk = ctx.psum_f()
            p.t_matmul(pf[:, 0:512], onesrow[0:1, :], lrow[0:1, :], True,
                       True, reads=[onesrow_k, lrow.name], writes=[pfk],
                       tag="l3a:gb")
            p.v("tensor_copy", gbp[:, off + hb * 512:off + (hb + 1) * 512],
                pf[:, 0:512], reads=[pfk], writes=[gbp_k], tag="l3a:gbc",
                lowp=True)
    gbc = gbp[:, 0:D]
    bbc = gbp[:, D:2 * D]

    ws_o = WStream(ctx, "l3ao", weights["WoT"], 512, share=ws_q)
    bot, bok = col_tile(ctx, "l3abo", weights["boc"])
    y = ctx.pool.alloc_fm("ytt", D, 512, BF16)
    qb2 = [ctx.pool.alloc_fm(f"l3a_qb{i}", 512, 512, BF16) for i in range(2)]

    def ep_o(ctx, ps, pskey, ocg, tt0, tw):
        p.v("tensor_scalar", y.chunks[ocg][:, 0:tw], ps[:, :tw],
            bot[:, ocg:ocg + 1], None, ALU.add, reads=[pskey, bok],
            writes=[y.key(ocg)], tag="l3a:o")

    for tt in range(T // 512):
        tt0 = tt * 512
        qb = qb2[tt % 2]
        for c in range(4):
            p.load(qb.chunks[c][:, 0:512], qs_dram[c, :, tt0:tt0 + 512],
                   engine="sync", reads=[f"l3a_qs:{c}:{tt0 // TH}"],
                   writes=[qb.key(c)], tag="l3a:qbl")
        gemm_fm(ctx, qb, ws_o, ep_o, t0=0, tn=512, toff=tt0, tag=f"{tag}:wo")
        for ii in range(4):
            i = tt * 4 + ii
            r = i % 2
            yt = sb.get(f"resx{r}", 128, D, F32)
            st = sb.get(f"lnst{r}", 128, 8, F32)
            xt = sb.get(f"lnx{r}", 128, D, F32)
            for j in range(8):
                pb, pbk = ctx.psum_b()
                p.t_transpose(pb[:, 0:128],
                              y.chunks[j][:, ii * 128:(ii + 1) * 128],
                              identb[:, :], reads=[y.key(j), identb_k],
                              writes=[pbk], tag="l3a:yT")
                p.v("tensor_copy", yt[:, j * 128:(j + 1) * 128], pb[:, 0:128],
                    reads=[pbk], writes=[yt.name], tag="l3a:yTc")
            p.v("tensor_reduce", st[:, 0:1], yt[:, :], AX.X, ALU.add,
                reads=[yt.name], writes=[st.name], tag="l3a:lsum")
            p.v("tensor_scalar", st[:, 0:1], st[:, 0:1], 1.0 / D, None,
                ALU.mult, reads=[st.name], writes=[st.name], tag="l3a:lm")
            p.v("tensor_scalar", yt[:, :], yt[:, :], st[:, 0:1], None,
                ALU.subtract, reads=[yt.name, st.name], writes=[yt.name],
                tag="l3a:lsub")
            sq = sb.get(f"lnt{r}", 128, D, BF16)
            p.s("activation", sq[:, :], yt[:, :], AF.Square,
                accum_out=st[:, 1:2], reads=[yt.name],
                writes=[sq.name, st.name], tag="l3a:lsq")
            p.v("tensor_scalar", st[:, 2:3], st[:, 1:2], 1.0 / D, 1e-6,
                ALU.mult, ALU.add, reads=[st.name], writes=[st.name],
                tag="l3a:lvar")
            p.s("activation", st[:, 3:4], st[:, 2:3], AF.Sqrt,
                reads=[st.name], writes=[st.name], tag="l3a:lsd")
            p.v("reciprocal", st[:, 4:5], st[:, 3:4], reads=[st.name],
                writes=[st.name], tag="l3a:lrs")
            p.v("tensor_scalar", yt[:, :], yt[:, :], st[:, 4:5], None,
                ALU.mult, reads=[yt.name, st.name], writes=[yt.name],
                tag="l3a:lnorm")
            p.v("tensor_mul", yt[:, :], yt[:, :], gbc[:, :],
                reads=[yt.name, gbp_k], writes=[yt.name], tag="l3a:lgam")
            p.v("tensor_add", yt[:, :], yt[:, :], bbc[:, :],
                reads=[yt.name, gbp_k], writes=[yt.name], tag="l3a:lbet")
            p.load(xt[:, :], x_src[i * 128:(i + 1) * 128, :], engine="sync",
                   reads=[f"xres:{i}"], writes=[xt.name], tag="l3a:rx")
            p.v("tensor_add", xt[:, :], xt[:, :], yt[:, :],
                reads=[xt.name, yt.name], writes=[xt.name], tag="l3a:racc")
            p.dma("sync", lambda e, xt=xt, i=i: e.dma_start(
                out=x_dst[i * 128:(i + 1) * 128, :], in_=xt[:, :]),
                reads=[xt.name], writes=[f"xres:{i}"], group=xt.name,
                tag="l3a:rs")


def prep_l3_attn(inp):
    hind = {}
    for c in range(4):
        m = np.zeros((128, 16), BF16NP)
        for j in range(2):
            m[j * 64:(j + 1) * 64, c * 2 + j] = 1
        hind[f"hind{c}"] = m
        m2 = np.zeros((16, 128), BF16NP)
        for j in range(2):
            m2[c * 2 + j, j * 64:(j + 1) * 64] = 1
        hind[f"hbc{c}"] = m2
    out = {
        "WqT": np.ascontiguousarray(inp["L3_attn_Wq"]),
        "WkT": np.ascontiguousarray(inp["L3_attn_Wk"]),
        "WvT": np.ascontiguousarray(inp["L3_attn_Wv"]),
        "WoT": np.ascontiguousarray(inp["L3_attn_Wo"]),
        "bqc": np.ascontiguousarray(inp["L3_attn_bq"].reshape(-1, 128).T),
        "bkc": np.ascontiguousarray(inp["L3_attn_bk"].reshape(-1, 128).T),
        "bvc": np.ascontiguousarray(inp["L3_attn_bv"].reshape(-1, 128).T),
        "boc": np.ascontiguousarray(inp["L3_attn_bo"].reshape(-1, 128).T),
        "alphac": np.ascontiguousarray(
            inp["L3_attn_alpha"].reshape(-1).reshape(-1, 128).T),
        "augc": np.ascontiguousarray(
            inp["L3_attn_aug_param"].reshape(-1).reshape(-1, 128).T),
        "ln_scale_row": inp["L3_attn_ln_scale"][None, :].astype(BF16NP),
        "ln_bias_row": inp["L3_attn_ln_bias"][None, :].astype(BF16NP),
    }
    out.update(hind)
    return out



def ep_store_tok(ctx, asm, asm_keys, identb, identb_k, nchunk):
    p, sb = ctx.p, ctx.sb

    def ep(ctx, ps, pskey, ocg, tt0, tw):
        r = ocg % 2
        st = sb.get(f"spill{r}", 128, 512, BF16)
        p.s("copy", st[:, :tw], ps[:, :tw], reads=[pskey], writes=[st.name],
            tag="est:cp")
        for u in range(tw // 128):
            pb, pbk = ctx.psum_b()
            p.t_transpose(pb[:, 0:128], st[:, u * 128:(u + 1) * 128],
                          identb[:, :], reads=[st.name, identb_k],
                          writes=[pbk], tag="est:tr")
            p.v("tensor_copy", asm[u][:, ocg * 128:(ocg + 1) * 128],
                pb[:, 0:128], reads=[pbk], writes=[asm_keys[u]], tag="est:ac")
    return ep


def emit_l1_attn(ctx, x_src, x_dst, weights, tag):
    p, sb = ctx.p, ctx.sb
    ctx.new_phase()
    identb, identb_k = ctx.ident_bf()
    onescol, onescol_k = ctx.const("onescol", np.ones((128, 1), BF16NP))
    onesrow, onesrow_k = ctx.ones_row_bf()
    negrow = weights_negrow(ctx)

    q_dram = ctx.dscratch("l1a_q", [T, D], BF16)
    k_dram = ctx.dscratch("l1a_k", [T, D], BF16)
    v_dram = ctx.dscratch("l1a_v", [T, D], BF16)
    b1_dram = ctx.dscratch("l1a_b1", [D, T], BF16)
    b2_dram = ctx.dscratch("l1a_b2", [T, D], BF16)

    ln_bf = ctx.pool.alloc_fm("l1a_ln", D, T, BF16)
    ln_pass(ctx, x_src, ln_bf, tag=f"{tag}:ln")
    ws_q = WStream(ctx, "l1aq", weights["WqT"], 512)
    ws_k = WStream(ctx, "l1ak", weights["WkT"], 512, share=ws_q)
    ws_v = WStream(ctx, "l1av", weights["WvT"], 512, share=ws_q)
    asm = [sb.get(f"l1asm{u}", 128, D, BF16) for u in range(4)]
    asm_keys = [a.name for a in asm]

    for tt in range(T // 512):
        tt0 = tt * 512
        for ws, dram, dname in ((ws_q, q_dram, "l1aq"), (ws_k, k_dram, "l1ak"),
                                (ws_v, v_dram, "l1av")):
            gemm_fm(ctx, ln_bf, ws, ep_store_tok(ctx, asm, asm_keys, identb,
                                                 identb_k, 8),
                    t0=tt0, tn=tt0 + 512, tag=f"{tag}:{dname}")
            for u in range(4):
                i = tt * 4 + u
                p.dma("sync", lambda e, a=asm[u], i=i, dram=dram: e.dma_start(
                    out=dram[i * 128:(i + 1) * 128, :], in_=a[:, :]),
                    reads=[asm_keys[u]], writes=[f"{dname}:{i}"],
                    group=asm_keys[u], tag=f"{tag}:st{dname}")

    ctx.pool.reset()
    attn_fm = ctx.pool.alloc_fm("l1a_attn", D, T, BF16)
    qtok, qtok_k = ctx.pool.alloc_cols(T)
    ktok, ktok_k = ctx.pool.alloc_cols(T)
    vTs = [ctx.pool.alloc_cols(16 * 130) for _ in range(2)]
    qp_t, qp_k = ctx.pool.alloc_cols(T)
    kp_t, kp_k = ctx.pool.alloc_cols(T)
    sqt, sqt_key = ctx.pool.alloc_cols(T)
    chb = sb.get("l2chb", 1, 2, BF16)
    chcol = [sb.get(f"l2chc{i}", 128, 1, F32) for i in range(2)]
    cos_dram = ctx.din("w_l1cos", weights["cosp"])
    sin_dram = ctx.din("w_l1sin", weights["sinp"])
    ropwc = sb.get("ropwc", 128, 512, BF16)
    ropws = sb.get("ropws", 128, 512, BF16)
    rotP, rotP_k = ctx.const("l1rotP", weights["rotP"])
    kst = sb.get("l2kst", 1, 8, F32)
    normr = sb.get("l2normr", 1, 512, BF16)
    rbcs = sb.get("l2rbcs", 128, 512, BF16)
    pv_ps, pv_psk = ctx.psf[4], "psf4"
    nm_ps, nm_psk = ctx.psf[5], "psf5"

    def load_head_pair(dst, dst_key, dram, dname, pr):
        for hh in range(2):
            h = pr * 2 + hh
            src = dram[h * 256:(h + 1) * 256, :].rearrange(
                "r (m d) -> (r m) d", d=64).rearrange("(i p) d -> p i d", p=128)
            dst_ap = dst.rearrange("p (i d) -> p i d", d=128)[
                :, :, hh * 64:(hh + 1) * 64]
            p.dma("sync", lambda e, dst_ap=dst_ap, src=src: e.dma_start(
                out=dst_ap, in_=src),
                reads=[f"{dname}:{2 * h}", f"{dname}:{2 * h + 1}"],
                writes=[dst_key], group=dst_key, tag=f"{tag}:ldh")

    def load_v_pair(dram, dname, pr):
        for half in range(2):
            p.v("memset", vTs[half][0][:, :], 1.0, writes=[vTs[half][1]],
                tag="l1:v1")
        for hh in range(2):
            h = pr * 2 + hh
            srcf = dram[h * 256:(h + 1) * 256, :].rearrange(
                "r (m d) -> (r m) d", d=64).rearrange("(i p) d -> p i d", p=128)
            for half in range(2):
                vb, vbk = vTs[half]
                src = srcf[:, half * 16:(half + 1) * 16, :]
                dst_ap = vb.rearrange("p (i d) -> p i d", d=130)[
                    :, :, hh * 65:hh * 65 + 64]
                p.dma("sync", lambda e, dst_ap=dst_ap, src=src: e.dma_start(
                    out=dst_ap, in_=src),
                    reads=[f"{dname}:{2 * h}", f"{dname}:{2 * h + 1}"],
                    writes=[vbk], group=vbk, tag=f"{tag}:ldv")

    def rope_pair(dst_t, dst_k):
        for b in range(T // 512):
            c0 = b * 512
            p.load(ropwc[:, :], cos_dram[:, c0:c0 + 512], engine="sync",
                   writes=["ropwc"], tag="l1:cw")
            p.load(ropws[:, :], sin_dram[:, c0:c0 + 512], engine="sync",
                   writes=["ropws"], tag="l1:sw")
            pf, pfk = ctx.psum_f()
            p.t_matmul(pf[:, 0:512], rotP[:, :], dst_t[:, c0:c0 + 512],
                       True, True, reads=[rotP_k, dst_k], writes=[pfk],
                       tag="l1:rot")
            qc = sb.get("l2ropetmp", 128, 512, BF16)
            p.v("tensor_mul", qc[:, :], dst_t[:, c0:c0 + 512],
                ropwc[:, :], reads=[dst_k, "ropwc"], writes=[qc.name],
                tag="l1:ropec")
            p.v("tensor_mul", pf[:, 0:512], pf[:, 0:512], ropws[:, :],
                reads=[pfk, "ropws"], writes=[pfk], tag="l1:ropes")
            p.v("tensor_add", dst_t[:, c0:c0 + 512], qc[:, :], pf[:, 0:512],
                reads=[qc.name, pfk], writes=[dst_k], tag="l1:ropea",
                lowp=True)

    for pr in range(8):
        load_head_pair(qtok, qtok_k, q_dram, "l1aq", pr)
        load_head_pair(ktok, ktok_k, k_dram, "l1ak", pr)
        load_v_pair(v_dram, "l1av", pr)
        for i in range(T // 128):
            for src_t, src_k, dst_t, dst_k in ((qtok, qtok_k, qp_t, qp_k),
                                               (ktok, ktok_k, kp_t, kp_k)):
                pb, pbk = ctx.psum_b()
                p.t_transpose(pb[:, 0:128], src_t[:, i * 128:(i + 1) * 128],
                              identb[:, :], reads=[src_k, identb_k],
                              writes=[pbk], tag="l1:qT")
                p.v("tensor_copy", dst_t[:, i * 128:(i + 1) * 128],
                    pb[:, 0:128], reads=[pbk], writes=[dst_k], tag="l1:qTc")
        rope_pair(qp_t, qp_k)
        rope_pair(kp_t, kp_k)
        p.v("tensor_scalar", qp_t[:, :], qp_t[:, :], 0.25, None, ALU.mult,
            reads=[qp_k], writes=[qp_k], tag="l1:qsc", lowp=True)
        for hh in range(2):
            rb = hh * 64
            rb2 = hh * 65
            kmax_parts = sb.get("l2kmx", 1, 16, F32)
            p.s("activation", sqt[rb:rb + 64, :], qp_t[rb:rb + 64, :],
                AF.Square, reads=[qp_k], writes=[sqt_key], tag="l1:sqq")
            for b in range(T // 512):
                pf, pfk = ctx.psum_f()
                p.t_matmul(pf[0:1, 0:512], onescol[rb:rb + 64, 0:1],
                           sqt[rb:rb + 64, b * 512:(b + 1) * 512], True, True,
                           reads=[onescol_k, sqt_key], writes=[pfk],
                           tag="l1:qn")
                p.v("reduce_max", kmax_parts[0:1, 8 + b:9 + b], pf[0:1, 0:512],
                    AX.X, reads=[pfk], writes=[kmax_parts.name], tag="l1:qnm")
            p.v("reduce_max", kst[0:1, 3:4], kmax_parts[0:1, 8:16], AX.X,
                reads=[kmax_parts.name], writes=[kst.name], tag="l1:qnm2")
            p.s("activation", kst[0:1, 4:5], kst[0:1, 3:4], AF.Sqrt,
                reads=[kst.name], writes=[kst.name], tag="l1:qns")
            p.s("activation", sqt[rb:rb + 64, :], kp_t[rb:rb + 64, :],
                AF.Square, reads=[kp_k], writes=[sqt_key], tag="l1:sqk")
            for b in range(T // 512):
                pf2, pfk2 = ctx.psum_f()
                p.t_matmul(pf2[0:1, 0:512], onescol[rb:rb + 64, 0:1],
                           sqt[rb:rb + 64, b * 512:(b + 1) * 512], True, True,
                           reads=[onescol_k, sqt_key], writes=[pfk2],
                           tag="l1:kn")
                p.v("reduce_max", kmax_parts[0:1, b:b + 1], pf2[0:1, 0:512],
                    AX.X, reads=[pfk2], writes=[kmax_parts.name], tag="l1:knm")
            p.v("reduce_max", kst[0:1, 0:1], kmax_parts[0:1, 0:8], AX.X,
                reads=[kmax_parts.name], writes=[kst.name], tag="l1:knm2")
            p.s("activation", kst[0:1, 1:2], kst[0:1, 0:1], AF.Sqrt,
                reads=[kst.name], writes=[kst.name], tag="l1:kns")
            p.v("tensor_scalar", kst[0:1, 5:6], kst[0:1, 4:5], kst[0:1, 1:2],
                -1.0, ALU.mult, ALU.mult, reads=[kst.name],
                writes=[kst.name], tag="l1:chm")
            p.v("tensor_copy", chb[0:1, 0:1], kst[0:1, 5:6],
                reads=[kst.name], writes=[chb.name], tag="l1:chb", lowp=True)
            ch = chcol[hh]
            pf3, pfk3 = ctx.psum_f()
            p.t_matmul(pf3[:, 0:1], onesrow[0:1, :], chb[0:1, 0:1], True,
                       True, reads=[onesrow_k, chb.name], writes=[pfk3],
                       tag="l1:chbc")
            p.v("tensor_copy", ch[:, 0:1], pf3[:, 0:1], reads=[pfk3],
                writes=[ch.name], tag="l1:chc")


        pvb = [(pv_ps, pv_psk), (nm_ps, nm_psk)]
        for qb in range(8):
            q0 = qb * 512
            pend = []

            def flush_pv(pend=None):
                for kt_, hh_, pt_ in pend:
                    rb2 = hh_ * 65
                    pvp, pvk = pvb[hh_]
                    vb, vbk = vTs[kt_ // 16]
                    base = (kt_ % 16) * 130
                    p.t_matmul(pvp[0:65, 0:512],
                               vb[:, base + rb2:base + rb2 + 65],
                               pt_[:, :], kt_ == 0, kt_ == 31,
                               reads=[vbk, pt_.name], writes=[pvk],
                               tag="l1:pv")
                pend.clear()

            for kt in range(32):
                new = []
                for hh in range(2):
                    rb = hh * 64
                    ch = chcol[hh]
                    ps, psk = ctx.psum_f()
                    p.t_matmul(ps[:, 0:512],
                               kp_t[rb:rb + 64, kt * 128:(kt + 1) * 128],
                               qp_t[rb:rb + 64, q0:q0 + 512], True, True,
                               reads=[kp_k, qp_k], writes=[psk], tag="l1:sc")
                    pt = sb.get(f"l2pt{(kt % 2) * 2 + hh}", 128, 512, BF16)
                    p.s("activation", pt[:, :], ps[:, 0:512], AF.Exp,
                        bias=ch[:, 0:1], reads=[psk, ch.name],
                        writes=[pt.name], tag="l1:exp")
                    new.append((kt, hh, pt))
                flush_pv(pend)
                pend.extend(new)
            flush_pv(pend)
            for hh in range(2):
                rb = hh * 64
                pvp, pvk = pvb[hh]
                nrh = sb.get("l2ropetmp", 128, 512, BF16) if hh == 0 \
                    else sb.get("l3qk", 128, 512, BF16)
                nr = nrh
                rbc = sb.get(f"l2gt{hh}", 128, 512, BF16)
                p.v("reciprocal", nr[0:1, 0:512], pvp[64:65, 0:512],
                    reads=[pvk], writes=[nr.name], tag="l1:rcp", lowp=True)
                pf, pfk = ctx.psum_f()
                p.t_matmul(pf[0:64, 0:512], onesrow[0:1, 0:64],
                           nr[0:1, 0:512], True, True,
                           reads=[onesrow_k, nr.name], writes=[pfk],
                           tag="l1:rbc")
                p.v("tensor_copy", rbc[0:64, 0:512], pf[0:64, 0:512],
                    reads=[pfk], writes=[rbc.name], tag="l1:rbcc")
                p.v("tensor_mul", attn_fm.chunks[pr][rb:rb + 64, q0:q0 + 512],
                    pvp[0:64, 0:512], rbc[0:64, 0:512],
                    reads=[pvk, rbc.name], writes=[attn_fm.key(pr)],
                    tag="l1:onrm", lowp=True)

    for c in range(8):
        p.dma("sync", lambda e, c=c: e.dma_start(
            out=b1_dram[c * 128:(c + 1) * 128, :], in_=attn_fm.chunks[c][:, :]),
            reads=[attn_fm.key(c)], writes=[f"l1b1:{c}"],
            group="l1b1st", tag=f"{tag}:b1")

    ctx.pool.reset()
    newfm = ctx.pool.alloc_fm("l1a_new", D, T, BF16)
    b1v = b1_dram.rearrange("a b -> (a b)").rearrange("(t d) -> t d", d=D)
    for i in range(T // 128):
        r = i % 2
        nt = sb.get(f"lnt{r}", 128, D, BF16)
        p.load(nt[:, :], b1v[i * 128:(i + 1) * 128, :], engine="sync",
               reads=[f"l1b1:{i // 4}"], writes=[nt.name],
               tag=f"{tag}:ldnew")
        for j in range(8):
            pb, pbk = ctx.psum_b()
            p.t_transpose(pb[:, 0:128], nt[:, j * 128:(j + 1) * 128],
                          identb[:, :], reads=[nt.name, identb_k],
                          writes=[pbk], tag="l1:nT")
            p.v("tensor_copy", newfm.chunks[j][:, i * 128:(i + 1) * 128],
                pb[:, 0:128], reads=[pbk], writes=[newfm.key(j)],
                tag="l1:nTc")

    ws_r = WStream(ctx, "l1ar", weights["WrT"], 512)
    for tt in range(T // 512):
        tt0 = tt * 512
        gemm_fm(ctx, newfm, ws_r, ep_store_tok(ctx, asm, asm_keys, identb,
                                               identb_k, 8),
                t0=tt0, tn=tt0 + 512, tag=f"{tag}:wr")
        for u in range(4):
            i = tt * 4 + u
            p.dma("sync", lambda e, a=asm[u], i=i: e.dma_start(
                out=b2_dram[i * 128:(i + 1) * 128, :], in_=a[:, :]),
                reads=[asm_keys[u]], writes=[f"l1b2:{i}"],
                group=asm_keys[u], tag=f"{tag}:stb2")

    b2v = b2_dram.rearrange("a b -> (a b)").rearrange("(d t) -> d t", t=T)
    y = ctx.pool.alloc_fm("ytt", D, 512, BF16)
    for tt in range(T // 512):
        tt0 = tt * 512
        for c in range(8):
            p.load(y.chunks[c][:, 0:512], b2v[c * 128:(c + 1) * 128, tt0:tt0 + 512],
                   engine="sync",
                   reads=[f"l1b2:{c * 4 + j}" for j in range(4)],
                   writes=[y.key(c)], tag=f"{tag}:ldy")
        residual_add(ctx, x_src, x_dst, y, tt0, 512, scale=1.0,
                     tag=f"{tag}:res")


def prep_l1_attn(inp):
    DH, ROT = 64, 32
    inv_freq = 1.0 / 10000.0 ** (np.arange(0, ROT, 2) / ROT)
    tpos = np.arange(T)
    fr = tpos[:, None] * inv_freq
    fr2 = np.repeat(fr, 2, axis=-1)
    cos = np.ones((T, DH))
    sin = np.zeros((T, DH))
    cos[:, :ROT] = np.cos(fr2)
    sin[:, :ROT] = np.sin(fr2)
    cosp = np.concatenate([cos.T, cos.T], 0).astype(BF16NP)
    sinp = np.concatenate([sin.T, sin.T], 0).astype(BF16NP)
    P = np.zeros((128, 128), np.float32)
    for base in (0, 64):
        for i2 in range(16):
            P[base + 2 * i2 + 1, base + 2 * i2] = -1.0
            P[base + 2 * i2, base + 2 * i2 + 1] = 1.0
    return {
        "WqT": np.ascontiguousarray(inp["L1_attn_Wq"].T),
        "WkT": np.ascontiguousarray(inp["L1_attn_Wk"].T),
        "WvT": np.ascontiguousarray(inp["L1_attn_Wv"].T),
        "WrT": np.ascontiguousarray(inp["L1_attn_Wr"].T),
        "cosp": cosp,
        "sinp": sinp,
        "rotP": P.astype(BF16NP),
    }



def emit_l0_attn(ctx, x_src, x_dst, weights, tag):
    p, sb = ctx.p, ctx.sb
    NH, DK, DV, C = 4, 128, 256, 32
    NCHK = T // C
    ctx.new_phase()
    identb, identb_k = ctx.ident_bf()
    identf, identf_k = ctx.ident_f32()
    onesrow, onesrow_k = ctx.ones_row_bf()

    r_dram = ctx.dscratch("l0_r", [4, 128, T], BF16)
    k_dram = ctx.dscratch("l0_k", [4, 128, T], BF16)
    v_dram = ctx.dscratch("l0_v", [8, 128, T], BF16)
    w_dram = ctx.dscratch("l0_w", [4, 128, T], F32)
    g_dram = ctx.dscratch("l0_g", [T, D], BF16)
    o_dram = ctx.dscratch("l0_o", [8, 128, T], BF16)
    o2_dram = ctx.dscratch("l0_o2", [8, 128, T], BF16)

    mux, muxk = col_tile(ctx, "l0mux", weights["muxc"])
    xbias, xbiask = col_tile(ctx, "l0xb", weights["xbiasc"])
    bw2t, bw2k = col_tile(ctx, "l0bw2", weights["bw2c"])
    bonust, bonusk = col_tile(ctx, "l0bonus", weights["bonusc"])

    ws_x1 = WStream(ctx, "l0wx1", weights["Wx1T"], 128)
    ws_mus = WStream(ctx, "l0wmus", weights["Wx2T"], 512)
    ws_r = WStream(ctx, "l0wr", weights["WrT"], 512)
    ws_w1 = WStream(ctx, "l0ww1", weights["Ww1T"], 128, share=ws_r)
    ws_w2 = WStream(ctx, "l0ww2", weights["Ww2T"], 512, share=ws_r)
    ws_k = WStream(ctx, "l0wk", weights["WkT"], 512, share=ws_r)
    ws_v = WStream(ctx, "l0wv", weights["WvT"], 512, share=ws_r)
    ws_g = WStream(ctx, "l0wg2", weights["WgT"], 512, share=ws_r)

    pm0 = ctx.pool.mark()
    TB = 1024
    prev = sb.get("l0prev", 128, 8, F32)
    p.v("memset", prev[:, :], 0.0, writes=[prev.name], tag="l0a:pz")
    asm = [sb.get(f"l1asm{u}", 128, D, BF16) for u in range(4)]
    asm_keys = [a.name for a in asm]

    for tb in range(T // TB):
        tb0 = tb * TB
        ctx.pool.release(pm0)
        lnf = ctx.pool.alloc_fm("l0a_ln", D, TB, F32)
        ln_pass(ctx, x_src, None, out_f32=lnf, t0=tb0, nt=TB, tag=f"{tag}:ln")
        delta = ctx.pool.alloc_fm("l0a_dl", D, TB, F32)
        for j in range(8):
            p.v("tensor_sub", delta.chunks[j][:, 1:TB], lnf.chunks[j][:, 0:TB - 1],
                lnf.chunks[j][:, 1:TB], reads=[lnf.key(j)],
                writes=[delta.key(j)], tag="l0a:dl")
            p.v("tensor_sub", delta.chunks[j][:, 0:1], prev[:, j:j + 1],
                lnf.chunks[j][:, 0:1], reads=[prev.name, lnf.key(j)],
                writes=[delta.key(j)], tag="l0a:dl0")
            p.v("tensor_copy", prev[:, j:j + 1], lnf.chunks[j][:, TB - 1:TB],
                reads=[lnf.key(j)], writes=[prev.name], tag="l0a:pcp")
        xp = ctx.pool.alloc_fm("l0a_xp", 256, TB, BF16)
        pmxa = ctx.pool.mark()
        xa = ctx.pool.alloc_fm("l0a_xa", D, TB, BF16)
        for j in range(8):
            p.v("scalar_tensor_tensor", xa.chunks[j][:, :], delta.chunks[j][:, :],
                mux[:, j:j + 1], lnf.chunks[j][:, :], ALU.mult, ALU.add,
                reads=[delta.key(j), muxk, lnf.key(j)], writes=[xa.key(j)],
                tag="l0a:xax", lowp=True)

        def ep_xp(ctx, ps, pskey, ocg, tt0l, tw):
            p.s("activation", xp.chunks[ocg][:, tt0l - tb0:tt0l - tb0 + tw],
                ps[:, :tw], AF.Tanh, reads=[pskey], writes=[xp.key(ocg)],
                tag="l0a:tanh")
        gemm_fm(ctx, xa, ws_x1, ep_xp, t0=0, tn=TB, toff=tb0, tag=f"{tag}:x1")

        ctx.pool.release(pmxa)
        pmn = ctx.pool.mark()
        for n, (wsn, dram, dname, nch, store_tok) in enumerate((
                (ws_r, r_dram, "l0_r", 4, False),
                (ws_w1, None, None, 1, False),
                (ws_k, k_dram, "l0_k", 4, False),
                (ws_v, v_dram, "l0_v", 8, False),
                (ws_g, g_dram, "l0_g", 8, True))):
            ctx.pool.release(pmn)
            mu = ctx.pool.alloc_fm("l0a_mu", D, TB, BF16)

            def ep_mu(ctx, ps, pskey, ocg, tt0l, tw, n=n):
                loc = ocg - n * 8
                p.v("tensor_scalar", mu.chunks[loc][:, tt0l - tb0:tt0l - tb0 + tw],
                    ps[:, :tw], xbias[:, n * 8 + loc:n * 8 + loc + 1], None,
                    ALU.add, reads=[pskey, xbiask], writes=[mu.key(loc)],
                    tag="l0a:mu")
            gemm_fm(ctx, xp, ws_mus, ep_mu, t0=0, tn=TB, toff=tb0,
                    blocks=list(range(n * 2, n * 2 + 2)), tag=f"{tag}:mus")
            xan = ctx.pool.alloc_fm("l0a_xan", D, TB, BF16)
            for j in range(8):
                p.v("tensor_mul", xan.chunks[j][:, :], delta.chunks[j][:, :],
                    mu.chunks[j][:, :], reads=[delta.key(j), mu.key(j)],
                    writes=[xan.key(j)], tag="l0a:xm", lowp=True)
                p.v("tensor_add", xan.chunks[j][:, :], xan.chunks[j][:, :],
                    lnf.chunks[j][:, :], reads=[xan.key(j), lnf.key(j)],
                    writes=[xan.key(j)], tag="l0a:xa2", lowp=True)
            if n == 1:
                w64 = ctx.pool.alloc_fm("l0a_w64", 128, TB, BF16)

                def ep_w64(ctx, ps, pskey, ocg, tt0l, tw):
                    p.s("activation", w64.chunks[0][:, tt0l - tb0:tt0l - tb0 + tw],
                        ps[:, :tw], AF.Tanh, reads=[pskey],
                        writes=[w64.key(0)], tag="l0a:w64")
                gemm_fm(ctx, xan, ws_w1, ep_w64, t0=0, tn=TB, toff=tb0,
                        tag=f"{tag}:w1")
                gemm_fm(ctx, w64, ws_w2,
                        ep_store_dram(ctx, w_dram, "l0_w",
                                      scale_tile=None, func=None,
                                      dst_dtype=F32, bias_tile=bw2t,
                                      bias_key=bw2k),
                        t0=0, tn=TB, toff=tb0, tag=f"{tag}:w2")
            elif store_tok:
                for sub in range(TB // 512):
                    gemm_fm(ctx, xan, wsn, ep_store_tok(ctx, asm, asm_keys,
                                                        identb, identb_k, 8),
                            t0=sub * 512, tn=(sub + 1) * 512, toff=tb0,
                            tag=f"{tag}:g")
                    for u in range(4):
                        i = (tb0 + sub * 512) // 128 + u
                        p.dma("sync", lambda e, a=asm[u], i=i: e.dma_start(
                            out=g_dram[i * 128:(i + 1) * 128, :], in_=a[:, :]),
                            reads=[asm_keys[u]], writes=[f"l0_g:{i}"],
                            group=asm_keys[u], tag=f"{tag}:gst")
            else:
                gemm_fm(ctx, xan, wsn,
                        ep_store_dram(ctx, dram, dname),
                        t0=0, tn=TB, toff=tb0, tag=f"{tag}:p{n}")

    ctx.pool.reset()
    ohead = [ctx.pool.alloc_cols(T) for _ in range(2)]
    pm2 = ctx.pool.mark()
    S_r = [sb.get("l0S", 128, DV, BF16), sb.get("l0S2", 128, DV, BF16)]
    NR = 3
    Am_r = [sb.get(f"l0Am{i}", 32, 32, BF16) for i in range(NR)]
    Amt_r = [sb.get(f"l0Amt{i}", 32, 32, BF16) for i in range(NR)]
    vt_r = [sb.get(f"l0vt{i}", 32, DV, BF16) for i in range(NR)]
    khat_r = [sb.get(f"l0khat{i}", 128, C, BF16) for i in range(NR)]
    dcol_r = [sb.get(f"l0dcol{i}", 32, 2, F32) for i in range(NR)]
    otok_r = [sb.get(f"l0otok{i}", 32, DV, BF16) for i in range(NR)]
    rk_r = [sb.get(f"l0rk{i}", 128, C, BF16) for i in range(NR)]
    kht_r = [sb.get(f"l0khT{i}", 32, 128, BF16) for i in range(NR)]
    emb_blk = sb.get("l0em", 128, 512, F32)
    maskt, maskt_k = ctx.const("l0mask", weights["trimask"])

    for h in range(NH):
        ctx.pool.release(pm2)
        rt, rt_k = ctx.pool.alloc_cols(T)
        kt, kt_k = ctx.pool.alloc_cols(T)
        cumh = [ctx.pool.alloc_cols(T) for _ in range(2)]
        e3h = [ctx.pool.alloc_cols(T) for _ in range(2)]
        rtl_t, rtl_k = ctx.pool.alloc_cols(T)
        ktl_t, ktl_k = ctx.pool.alloc_cols(T)
        vt_fm0, vt_fm0k = ctx.pool.alloc_cols(T)
        vt_fm1, vt_fm1k = ctx.pool.alloc_cols(T)
        TH = T // 2
        NCH2 = NCHK // 2
        p.load(rt[:, :], r_dram[h, :, :], engine="sync", reads=[f"l0_r:{h}"],
               writes=[rt_k], tag="l0a:ldr")
        p.load(kt[:, :], k_dram[h, :, :], engine="sync", reads=[f"l0_k:{h}"],
               writes=[kt_k], tag="l0a:ldk")
        p.load(vt_fm0[:, :], v_dram[2 * h, :, :], engine="sync",
               reads=[f"l0_v:{2 * h}"], writes=[vt_fm0k], tag="l0a:ldv")
        p.load(vt_fm1[:, :], v_dram[2 * h + 1, :, :], engine="sync",
               reads=[f"l0_v:{2 * h + 1}"], writes=[vt_fm1k], tag="l0a:ldv2")
        hand = sb.get("l0hand", 128, 2, F32)
        for hf in range(2):
            cum = cumh[hf][0].bitcast(F32)
            cum_k = cumh[hf][1]
            p.load(cum[:, :], w_dram[h, :, hf * TH:(hf + 1) * TH],
                   engine="sync", reads=[f"l0_w:{h}"], writes=[cum_k],
                   tag="l0a:ldw")
            p.s("activation", cum[:, :], cum[:, :], AF.Exp, reads=[cum_k],
                writes=[cum_k], tag="l0a:wexp")
            p.v("tensor_scalar", cum[:, :], cum[:, :], -1.0, None, ALU.mult,
                reads=[cum_k], writes=[cum_k], tag="l0a:wneg")
            init = 0.0 if hf == 0 else hand[:, 0:1]
            rds = [cum_k, "cs_onescol_f"] + ([] if hf == 0 else [hand.name])
            p.v("tensor_tensor_scan", cum[:, :], _ones_bcast(ctx, TH),
                cum[:, :], init, ALU.mult, ALU.add, reads=rds,
                writes=[cum_k], tag="l0a:cums")
            p.v("tensor_copy", hand[:, hf:hf + 1], cum[:, TH - 1:TH],
                reads=[cum_k], writes=[hand.name], tag="l0a:hand")
            cview = cum.rearrange("p (c j) -> p c j", j=C)
            p.v("tensor_tensor", cview[:, 1:NCH2, :], cview[:, 1:NCH2, :],
                cview[:, 0:NCH2 - 1, C - 1:C].broadcast_to((128, NCH2 - 1, C)),
                ALU.subtract, reads=[cum_k], writes=[cum_k], tag="l0a:seg")
            if hf == 1:
                p.v("tensor_tensor", cview[:, 0:1, :], cview[:, 0:1, :],
                    hand[:, 0:1].unsqueeze(1).broadcast_to((128, 1, C)),
                    ALU.subtract,
                    reads=[cum_k, hand.name], writes=[cum_k], tag="l0a:seg0")
            e3 = e3h[hf][0].bitcast(F32)
            e3_k = e3h[hf][1]
            p.s("activation", e3[:, :], cum[:, :], AF.Exp, reads=[cum_k],
                writes=[e3_k], tag="l0a:e3")
            rview = rt[:, hf * TH:(hf + 1) * TH].rearrange(
                "p (c j) -> p c j", j=C)
            rtlview = rtl_t[:, hf * TH:(hf + 1) * TH].rearrange(
                "p (c j) -> p c j", j=C)
            e3view = e3.rearrange("p (c j) -> p c j", j=C)
            p.v("tensor_tensor", rtlview[:, :, 1:C], rview[:, :, 1:C],
                e3view[:, :, 0:C - 1], ALU.mult, reads=[rt_k, e3_k],
                writes=[rtl_k], tag="l0a:rtl", lowp=True)
            p.v("tensor_copy", rtlview[:, :, 0:1], rview[:, :, 0:1],
                reads=[rt_k], writes=[rtl_k], tag="l0a:rtl0")
            for b in range(TH // 512):
                p.s("activation", emb_blk[:, :], cum[:, b * 512:(b + 1) * 512],
                    AF.Exp, scale=-1.0, reads=[cum_k], writes=[emb_blk.name],
                    tag="l0a:em")
                p.v("tensor_mul", ktl_t[:, hf * TH + b * 512:hf * TH + (b + 1) * 512],
                    kt[:, hf * TH + b * 512:hf * TH + (b + 1) * 512],
                    emb_blk[:, :], reads=[kt_k, emb_blk.name],
                    writes=[ktl_k], tag="l0a:ktl", lowp=True)
        p.v("memset", S_r[0][:, :], 0.0, writes=[S_r[0].name], tag="l0a:Sz")
        p.v("memset", S_r[1][:, :], 0.0, writes=[S_r[1].name], tag="l0a:Sz2")
        for c in range(NCHK):
            c0 = c * C
            Am = Am_r[c % NR]
            Amt = Amt_r[c % NR]
            vt_t = vt_r[c % NR]
            khat = khat_r[c % NR]
            dcol = dcol_r[c % NR]
            otok = otok_r[c % NR]
            rk = rk_r[c % NR]
            kht = kht_r[c % NR]
            Scur, Sprev = S_r[c % 2], S_r[(c + 1) % 2]
            e3c = e3h[c // NCH2][0].bitcast(F32)
            e3c_k = e3h[c // NCH2][1]
            cl0 = (c % NCH2) * C
            p.v("tensor_mul", rk[:, :], rt[:, c0:c0 + C], kt[:, c0:c0 + C],
                reads=[rt_k, kt_k], writes=[rk.name], tag="l0a:rk", lowp=True)
            pD, pDk = ctx.psum_f()
            p.t_matmul(pD[0:C, 0:1], rk[:, :], bonust[:, h:h + 1], True, True,
                       reads=[rk.name, bonusk], writes=[pDk], tag="l0a:Dmm")
            p.s("copy", dcol[:, 0:1], pD[0:C, 0:1], reads=[pDk],
                writes=[dcol.name], tag="l0a:Dcp")
            pA, pAk = ctx.psum_f()
            p.t_matmul(pA[0:C, 0:C], rtl_t[:, c0:c0 + C], ktl_t[:, c0:c0 + C],
                       True, True, reads=[rtl_k, ktl_k], writes=[pAk],
                       tag="l0a:Amm")
            p.v("tensor_mul", Am[:, :], pA[0:C, 0:C], maskt[:, :],
                reads=[pAk, maskt_k], writes=[Am.name], tag="l0a:Amsk",
                lowp=True)
            p.v("transpose", Amt[:, :], Am[:, :], reads=[Am.name],
                writes=[Amt.name], tag="l0a:AT")
            for half in range(2):
                vv = vt_fm0 if half == 0 else vt_fm1
                vvk = vt_fm0k if half == 0 else vt_fm1k
                pb, pbk = ctx.psum_b()
                p.t_transpose(pb[0:C, 0:128], vv[:, c0:c0 + C], identb[:, :],
                              reads=[vvk, identb_k], writes=[pbk],
                              tag="l0a:vT")
                p.s("copy", vt_t[:, half * 128:(half + 1) * 128],
                    pb[0:C, 0:128], reads=[pbk], writes=[vt_t.name],
                    tag="l0a:vTc")
            p.v("tensor_tensor", khat[:, :], ktl_t[:, c0:c0 + C],
                e3c[:, cl0 + C - 1:cl0 + C].broadcast_to((128, C)), ALU.mult,
                reads=[ktl_k, e3c_k], writes=[khat.name], tag="l0a:khat",
                lowp=True)
            pb2, pb2k = ctx.psum_b()
            p.t_transpose(pb2[0:C, 0:128], khat[:, :], identb[:, :],
                          reads=[khat.name, identb_k], writes=[pb2k],
                          tag="l0a:khT")
            p.s("copy", kht[:, :], pb2[0:C, 0:128], reads=[pb2k],
                writes=[kht.name], tag="l0a:khTc")
            pU, pUk = ctx.psum_f()
            p.t_matmul(pU[:, 0:DV], kht[:, :], vt_t[:, :], True, True,
                       reads=[kht.name, vt_t.name], writes=[pUk],
                       tag="l0a:Umm")
            pO, pOk = ctx.psum_f()
            p.t_matmul(pO[0:C, 0:DV], Amt[:, :], vt_t[:, :], True, False,
                       reads=[Amt.name, vt_t.name], writes=[pOk],
                       tag="l0a:intra")
            p.t_matmul(pO[0:C, 0:DV], rtl_t[:, c0:c0 + C], Sprev[:, :], False,
                       True, reads=[rtl_k, Sprev.name], writes=[pOk],
                       tag="l0a:inter")
            p.v("scalar_tensor_tensor", otok[:, :], vt_t[:, :], dcol[:, 0:1],
                pO[0:C, 0:DV], ALU.mult, ALU.add,
                reads=[vt_t.name, dcol.name, pOk], writes=[otok.name],
                tag="l0a:oeps", lowp=True)
            p.v("scalar_tensor_tensor", Scur[:, :], Sprev[:, :],
                e3c[:, cl0 + C - 1:cl0 + C], pU[:, 0:DV], ALU.mult, ALU.add,
                reads=[Sprev.name, e3c_k, pUk], writes=[Scur.name],
                tag="l0a:Supd", lowp=True)
            for half in range(2):
                pb3, pb3k = ctx.psum_b()
                p.t_transpose(pb3[0:128, 0:C],
                              otok[:, half * 128:(half + 1) * 128],
                              identb[0:C, 0:C], reads=[otok.name, identb_k],
                              writes=[pb3k], tag="l0a:oT")
                p.s("copy",
                    ohead[half][0][:, c0:c0 + C], pb3[0:128, 0:C],
                    reads=[pb3k], writes=[ohead[half][1]],
                    tag="l0a:oTc")
        for half in range(2):
            p.dma("sync", lambda e, h=h, half=half, ap=ohead[half][0]:
                  e.dma_start(out=o_dram[2 * h + half, :, :], in_=ap[:, :]),
                  reads=[ohead[half][1]], writes=[f"l0_oA:{2 * h + half}"],
                  group=ohead[half][1], tag="l0a:ost")

    lnw_bc = sb.get("l3gbc", 128, D, F32)
    lnb_bc = sb.get("l3bbc", 128, D, F32)
    p.load(asm[0][0:1, :], ctx.din("w_l0lnw", weights["lnw_row"]),
           engine="gpsimd", writes=[asm_keys[0]], tag="l0a:lnw")
    p.load(asm[1][0:1, :], ctx.din("w_l0lnb", weights["lnb_row"]),
           engine="gpsimd", writes=[asm_keys[1]], tag="l0a:lnb")
    for hb in range(2):
        for dst, rowt, rowk in ((lnw_bc, asm[0], asm_keys[0]),
                                (lnb_bc, asm[1], asm_keys[1])):
            pf, pfk = ctx.psum_f()
            p.t_matmul(pf[:, 0:512], onesrow[0:1, :],
                       rowt[0:1, hb * 512:(hb + 1) * 512], True,
                       True, reads=[onesrow_k, rowk], writes=[pfk],
                       tag="l0a:lbc")
            p.v("tensor_copy", dst[:, hb * 512:(hb + 1) * 512], pf[:, 0:512],
                reads=[pfk], writes=[dst.name], tag="l0a:lbcc")

    ock = [ctx.pool.alloc_cols(1024) for _ in range(2)]
    for i in range(T // 128):
        r = i % 2
        ot = sb.get(f"resx{r}", 128, D, F32)
        gt = sb.get(f"lnx{r}", 128, D, F32)
        st = sb.get(f"lnst{r}", 128, 8, F32)
        sg = sb.get(f"l0sg{r}", 128, 16, F32)
        oca, oca_k = ock[i % 2]
        srco = o_dram[0:8, :, i * 128:(i + 1) * 128].rearrange("j p t -> p j t")
        p.dma("sync", lambda e, oca=oca, srco=srco: e.dma_start(
            out=oca.rearrange("p (j t) -> p j t", t=128), in_=srco),
            reads=[f"l0_oA:{j}" for j in range(8)], writes=[oca_k],
            group=oca_k, tag="l0a:old")
        for j in range(8):
            pb, pbk = ctx.psum_b()
            p.t_transpose(pb[:, 0:128], oca[:, j * 128:(j + 1) * 128],
                          identb[:, :], reads=[oca_k, identb_k],
                          writes=[pbk], tag="l0a:gT")
            p.v("tensor_copy", ot[:, j * 128:(j + 1) * 128], pb[:, 0:128],
                reads=[pbk], writes=[ot.name], tag="l0a:gTc")
        otv = ot.rearrange("p (h v) -> p h v", v=DV)
        p.v("tensor_reduce", sg[:, 0:4].rearrange("p (h o) -> p h o", o=1),
            otv, AX.X, ALU.add, reads=[ot.name], writes=[sg.name],
            tag="l0a:gm")
        p.v("tensor_scalar", sg[:, 0:4], sg[:, 0:4], 1.0 / DV, None, ALU.mult,
            reads=[sg.name], writes=[sg.name], tag="l0a:gms")
        p.v("tensor_tensor", otv, otv,
            sg[:, 0:4].rearrange("p (h o) -> p h o", o=1).broadcast_to((128, 4, DV)),
            ALU.subtract, reads=[ot.name, sg.name], writes=[ot.name],
            tag="l0a:gsub")
        p.s("activation", gt[:, :], ot[:, :], AF.Square, reads=[ot.name],
            writes=[gt.name], tag="l0a:gsq")
        p.v("tensor_reduce", sg[:, 4:8].rearrange("p (h o) -> p h o", o=1),
            gt.rearrange("p (h v) -> p h v", v=DV), AX.X, ALU.add,
            reads=[gt.name], writes=[sg.name], tag="l0a:gv")
        p.v("tensor_scalar", sg[:, 4:8], sg[:, 4:8], 1.0 / DV, 1e-5,
            ALU.mult, ALU.add, reads=[sg.name], writes=[sg.name],
            tag="l0a:gvs")
        p.s("activation", sg[:, 8:12], sg[:, 4:8], AF.Sqrt, reads=[sg.name],
            writes=[sg.name], tag="l0a:gsd")
        p.v("reciprocal", sg[:, 12:16], sg[:, 8:12], reads=[sg.name],
            writes=[sg.name], tag="l0a:grs")
        p.v("tensor_tensor", otv, otv,
            sg[:, 12:16].rearrange("p (h o) -> p h o", o=1).broadcast_to((128, 4, DV)),
            ALU.mult, reads=[ot.name, sg.name], writes=[ot.name],
            tag="l0a:gnm")
        p.v("tensor_mul", ot[:, :], ot[:, :], lnw_bc[:, :],
            reads=[ot.name, lnw_bc.name], writes=[ot.name], tag="l0a:glw")
        p.v("tensor_add", ot[:, :], ot[:, :], lnb_bc[:, :],
            reads=[ot.name, lnb_bc.name], writes=[ot.name], tag="l0a:glb")
        p.load(gt[:, :].bitcast(BF16)[:, 0:D], g_dram[i * 128:(i + 1) * 128, :],
               engine="sync", reads=[f"l0_g:{i}"], writes=[gt.name],
               tag="l0a:gld")
        p.s("activation", gt[:, :].bitcast(BF16)[:, D:2 * D],
            gt[:, :].bitcast(BF16)[:, 0:D], AF.Silu, reads=[gt.name],
            writes=[gt.name], tag="l0a:gsilu")
        p.v("tensor_mul", ot[:, :], ot[:, :], gt[:, :].bitcast(BF16)[:, D:2 * D],
            reads=[ot.name, gt.name], writes=[ot.name], tag="l0a:gmul")
        lt = sb.get(f"lnt{r}", 128, D, BF16)
        for j in range(8):
            p.v("tensor_copy", lt[:, j * 128:(j + 1) * 128],
                ot[:, j * 128:(j + 1) * 128], reads=[ot.name],
                writes=[lt.name], tag="l0a:ocast")
            pb, pbk = ctx.psum_b()
            p.t_transpose(pb[:, 0:128], lt[:, j * 128:(j + 1) * 128],
                          identb[:, :], reads=[lt.name, identb_k],
                          writes=[pbk], tag="l0a:obT")
            p.v("tensor_copy", oca[:, j * 128:(j + 1) * 128],
                pb[:, 0:128], reads=[pbk], writes=[oca_k],
                tag="l0a:obTc")
        dsto = o2_dram[0:8, :, i * 128:(i + 1) * 128].rearrange("j p t -> p j t")
        p.dma("sync", lambda e, oca=oca, dsto=dsto: e.dma_start(
            out=dsto, in_=oca.rearrange("p (j t) -> p j t", t=128)),
            reads=[oca_k], writes=[f"l0_oB:{i}"], group=oca_k,
            tag="l0a:o2st")

    ctx.pool.release(pm2)
    ws_o = WStream(ctx, "l0wo", weights["WoT"], 512)
    y = ctx.pool.alloc_fm("ytt", D, 512, BF16)
    ob2 = [ctx.pool.alloc_fm(f"l0a_ob{i}", D, 512, BF16) for i in range(2)]

    def ep_o(ctx, ps, pskey, ocg, tt0, tw):
        p.s("copy", y.chunks[ocg][:, 0:tw], ps[:, :tw], reads=[pskey],
            writes=[y.key(ocg)], tag="l0a:wo")

    for tt in range(T // 512):
        tt0 = tt * 512
        ob = ob2[tt % 2]
        for j in range(8):
            p.load(ob.chunks[j][:, 0:512], o2_dram[j, :, tt0:tt0 + 512],
                   engine="sync",
                   reads=[f"l0_oB:{ii}" for ii in range(tt * 4, tt * 4 + 4)],
                   writes=[ob.key(j)], tag="l0a:obl")
        gemm_fm(ctx, ob, ws_o, ep_o, t0=0, tn=512, toff=tt0, tag=f"{tag}:wo")
        residual_add(ctx, x_src, x_dst, y, tt0, 512, scale=1.0,
                     tag=f"{tag}:res")


def _ones_bcast(ctx, n, parts=128):
    t, k = ctx.const("onescol_f", np.ones((128, 1), np.float32))
    return t[0:parts, 0:1].broadcast_to((parts, n))


def prep_l0_attn(inp):
    H = 1024
    Wx1 = inp["L0_attn_Wx1"]
    Wx1p = np.zeros((256, H), np.float32)
    Wx1p[:160] = Wx1
    Wx2 = inp["L0_attn_Wx2"].reshape(H, 5, 32)
    W2 = np.zeros((256, 5 * H), np.float32)
    for n in range(5):
        W2[n * 32:(n + 1) * 32, n * H:(n + 1) * H] = Wx2[:, n, :].T
    Ww1 = inp["L0_attn_Ww1"]
    Ww1p = np.zeros((128, H), np.float32)
    Ww1p[:64] = Ww1
    Ww2 = inp["L0_attn_Ww2"]
    Ww2p = np.zeros((128, 512), np.float32)
    Ww2p[:64] = Ww2.T
    trimask = np.tril(np.ones((32, 32)), -1).astype(BF16NP)
    xbias = inp["L0_attn_x_bias"]
    xb = np.zeros((128, 5 * 8), np.float32)
    for n in range(5):
        xb[:, n * 8:(n + 1) * 8] = xbias[n].reshape(8, 128).T
    return {
        "muxc": np.ascontiguousarray(inp["L0_attn_mu_x"].reshape(-1, 128).T),
        "xbiasc": np.ascontiguousarray(xb),
        "bw2c": np.ascontiguousarray(inp["L0_attn_bw2"].reshape(-1, 128).T),
        "bonusc": np.ascontiguousarray(inp["L0_attn_bonus"].T.astype(BF16NP)),
        "Wx1T": np.ascontiguousarray(Wx1p.T),
        "Wx2T": np.ascontiguousarray(W2),
        "WrT": np.ascontiguousarray(inp["L0_attn_Wr"].T),
        "Ww1T": np.ascontiguousarray(Ww1p.T),
        "Ww2T": np.ascontiguousarray(Ww2p),
        "WkT": np.ascontiguousarray(inp["L0_attn_Wk"].T),
        "WvT": np.ascontiguousarray(inp["L0_attn_Wv"].T),
        "WgT": np.ascontiguousarray(inp["L0_attn_Wg"].T),
        "WoT": np.ascontiguousarray(inp["L0_attn_Wo"].T),
        "trimask": trimask,
        "lnw_row": np.tile(inp["L0_attn_ln_w"], 4)[None, :].astype(BF16NP),
        "lnb_row": np.tile(inp["L0_attn_ln_b"], 4)[None, :].astype(BF16NP),
    }



def emit_l2_attn(ctx, x_src, x_dst, weights, tag):
    p, sb = ctx.p, ctx.sb
    NH, HD, DS, Q = 32, 64, 128, 128
    NC = T // Q
    DIN = 2048
    ctx.new_phase()
    identb, identb_k = ctx.ident_bf()
    identf, identf_k = ctx.ident_f32()
    onesrow, onesrow_k = ctx.ones_row_bf()

    z_dram = ctx.dscratch("l2a_z", [16, 128, T], BF16)
    xbc_dram = ctx.dscratch("l2a_xbc", [18, 128, T], BF16)
    xc_dram = ctx.dscratch("l2a_xc", [18, 128, T], BF16)
    dt_dram = ctx.dscratch("l2a_dt", [128, T], F32)
    y_dram = ctx.dscratch("l2a_y", [16, 128, T], BF16)
    ends_dram = ctx.dscratch("l2a_ends", [32, 32], F32)

    ln_bf = ctx.pool.alloc_fm("l2a_ln", D, T, BF16)
    ln_pass(ctx, x_src, ln_bf, tag=f"{tag}:ln")
    ws_ip = WStream(ctx, "l2aip", weights["ipT"], 512)

    def ep_ip(ctx, ps, pskey, ocg, tt0, tw):
        r = (ocg + tt0 // 512) % 2
        if ocg < 16:
            st = sb.get(f"spill{r}", 128, 512, BF16)
            p.s("copy", st[:, :tw], ps[:, :tw], reads=[pskey],
                writes=[st.name], tag="l2a:zc")
            p.dma("sync", lambda e, st=st, ocg=ocg, tt0=tt0, tw=tw:
                  e.dma_start(out=z_dram[ocg, :, tt0:tt0 + tw],
                              in_=st[:, :tw]),
                  reads=[st.name], writes=[f"l2a_z:{ocg}"], group=st.name,
                  tag="l2a:zd")
        elif ocg < 34:
            st = sb.get(f"spill{r}", 128, 512, BF16)
            p.s("copy", st[:, :tw], ps[:, :tw], reads=[pskey],
                writes=[st.name], tag="l2a:xc")
            p.dma("sync", lambda e, st=st, ocg=ocg, tt0=tt0, tw=tw:
                  e.dma_start(out=xbc_dram[ocg - 16, :, tt0:tt0 + tw],
                              in_=st[:, :tw]),
                  reads=[st.name], writes=[f"l2a_xbc:{ocg - 16}"],
                  group=st.name, tag="l2a:xd")
        elif ocg == 34:
            st = sb.get("spillf0", 128, 512, F32)
            p.v("tensor_copy", st[:, :tw], ps[:, :tw], reads=[pskey],
                writes=[st.name], tag="l2a:dc")
            p.dma("sync", lambda e, st=st, tt0=tt0, tw=tw:
                  e.dma_start(out=dt_dram[:, tt0:tt0 + tw], in_=st[:, :tw]),
                  reads=[st.name], writes=["l2a_dt:0"], group=st.name,
                  tag="l2a:dd")

    gemm_fm(ctx, ln_bf, ws_ip, ep_ip, tag=f"{tag}:ip")

    ctx.pool.reset()
    cw = [col_tile(ctx, f"l2acw{k}", weights[f"cwc{k}"]) for k in range(4)]
    cbt, cbk = col_tile(ctx, "l2acb", weights["cbc"])
    for j in range(18):
        xt, xt_k = ctx.pool.alloc_cols(T) if j == 0 else (xt, xt_k)
        p.load(xt[:, :], xbc_dram[j, :, :], engine="sync",
               reads=[f"l2a_xbc:{j}"], writes=[xt_k], tag="l2a:cl")
        ct, ct_k = ctx.pool.alloc_cols(T) if j == 0 else (ct, ct_k)
        p.v("tensor_scalar", ct[:, :], xt[:, :], cw[3][0][:, j:j + 1],
            cbt[:, j:j + 1], ALU.mult, ALU.add,
            reads=[xt_k, cw[3][1], cbk], writes=[ct_k], tag="l2a:c3")
        for k, sh in ((2, 1), (1, 2), (0, 3)):
            p.v("scalar_tensor_tensor", ct[:, sh:T], xt[:, 0:T - sh],
                cw[k][0][:, j:j + 1], ct[:, sh:T], ALU.mult, ALU.add,
                reads=[xt_k, cw[k][1], ct_k], writes=[ct_k],
                tag=f"l2a:c{k}", lowp=True)
        p.s("activation", ct[:, :], ct[:, :], AF.Silu, reads=[ct_k],
            writes=[ct_k], tag="l2a:csilu")
        p.dma("sync", lambda e, ct=ct, j=j: e.dma_start(
            out=xc_dram[j, :, :], in_=ct[:, :]),
            reads=[ct_k], writes=[f"l2a_xc:{j}"], group="l2a_xcst",
            tag="l2a:cst")

    ctx.pool.reset()
    dtb, dtbk = col_tile(ctx, "l2adtb", weights["dtbc"])
    alog, alogk = col_tile(ctx, "l2aal", weights["alogc"])
    acol = sb.get("l2aacol", 32, 2, F32)
    p.s("activation", acol[:, 0:1], alog[0:32, 0:1], AF.Exp, reads=[alogk],
        writes=[acol.name], tag="l2a:aexp")
    p.v("tensor_scalar", acol[:, 0:1], acol[:, 0:1], -1.0, None, ALU.mult,
        reads=[acol.name], writes=[acol.name], tag="l2a:aneg")
    cumh = [ctx.pool.alloc_cols(T) for _ in range(2)]
    ldth = [ctx.pool.alloc_cols(T) for _ in range(2)]
    Rh = [ctx.pool.alloc_cols(T) for _ in range(2)]
    TH = T // 2
    hand = sb.get("l0hand", 128, 2, F32)
    ends = sb.get("l2aends", 32, 32, F32)
    for hf in range(2):
        cum = cumh[hf][0].bitcast(F32)[0:32, :]
        cum_k = cumh[hf][1]
        ldt = ldth[hf][0].bitcast(F32)[0:32, :]
        ldt_k = ldth[hf][1]
        R = Rh[hf][0].bitcast(F32)[0:32, :]
        R_k = Rh[hf][1]
        p.load(cum[:, :], dt_dram[0:32, hf * TH:(hf + 1) * TH], engine="sync",
               reads=["l2a_dt:0"], writes=[cum_k], tag="l2a:dtl")
        p.v("tensor_scalar", cum[:, :], cum[:, :], dtb[0:32, 0:1], None,
            ALU.add, reads=[cum_k, dtbk], writes=[cum_k], tag="l2a:dtb")
        p.s("activation", R[:, :], cum[:, :], AF.Abs,
            reads=[cum_k], writes=[R_k], tag="l2a:dabs")
        p.s("activation", R[:, :], R[:, :], AF.Exp, scale=-1.0, reads=[R_k],
            writes=[R_k], tag="l2a:dexp")
        p.s("activation", R[:, :], R[:, :], AF.Ln, bias=1.0, reads=[R_k],
            writes=[R_k], tag="l2a:dln")
        p.v("tensor_scalar", cum[:, :], cum[:, :], 0.0, None, ALU.max,
            reads=[cum_k], writes=[cum_k], tag="l2a:dmax")
        p.v("tensor_add", cum[:, :], cum[:, :], R[:, :], reads=[cum_k, R_k],
            writes=[cum_k], tag="l2a:dtf")
        p.s("activation", ldt[:, :], cum[:, :], AF.Ln, reads=[cum_k],
            writes=[ldt_k], tag="l2a:ldt")
        p.v("tensor_scalar", cum[:, :], cum[:, :], acol[:, 0:1], None,
            ALU.mult, reads=[cum_k, acol.name], writes=[cum_k], tag="l2a:la")
        init = 0.0 if hf == 0 else hand[0:32, 0:1]
        rds = [cum_k, "cs_onescol_f"] + ([] if hf == 0 else [hand.name])
        p.v("tensor_tensor_scan", cum[:, :], _ones_bcast(ctx, TH, 32),
            cum[:, :], init, ALU.mult, ALU.add, reads=rds, writes=[cum_k],
            tag="l2a:cums")
        p.v("tensor_copy", hand[0:32, hf:hf + 1], cum[:, TH - 1:TH],
            reads=[cum_k], writes=[hand.name], tag="l2a:hand")
        NC2 = NC // 2
        cview = cum.rearrange("p (c j) -> p c j", j=Q)
        p.v("tensor_tensor", cview[:, 1:NC2, :], cview[:, 1:NC2, :],
            cview[:, 0:NC2 - 1, Q - 1:Q].broadcast_to((32, NC2 - 1, Q)),
            ALU.subtract, reads=[cum_k], writes=[cum_k], tag="l2a:seg")
        if hf == 1:
            p.v("tensor_tensor", cview[:, 0:1, :], cview[:, 0:1, :],
                hand[0:32, 0:1].unsqueeze(1).broadcast_to((32, 1, Q)),
                ALU.subtract, reads=[cum_k, hand.name], writes=[cum_k],
                tag="l2a:seg0")
        p.v("tensor_tensor", R.rearrange("p (c j) -> p c j", j=Q),
            cview[:, :, Q - 1:Q].broadcast_to((32, NC2, Q)),
            cum.rearrange("p (c j) -> p c j", j=Q),
            ALU.subtract, reads=[cum_k], writes=[R_k], tag="l2a:R")
        p.v("tensor_add", R[:, :], R[:, :], ldt[:, :], reads=[R_k, ldt_k],
            writes=[R_k], tag="l2a:Rldt")
        p.v("tensor_copy", ends[:, hf * NC2:(hf + 1) * NC2],
            cum[:, Q - 1::Q] if False else
            cum.rearrange("p (c j) -> p c j", j=Q)[:, :, Q - 1:Q]
            .rearrange("p c j -> p (c j)"),
            reads=[cum_k], writes=[ends.name], tag="l2a:ends")
    p.dma("sync", lambda e: e.dma_start(
        out=ends_dram.rearrange("c h -> h c"), in_=ends[:, :],
        allow_slow_non_contiguous=True),
        reads=[ends.name], writes=["l2a_ends"], group="l2a_endst",
        tag="l2a:endst")
    erowb = sb.get("l2aerowb", 1, 1024, BF16)
    for hb in range(2):
        spf = sb.get("spillf0", 128, 512, F32)
        p.load(spf[0:1, 0:512],
               ends_dram.rearrange("c h -> (c h)")[None, hb * 512:(hb + 1) * 512],
               engine="sync", reads=["l2a_ends"], writes=[spf.name],
               tag="l2a:endl")
        p.s("activation", erowb[0:1, hb * 512:(hb + 1) * 512], spf[0:1, 0:512],
            AF.Exp, reads=[spf.name], writes=[erowb.name], tag="l2a:ee")
    EQbc, EQbc_k = ctx.pool.alloc_cols(1024)
    for hb in range(2):
        pf, pfk = ctx.psum_f()
        p.t_matmul(pf[:, 0:512], onesrow[0:1, :],
                   erowb[0:1, hb * 512:(hb + 1) * 512], True, True,
                   reads=[onesrow_k, erowb.name], writes=[pfk], tag="l2a:eb")
        p.v("tensor_copy", EQbc[:, hb * 512:(hb + 1) * 512], pf[:, 0:512],
            reads=[pfk], writes=[EQbc_k], tag="l2a:ebc", lowp=True)

    Dbc = sb.get("l2aDbc", 128, 32, F32)
    spb = sb.get("spill0", 128, 512, BF16)
    p.load(spb[0:1, 0:32], ctx.din("w_l2aD", weights["D_row"]),
           engine="gpsimd", writes=[spb.name], tag="l2a:Dl")
    pf, pfk = ctx.psum_f()
    p.t_matmul(pf[:, 0:32], onesrow[0:1, :], spb[0:1, 0:32], True, True,
               reads=[onesrow_k, spb.name], writes=[pfk], tag="l2a:Db")
    p.v("tensor_copy", Dbc[:, :], pf[:, 0:32], reads=[pfk], writes=[Dbc.name],
        tag="l2a:Dbc")
    nwbc, nwbc_k = ctx.pool.alloc_cols(DIN)
    nw_dram = ctx.din("w_l2anw", weights["nw_row"])
    for hb in range(4):
        p.load(spb[0:1, 0:512], nw_dram[0:1, hb * 512:(hb + 1) * 512],
               engine="gpsimd", writes=[spb.name], tag="l2a:nwl")
        pf, pfk = ctx.psum_f()
        p.t_matmul(pf[:, 0:512], onesrow[0:1, :], spb[0:1, 0:512], True, True,
                   reads=[onesrow_k, spb.name], writes=[pfk], tag="l2a:nb")
        p.v("tensor_copy", nwbc[:, hb * 512:(hb + 1) * 512], pf[:, 0:512],
            reads=[pfk], writes=[nwbc_k], tag="l2a:nbc")

    Bfm, Bfm_k = ctx.pool.alloc_cols(T)
    Cfm, Cfm_k = ctx.pool.alloc_cols(T)
    p.load(Bfm[:, :], xc_dram[16, :, :], engine="sync", reads=["l2a_xc:16"],
           writes=[Bfm_k], tag="l2a:Bl")
    p.load(Cfm[:, :], xc_dram[17, :, :], engine="sync", reads=["l2a_xc:17"],
           writes=[Cfm_k], tag="l2a:Cl")
    xck = [ctx.pool.alloc_cols(DIN) for _ in range(2)]
    ybuf, ybuf_k = ctx.pool.alloc_cols(DIN)
    Sst, Sst_k = ctx.pool.alloc_cols(DIN)
    p.v("memset", Sst[:, :], 0.0,
        writes=[Sst_k] + [f"l2aS:{h}" for h in range(NH)], tag="l2a:S0")
    mask_l, mask_lk = ctx.const("l2amask",
                                np.triu(np.ones((128, 128))).astype(BF16NP))
    xT, xT_k = ctx.pool.alloc_cols(DIN)
    zT, zT_k = xT, xT_k
    ctok = sb.get("l2actok", 128, 96, F32)
    GT = sb.get("l2aGT", 128, 128, BF16)
    BT = sb.get("l2aBT", 128, 128, BF16)
    NR2 = 3
    Mt_r = [sb.get(f"l2aMt{i}", 128, 128, BF16) for i in range(NR2)]
    Ct_r = [sb.get(f"l2aCt{i}", 128, 128, BF16) for i in range(NR2)]
    Bh_r = [sb.get(f"l2aBh{i}", 128, 128, BF16) for i in range(NR2)]
    ec_r = [sb.get(f"l2aec{i}", 128, 4, F32) for i in range(NR2)]
    onesrow_f, onesrow_f_k = ctx.const("l2aonesrf",
                                       np.ones((1, 128), np.float32))
    sp0f = sb.get("spill0", 128, 512, BF16).bitcast(F32)
    sp1f = sb.get("spill1", 128, 512, BF16).bitcast(F32)
    crow_slots = [(sp0f[0:1, 0:128], "l2acr0", "spill0"),
                  (sp0f[0:1, 128:256], "l2acr1", "spill0"),
                  (sp1f[0:1, 0:128], "l2acr2", "spill1"),
                  (sp1f[0:1, 128:256], "l2acr3", "spill1")]
    Skeys = [f"l2aS:{h}" for h in range(NH)]
    ctx._psf_n = 6
    ytok_t, ytok_k = ctx.pool.alloc_cols(2 * DIN)
    ytok = ytok_t.bitcast(F32)

    for c in range(NC):
        c0 = c * Q
        hf = c // (NC // 2)
        cl0 = (c % (NC // 2)) * Q
        cum = cumh[hf][0].bitcast(F32)[0:32, :]
        cum_k = cumh[hf][1]
        Rl = Rh[hf][0].bitcast(F32)[0:32, :]
        Rl_k = Rh[hf][1]
        ldt = ldth[hf][0].bitcast(F32)[0:32, :]
        ldt_k = ldth[hf][1]
        pf, pfk = ctx.psum_f()
        p.t_transpose(pf[0:128, 0:32], cum[:, cl0:cl0 + Q], identf[0:32, 0:32],
                      reads=[cum_k, identf_k], writes=[pfk], tag="l2a:cT")
        p.v("tensor_copy", ctok[:, 0:32], pf[0:128, 0:32], reads=[pfk],
            writes=[ctok.name], tag="l2a:cTc")
        pf2, pfk2 = ctx.psum_f()
        p.t_transpose(pf2[0:128, 0:32], ldt[:, cl0:cl0 + Q],
                      identf[0:32, 0:32], reads=[ldt_k, identf_k],
                      writes=[pfk2], tag="l2a:lT")
        p.v("tensor_sub", ctok[:, 32:64], ctok[:, 0:32], pf2[0:128, 0:32],
            reads=[ctok.name, pfk2], writes=[ctok.name], tag="l2a:clT")
        pf3, pfk3 = ctx.psum_f()
        p.t_transpose(pf3[0:128, 0:32], Rl[:, cl0:cl0 + Q],
                      identf[0:32, 0:32], reads=[Rl_k, identf_k],
                      writes=[pfk3], tag="l2a:rT")
        p.v("tensor_copy", ctok[:, 64:96], pf3[0:128, 0:32], reads=[pfk3],
            writes=[ctok.name], tag="l2a:rTc")
        pG, pGk = ctx.psum_f()
        p.t_matmul(pG[:, 0:128], Bfm[:, c0:c0 + Q], Cfm[:, c0:c0 + Q], True,
                   True, reads=[Bfm_k, Cfm_k], writes=[pGk], tag="l2a:G")
        p.v("tensor_mul", GT[:, :], pG[:, 0:128], mask_l[:, :],
            reads=[pGk, mask_lk], writes=[GT.name], tag="l2a:Gc", lowp=True)
        pB, pBk = ctx.psum_b()
        p.t_transpose(pB[:, 0:128], Bfm[:, c0:c0 + Q], identb[:, :],
                      reads=[Bfm_k, identb_k], writes=[pBk], tag="l2a:BT")
        p.v("tensor_copy", BT[:, :], pB[:, 0:128], reads=[pBk],
            writes=[BT.name], tag="l2a:BTc")
        xca, xck_key = xck[c % 2]
        src = xc_dram[0:16, :, c0:c0 + Q].rearrange("j p t -> p j t")
        p.dma("sync", lambda e, xca=xca, src=src: e.dma_start(
            out=xca.rearrange("p (j t) -> p j t", t=Q), in_=src),
            reads=[f"l2a_xc:{j}" for j in range(16)], writes=[xck_key],
            group=xck_key, tag="l2a:xl")
        for j in range(16):
            pb, pbk = ctx.psum_b()
            p.t_transpose(pb[:, 0:128], xca[:, j * 128:(j + 1) * 128],
                          identb[:, :], reads=[xck_key, identb_k],
                          writes=[pbk], tag="l2a:xT")
            p.s("copy", xT[:, j * 128:(j + 1) * 128], pb[:, 0:128],
                reads=[pbk], writes=[xT_k], tag="l2a:xTc")
        for h in range(NH):
            Mt = Mt_r[h % NR2]
            Ct_s = Ct_r[h % NR2]
            Bh_s = Bh_r[h % NR2]
            ecol = ec_r[h % NR2]
            crow, crk, crtens = crow_slots[h % 4]
            wl = [crk, crtens] if (c == 0 and h < 4) else [crk]
            p.dma("sync", lambda e, crow=crow, h=h, cl0=cl0, cum=cum:
                  e.dma_start(out=crow, in_=cum[h:h + 1, cl0:cl0 + Q]),
                  reads=[cum_k], writes=wl, group=crk, tag="l2a:crd")
            pb2, pb2k = ctx.psum_f()
            p.t_matmul(pb2[:, 0:128], onesrow_f[0:1, :], crow, True, True,
                       reads=[onesrow_f_k, crk],
                       writes=[pb2k], tag="l2a:hb")
            p.v("tensor_scalar", Mt[:, :], pb2[:, 0:128],
                ctok[:, 32 + h:32 + h + 1], 0.0, ALU.subtract, ALU.min,
                reads=[pb2k, ctok.name], writes=[Mt.name], tag="l2a:Me")
            p.s("activation", Mt[:, :], Mt[:, :], AF.Exp, reads=[Mt.name],
                writes=[Mt.name], tag="l2a:Mx")
            p.v("tensor_mul", Mt[:, :], Mt[:, :], GT[:, :],
                reads=[Mt.name, GT.name], writes=[Mt.name], tag="l2a:MG",
                lowp=True)
            p.s("activation", Ct_s[:, :], pb2[:, 0:128], AF.Exp,
                reads=[pb2k], writes=[Ct_s.name], tag="l2a:Ce")
            p.v("tensor_mul", Ct_s[:, :], Ct_s[:, :], Cfm[:, c0:c0 + Q],
                reads=[Ct_s.name, Cfm_k], writes=[Ct_s.name], tag="l2a:Cm",
                lowp=True)
            p.s("activation", ecol[:, 0:1], ctok[:, 64 + h:64 + h + 1],
                AF.Exp, reads=[ctok.name], writes=[ecol.name], tag="l2a:be")
            p.v("tensor_scalar", Bh_s[:, :], BT[:, :], ecol[:, 0:1], None,
                ALU.mult, reads=[BT.name, ecol.name], writes=[Bh_s.name],
                tag="l2a:bm", lowp=True)
            pY, pYk = ctx.psum_f()
            p.t_matmul(pY[:, 0:HD], Mt[:, :], xT[:, h * 64:(h + 1) * 64],
                       True, False, reads=[Mt.name, xT_k], writes=[pYk],
                       tag="l2a:Yi")
            p.t_matmul(pY[:, 0:HD], Ct_s[:, :], Sst[:, h * 64:(h + 1) * 64],
                       False, True, reads=[Ct_s.name, Skeys[h]],
                       writes=[pYk], tag="l2a:Ys")
            p.v("scalar_tensor_tensor", ytok[:, h * 64:(h + 1) * 64],
                xT[:, h * 64:(h + 1) * 64], Dbc[:, h:h + 1], pY[:, 0:HD],
                ALU.mult, ALU.add, reads=[xT_k, Dbc.name, pYk],
                writes=[ytok_k], tag="l2a:Yd")
            pU, pUk = ctx.psum_f()
            p.t_matmul(pU[:, 0:HD], Bh_s[:, :], xT[:, h * 64:(h + 1) * 64],
                       True, True, reads=[Bh_s.name, xT_k], writes=[pUk],
                       tag="l2a:U")
            p.v("scalar_tensor_tensor", Sst[:, h * 64:(h + 1) * 64],
                Sst[:, h * 64:(h + 1) * 64], EQbc[:, c * 32 + h:c * 32 + h + 1],
                pU[:, 0:HD], ALU.mult, ALU.add,
                reads=[Skeys[h], EQbc_k, pUk], writes=[Skeys[h]],
                tag="l2a:Su", lowp=True)
        for j in range(16):
            zc = sb.get("l2azc", 128, 128, BF16)
            p.load(zc[:, :], z_dram[j, :, c0:c0 + Q], engine="sync",
                   reads=[f"l2a_z:{j}"], writes=[zc.name], tag="l2a:zl")
            pb, pbk = ctx.psum_b()
            p.t_transpose(pb[:, 0:128], zc[:, :], identb[:, :],
                          reads=[zc.name, identb_k], writes=[pbk],
                          tag="l2a:zT")
            p.s("copy", zT[:, j * 128:(j + 1) * 128], pb[:, 0:128],
                reads=[pbk], writes=[zT_k], tag="l2a:zTc")
        p.s("activation", zT[:, :], zT[:, :], AF.Silu, reads=[zT_k],
            writes=[zT_k], tag="l2a:zs")
        p.v("tensor_mul", ytok[:, :], ytok[:, :], zT[:, :],
            reads=[ytok_k, zT_k], writes=[ytok_k], tag="l2a:yz")
        st = sb.get("lnst0", 128, 8, F32)
        p.s("activation", zT[:, :], ytok[:, :], AF.Square,
            accum_out=st[:, 0:1], reads=[ytok_k],
            writes=[zT_k, st.name], tag="l2a:ysq")
        p.v("tensor_scalar", st[:, 1:2], st[:, 0:1], 1.0 / DIN, 1e-5,
            ALU.mult, ALU.add, reads=[st.name], writes=[st.name],
            tag="l2a:yv")
        p.s("activation", st[:, 2:3], st[:, 1:2], AF.Sqrt, reads=[st.name],
            writes=[st.name], tag="l2a:ysd")
        p.v("reciprocal", st[:, 3:4], st[:, 2:3], reads=[st.name],
            writes=[st.name], tag="l2a:yrs")
        p.v("tensor_scalar", ytok[:, :], ytok[:, :], st[:, 3:4], None,
            ALU.mult, reads=[ytok_k, st.name], writes=[ytok_k],
            tag="l2a:yn")
        p.v("tensor_mul", ytok[:, :], ytok[:, :], nwbc[:, :],
            reads=[ytok_k, nwbc_k], writes=[ytok_k], tag="l2a:ynw")
        for j in range(16):
            zc = sb.get("l2azc2", 128, 128, BF16)
            p.v("tensor_copy", zc[:, :], ytok[:, j * 128:(j + 1) * 128],
                reads=[ytok_k], writes=[zc.name], tag="l2a:ycst")
            pb, pbk = ctx.psum_b()
            p.t_transpose(pb[:, 0:128], zc[:, :], identb[:, :],
                          reads=[zc.name, identb_k], writes=[pbk],
                          tag="l2a:yT")
            p.s("copy", ybuf[:, j * 128:(j + 1) * 128], pb[:, 0:128],
                reads=[pbk], writes=[ybuf_k], tag="l2a:yTc")
        dsty = y_dram[0:16, :, c0:c0 + Q].rearrange("j p t -> p j t")
        p.dma("sync", lambda e, dsty=dsty, ybuf=ybuf: e.dma_start(
            out=dsty, in_=ybuf.rearrange("p (j t) -> p j t", t=Q)),
            reads=[ybuf_k], writes=[f"l2a_yd:{c}"], group=ybuf_k,
            tag="l2a:yst")

    p.v("tensor_copy", Sst[0:1, 0:1], Sst[0:1, 0:1], reads=Skeys,
        writes=[Sst_k], tag="l2a:Sbar")
    p.v("tensor_copy", sp0f[0:1, 0:1], sp0f[0:1, 0:1],
        reads=["l2acr0", "l2acr1", "l2acr2", "l2acr3"],
        writes=["spill0", "spill1"], tag="l2a:crbar")
    ctx._psf_n = 4

    ctx.pool.reset()
    ws_o = WStream(ctx, "l2aop", weights["opT"], 256)
    y = ctx.pool.alloc_fm("ytt", D, 512, BF16)
    ybk = [ctx.pool.alloc_fm(f"l2a_yb{i}", DIN, 512, BF16) for i in range(2)]

    def ep_o(ctx, ps, pskey, ocg, tt0, tw):
        p.s("copy", y.chunks[ocg][:, 0:tw], ps[:, :tw], reads=[pskey],
            writes=[y.key(ocg)], tag="l2a:op")

    for tt in range(T // 512):
        tt0 = tt * 512
        yb = ybk[tt % 2]
        for j in range(16):
            p.load(yb.chunks[j][:, 0:512], y_dram[j, :, tt0:tt0 + 512],
                   engine="sync",
                   reads=[f"l2a_yd:{cc}" for cc in range(tt * 4, tt * 4 + 4)],
                   writes=[yb.key(j)], tag="l2a:ybl")
        gemm_fm(ctx, yb, ws_o, ep_o, t0=0, tn=512, toff=tt0, tag=f"{tag}:op")
        residual_add(ctx, x_src, x_dst, y, tt0, 512, scale=1.0,
                     tag=f"{tag}:res")


def prep_l2_attn(inp):
    ip = inp["L2_attn_in_proj_w"]
    ipp = np.zeros((4608, 1024), np.float32)
    ipp[:4384] = ip
    cw = inp["L2_attn_conv_w"]
    out = {
        "ipT": np.ascontiguousarray(ipp.T),
        "opT": np.ascontiguousarray(inp["L2_attn_out_proj_w"].T),
        "dtbc": np.ascontiguousarray(
            np.pad(inp["L2_attn_dt_bias"], (0, 96)).reshape(-1, 128).T),
        "alogc": np.ascontiguousarray(
            np.pad(inp["L2_attn_A_log"], (0, 96)).reshape(-1, 128).T),
        "cbc": np.ascontiguousarray(
            np.pad(inp["L2_attn_conv_b"], (0, 0)).reshape(-1, 128).T),
        "D_row": inp["L2_attn_D"][None, :].astype(BF16NP),
        "nw_row": inp["L2_attn_norm_w"][None, :].astype(BF16NP),
    }
    for k in range(4):
        out[f"cwc{k}"] = np.ascontiguousarray(cw[:, k].reshape(-1, 128).T)
    return out



def prep_l0_ffn(inp):
    return {
        "wgT": np.ascontiguousarray(inp["L0_ffn_w_gate"].T),
        "wuT": np.ascontiguousarray(inp["L0_ffn_w_up"].T),
        "wdT": np.ascontiguousarray(inp["L0_ffn_w_down"].T),
    }


SUBLAYER_DEFS = {
    "l0_ffn": (emit_l0_ffn, prep_l0_ffn),
    "l1_ffn": (emit_l1_ffn, prep_l1_ffn),
    "l0_attn": (emit_l0_attn, prep_l0_attn),
    "l1_attn": (emit_l1_attn, prep_l1_attn),
    "l2_attn": (emit_l2_attn, prep_l2_attn),
    "l2_ffn": (emit_l2_ffn, prep_l2_ffn),
    "l3_attn": (emit_l3_attn, prep_l3_attn),
    "l3_ffn": (emit_l3_ffn, prep_l3_ffn),
}

MODULE_STEPS = [
    ("l0_ffn", "s0"), ("l0_attn", "s1"), ("l0_ffn", "s2"),
    ("l1_ffn", "s3"), ("l1_attn", "s4"), ("l1_ffn", "s5"),
    ("l2_ffn", "s6"), ("l2_attn", "s7"), ("l2_ffn", "s8"),
    ("l3_ffn", "s9"), ("l3_attn", "s10"), ("l3_ffn", "s11"),
]


def build_sublayer_program(step_names, weights_all, serial=False):
    nc = bass.Bass("TRN2", target_bir_lowering=False, debug=False)
    ctx = Ctx(nc, serial=serial)
    x_in = nc.dram_tensor("x_in", [T, D], F32, kind="ExternalInput").ap()
    x_out = nc.dram_tensor("x_out", [T, D], F32, kind="ExternalOutput").ap()
    n = len(step_names)
    if n > 1:
        x_mid = ctx.dscratch("x_mid", [T, D], F32)
    for si, name in enumerate(step_names):
        emit, _ = SUBLAYER_DEFS[name]
        src = x_in if si == 0 else x_mid
        dst = x_out if si == n - 1 else x_mid
        emit(ctx, src, dst, weights_all[name], tag=f"s{si}")
    ctx.p.op("sync", lambda e: e.nop(),
             reads=[f"xres:{i}" for i in range(T // 128)], tag="finish")
    ctx.p.build()
    return nc, ctx


_prog_cache = {}


def _get_program(key, builder):
    if key not in _prog_cache:
        _prog_cache[key] = builder()
    return _prog_cache[key]


def _emit_module(ctx, nc):
    x_in = nc.dram_tensor("x_in", [T, D], F32, kind="ExternalInput").ap()
    x_out = nc.dram_tensor("x_out", [T, D], F32, kind="ExternalOutput").ap()
    x_mid = ctx.dscratch("x_mid", [T, D], F32)
    n = len(MODULE_STEPS)
    for si, (name, tag) in enumerate(MODULE_STEPS):
        emit, _ = SUBLAYER_DEFS[name]
        src = x_in if si == 0 else x_mid
        dst = x_out if si == n - 1 else x_mid
        emit(ctx, src, dst, ctx._weights_all[name], tag=tag)
    ctx.p.op("sync", lambda e: e.nop(),
             reads=[f"xres:{i}" for i in range(T // 128)], tag="finish")


def build_fused_program(weights_all, serial=False):
    nc1 = bass.Bass("TRN2", target_bir_lowering=False, debug=False)
    ctx1 = Ctx(nc1)
    ctx1._weights_all = weights_all
    _emit_module(ctx1, nc1)
    nc = bass.Bass("TRN2", target_bir_lowering=False, debug=False)
    ctx = Ctx(nc, serial=serial)
    ctx._weights_all = weights_all
    ctx.const_group = "consts"
    for kind, name, arr, dtype in ctx1.const_log:
        if kind == "const":
            ctx.const(name, arr, dtype)
        else:
            col_tile(ctx, name, arr)
    ctx.pool.reset()
    _emit_module(ctx, nc)
    for kind, name, arr, dtype in ctx.const_log[len(ctx1.const_log):]:
        raise AssertionError(f"straggler const {name}")
    ctx.p.build()
    return nc, ctx


def kernel(**inputs):
    B = 8
    x_full = np.ascontiguousarray(inputs["L0_attn_x"], dtype=np.float32)

    weights_all = {}
    for name, (_, prep) in SUBLAYER_DEFS.items():
        weights_all[name] = prep({k: np.asarray(v) for k, v in inputs.items()
                                  if hasattr(v, 'shape')})

    nc, ctx = _get_program(("fused",), lambda: build_fused_program(weights_all))
    in_maps = []
    for c in range(B):
        m = dict(ctx.host_inputs)
        m["x_in"] = np.ascontiguousarray(x_full[c])
        in_maps.append(m)
    res = run_bass_kernel_spmd(nc, in_maps, core_ids=list(range(B)))
    return np.stack([np.asarray(res.results[c]["x_out"], dtype=np.float32)
                     for c in range(B)], axis=0)






# version 15 (final)
# speedup vs baseline: 1.1830x; 1.0042x over previous
import numpy as np
import ml_dtypes
import concourse.bass as bass
import concourse.mybir as mybir
from concourse.bass_utils import run_bass_kernel_spmd

F32 = mybir.dt.float32
BF16 = mybir.dt.bfloat16
I32 = mybir.dt.int32
AF = mybir.ActivationFunctionType
ALU = mybir.AluOpType
AX = mybir.AxisListType
BF16NP = ml_dtypes.bfloat16

T = 4096
D = 1024
NEG = -1e30

ENGINES = ("tensor", "vector", "scalar", "sync", "gpsimd")
DMA_ENGINES = ("sync", "gpsimd", "scalar")



class Op:
    __slots__ = ("idx", "engine", "emit", "reads", "writes", "is_dma", "deps",
                 "need_inc", "inc_idx", "gid", "waits", "snap", "tag", "lowp")

    def __init__(self):
        self.need_inc = False
        self.inc_idx = 0
        self.gid = -1
        self.waits = None
        self.snap = None
        self.tag = ""
        self.lowp = False


class Prog:

    def __init__(self, nc, serial=False):
        self.nc = nc
        self.ops = []
        self.key_state = {}
        self.serial = serial
        self.groups = {}
        self.group_batch = {}
        self.group_consumed = {}

    def op(self, engine, emit, reads=(), writes=(), is_dma=False, tag="",
           group=None, lowp=False):
        idx = len(self.ops)
        o = Op()
        o.lowp = lowp
        o.idx = idx
        o.engine = engine
        o.emit = emit
        o.reads = tuple(reads)
        o.writes = tuple(writes)
        o.is_dma = is_dma
        o.tag = tag
        if is_dma:
            gname = group if group is not None else (
                o.writes[0] if o.writes else f"anon{idx}")
            gname = f"{gname}@{engine}"
            if gname not in self.groups:
                self.groups[gname] = len(self.groups)
            o.gid = self.groups[gname]
        deps = set()
        ops = self.ops

        def keep(d):
            od = ops[d]
            return not (od.engine == engine == "tensor"
                        and not od.is_dma and not is_dma)

        for k in o.reads:
            st = self.key_state.get(k)
            if st is None:
                self.key_state[k] = [None, [idx]]
            else:
                lw = st[0]
                if lw is not None and keep(lw):
                    deps.add(lw)
                st[1].append(idx)
        for k in o.writes:
            st = self.key_state.get(k)
            if st is not None:
                lw = st[0]
                if lw is not None and keep(lw):
                    deps.add(lw)
                for r in st[1]:
                    if r != idx and keep(r):
                        deps.add(r)
            self.key_state[k] = [idx, []]
        if self.serial and idx > 0:
            deps.add(idx - 1)
        o.deps = deps
        ops.append(o)
        return idx

    def dma(self, engine, emit, reads=(), writes=(), tag="", group=None):
        return self.op(engine, emit, reads, writes, is_dma=True, tag=tag,
                       group=group)

    def _schedule(self):
        ops = self.ops
        ngroups = len(self.groups)
        inc_count = {e: 0 for e in ENGINES}
        for o in ops:
            for d in o.deps:
                ops[d].need_inc = True
        for o in ops:
            if not o.is_dma and o.need_inc:
                inc_count[o.engine] += 1
                o.inc_idx = inc_count[o.engine]

        know = {e: {} for e in ENGINES}
        cum_dma = {}
        gcount = [0] * ngroups
        cum_inc = {e: {} for e in ENGINES}
        pos_inc = {e: 0 for e in ENGINES}

        def merge(dst, src):
            for k, v in src.items():
                if dst.get(k, -1) < v:
                    dst[k] = v

        for o in ops:
            e = o.engine
            kn = know[e]
            waits = []
            for d in sorted(o.deps):
                od = ops[d]
                if od.is_dma:
                    g = od.gid
                    need = gcount[g]
                    key = f"d:{g}"
                    if kn.get(key, 0) < need:
                        waits.append(("dma", g, need))
                        kn[key] = need
                        merge(kn, cum_dma.get(g, {}))
                    self.group_consumed[g] = True
                else:
                    key = "i:" + od.engine
                    if kn.get(key, 0) < od.inc_idx:
                        waits.append(("inc", od.engine, od.inc_idx))
                        kn[key] = od.inc_idx
                        merge(kn, od.snap)
            o.waits = waits
            if o.is_dma:
                g = o.gid
                gcount[g] += 1
                snap = dict(kn)
                snap[f"d:{g}"] = gcount[g]
                cs = cum_dma.setdefault(g, {})
                merge(cs, snap)
                o.snap = cs
            else:
                if o.need_inc:
                    pos_inc[e] = o.inc_idx
                snap = dict(kn)
                snap["i:" + e] = pos_inc[e]
                if o.need_inc:
                    merge(cum_inc[e], snap)
                    o.snap = dict(cum_inc[e])
                else:
                    o.snap = snap

    def build(self):
        self._schedule()
        nc = self.nc
        per_engine = {e: [] for e in ENGINES}
        for o in self.ops:
            per_engine[o.engine].append(o)

        sem_ctxs = []
        prog_sem = {}
        dma_sem = {}
        used_eng = {o.engine for o in self.ops if not o.is_dma and o.need_inc}
        for e in ENGINES:
            if e not in used_eng:
                continue
            cm = nc.semaphore(f"p_{e}")
            prog_sem[e] = cm.__enter__()
            sem_ctxs.append(cm)
        ngroups = len(self.groups)
        if ngroups > 230:
            raise AssertionError(f"too many dma groups: {ngroups}")
        for g in range(ngroups):
            cm = nc.semaphore(f"g{g}")
            dma_sem[g] = cm.__enter__()
            sem_ctxs.append(cm)

        block_cm = nc.Block()
        block = block_cm.__enter__()

        def make_engine_fn(ename):
            def fn(eng):
                for o in per_engine[ename]:
                    for kind, de, val in o.waits:
                        if kind == "dma":
                            eng.wait_ge(dma_sem[de], 16 * val)
                        else:
                            eng.wait_ge(prog_sem[de], val)
                    if o.lowp:
                        with nc.allow_low_precision(reason="intended bf16 combine"):
                            ins = o.emit(eng)
                    else:
                        ins = o.emit(eng)
                    if o.is_dma:
                        ins.then_inc(dma_sem[o.gid], 16)
                    elif o.need_inc:
                        ins.then_inc(prog_sem[ename], 1)
            return fn

        block.tensor(make_engine_fn("tensor"))
        block.vector(make_engine_fn("vector"))
        block.scalar(make_engine_fn("scalar"))
        block.sync(make_engine_fn("sync"))
        block.gpsimd(make_engine_fn("gpsimd"))
        block_cm.__exit__(None, None, None)
        for cm in sem_ctxs:
            cm.__exit__(None, None, None)

    def t_matmul(self, out, lhsT, rhs, start, stop, reads, writes, tag=""):
        return self.op(
            "tensor",
            lambda t, out=out, lhsT=lhsT, rhs=rhs, start=start, stop=stop:
                t.matmul(out, lhsT, rhs, start=start, stop=stop),
            reads=reads, writes=writes, tag=tag)

    def t_transpose(self, out, in_, ident, reads, writes, tag=""):
        return self.op(
            "tensor",
            lambda t, out=out, in_=in_, ident=ident: t.transpose(out, in_, ident),
            reads=reads, writes=writes, tag=tag)

    def v(self, fname, *args, reads=(), writes=(), tag="", lowp=False, **kw):
        def emit(eng, fname=fname, args=args, kw=kw):
            return getattr(eng, fname)(*args, **kw)
        return self.op("vector", emit, reads=reads, writes=writes, tag=tag,
                       lowp=lowp)

    def s(self, fname, *args, reads=(), writes=(), tag="", lowp=False, **kw):
        def emit(eng, fname=fname, args=args, kw=kw):
            return getattr(eng, fname)(*args, **kw)
        return self.op("scalar", emit, reads=reads, writes=writes, tag=tag,
                       lowp=lowp)

    def load(self, dst, src, engine="sync", reads=(), writes=(), tag="",
             group=None):
        return self.dma(engine, lambda e, dst=dst, src=src: e.dma_start(out=dst, in_=src),
                        reads=reads, writes=writes, tag=tag, group=group)



class Sb:

    def __init__(self, nc):
        self.nc = nc
        self.tensors = {}

    def get(self, name, parts, cols, dtype):
        key = (name, parts, cols, dtype)
        if key not in self.tensors:
            for (n2, p2, c2, d2) in self.tensors:
                assert n2 != name, (
                    f"sbuf name '{name}' reused with different shape: "
                    f"({parts},{cols},{dtype}) vs ({p2},{c2},{d2})")
            self.tensors[key] = self.nc.alloc_sbuf_tensor(
                name, [parts, cols], dtype)
        return self.tensors[key]


class FM:

    def __init__(self, sb, name, d, t, dtype, backing=None):
        assert d % 128 == 0
        self.name, self.d, self.t, self.dtype = name, d, t, dtype
        self.nchunk = d // 128
        if backing is None:
            tens = [sb.get(f"{name}_c{i}", 128, t, dtype)
                    for i in range(self.nchunk)]
            self.chunks = [tt[:, 0:t] for tt in tens]
            self._keys = [tt.name for tt in tens]
        else:
            self.chunks = []
            self._keys = []
            for tt, off in backing:
                ap = tt[:, off:off + (t if dtype == tt.dtype else 2 * t)]
                if dtype != tt.dtype:
                    ap = ap.bitcast(dtype)
                self.chunks.append(ap)
                self._keys.append(tt.name)

    def key(self, i):
        return self._keys[i]

    def keys(self):
        return list(dict.fromkeys(self._keys))


class Pool:

    COLS = 4096

    def __init__(self, sb, n, name="gp"):
        self.tensors = [sb.get(f"{name}{i}", 128, self.COLS, BF16)
                        for i in range(n)]
        self.cursor = [0] * n

    def reset(self):
        self.cursor = [0] * len(self.tensors)

    def mark(self):
        return list(self.cursor)

    def release(self, m):
        self.cursor = list(m)

    def alloc_fm(self, name, d, t, dtype):
        need = t if dtype == BF16 else 2 * t
        nchunk = d // 128
        backing = []
        for _ in range(nchunk):
            for i, c in enumerate(self.cursor):
                if self.COLS - c >= need:
                    backing.append((self.tensors[i], c))
                    self.cursor[i] = c + need
                    break
            else:
                raise AssertionError(
                    f"pool exhausted allocating {name} ({d}x{t} {dtype})")
        return FM(None, name, d, t, dtype, backing=backing)

    def alloc_cols(self, ncols):
        for i, c in enumerate(self.cursor):
            if self.COLS - c >= ncols:
                self.cursor[i] = c + ncols
                return self.tensors[i][:, c:c + ncols], self.tensors[i].name
        raise AssertionError(f"pool exhausted allocating {ncols} cols")



class Ctx:
    def __init__(self, nc, serial=False, npool=16):
        self.nc = nc
        self.p = Prog(nc, serial=serial)
        self.sb = Sb(nc)
        self.host_inputs = {}
        self.dram = {}
        self.const_sbuf = {}
        self.const_log = []
        self.const_group = None
        self.pool = Pool(self.sb, npool)
        self._ws_rot = {}
        self.psf = [nc.alloc_psum_tensor(f"psf{i}", [128, 512], F32) for i in range(6)]
        self.psb = [nc.alloc_psum_tensor(f"psb{i}", [128, 1024], BF16) for i in range(2)]
        self._psf_i = 0
        self._psf_n = 4
        self._psb_i = 0

    def new_phase(self):
        self.pool.reset()
        self._ws_rot = {}

    def ws_get(self, cols):
        i = self._ws_rot.get(cols, 0)
        self._ws_rot[cols] = i + 1
        return self.sb.get(f"ws{cols}_{i}", 128, cols, BF16)

    def psum_f(self):
        i = self._psf_i % self._psf_n
        self._psf_i += 1
        return self.psf[i], f"psf{i}"

    def psum_b(self):
        i = self._psb_i % 2
        self._psb_i += 1
        return self.psb[i], f"psb{i}"

    def din(self, name, arr, dtype=None):
        if name in self.dram:
            return self.dram[name]
        if dtype is None:
            dtype = {np.dtype(np.float32): F32, np.dtype(np.int32): I32,
                     np.dtype(BF16NP): BF16}[arr.dtype]
        ap = self.nc.dram_tensor(name, list(arr.shape), dtype, kind="ExternalInput").ap()
        self.dram[name] = ap
        self.host_inputs[name] = arr
        return ap

    def dout(self, name, shape, dtype):
        ap = self.nc.dram_tensor(name, list(shape), dtype, kind="ExternalOutput").ap()
        self.dram[name] = ap
        return ap

    def dscratch(self, name, shape, dtype):
        if name in self.dram:
            return self.dram[name]
        ap = self.nc.dram_tensor(name, list(shape), dtype, kind="Internal").ap()
        self.dram[name] = ap
        return ap

    def const(self, name, arr, dtype=None):
        if name in self.const_sbuf:
            return self.const_sbuf[name]
        self.const_log.append(("const", name, arr, dtype))
        ap = self.din("c_" + name, arr, dtype)
        parts = arr.shape[0] if arr.ndim > 1 else 1
        cols = arr.shape[-1] if arr.ndim > 1 else arr.shape[0]
        sdt = {np.dtype(np.float32): F32, np.dtype(BF16NP): BF16,
               np.dtype(np.int32): I32}[arr.dtype] if dtype is None else dtype
        t = self.sb.get("cs_" + name, parts, cols, sdt)
        key = "cs_" + name
        src = ap if arr.ndim > 1 else ap[None, :]
        self.p.load(t[:, :], src, engine="sync", writes=[key], tag="const",
                    group=self.const_group)
        self.const_sbuf[name] = (t, key)
        return t, key

    def ident_bf(self):
        return self.const("identb", np.eye(128).astype(BF16NP))

    def ident_f32(self):
        return self.const("identf", np.eye(128, dtype=np.float32))

    def ones_row_bf(self):
        return self.const("ones1", np.ones((1, 128), BF16NP))


def pack_w(W, bw):
    din, dout = W.shape
    assert din % 128 == 0 and dout % bw == 0
    nk, nob = din // 128, dout // bw
    Wr = W.reshape(nk, 128, nob, bw).transpose(2, 1, 0, 3).reshape(nob, 128, nk * bw)
    return np.ascontiguousarray(Wr)



class WStream:

    def __init__(self, ctx, name, W_np, bw, nbuf=2, dtype_override=None,
                 share=None):
        self.ctx = ctx
        self.bw = bw
        self.din, self.dout = W_np.shape
        self.nk = self.din // 128
        self.nob = self.dout // bw
        packed = pack_w(W_np, bw)
        self.dram = ctx.din("w_" + name, packed, dtype_override)
        if share is not None:
            assert share.nk * share.bw >= self.nk * bw
            self.bufs = [ap[:, 0:self.nk * bw] for ap in share.bufs]
            self.buf_keys = list(share.buf_keys)
            nbuf = share.nbuf
        else:
            pairs = [ctx.pool.alloc_cols(self.nk * bw) for i in range(nbuf)]
            self.bufs = [ap for ap, _ in pairs]
            self.buf_keys = [k for _, k in pairs]
        self.nbuf = nbuf
        self.name = name

    def load_block(self, ob):
        b = ob % self.nbuf
        t = self.bufs[b]
        key = self.buf_keys[b]
        self.ctx.p.load(t[:, :], self.dram[ob], engine="gpsimd",
                        writes=[key], tag=f"w:{self.name}:{ob}")
        return t, key

    def tile(self, ob, k, oc):
        t = self.bufs[ob % self.nbuf]
        return t[:, k * self.bw + oc * 128: k * self.bw + (oc + 1) * 128]

    def buf_key(self, ob):
        return self.buf_keys[ob % self.nbuf]


def gemm_fm(ctx, A, ws, epilogue, t0=0, tn=None, tw=512, tag="", toff=0,
            blocks=None):
    p = ctx.p
    tn = tn if tn is not None else A.t
    ntt = (tn - t0) // tw
    old_psf_n = ctx._psf_n
    ctx._psf_n = 6
    for ob in (blocks if blocks is not None else range(ws.nob)):
        wt, wkey = ws.load_block(ob)
        tts = list(range(ntt))
        pairs = [tts[i:i + 4] for i in range(0, ntt, 4)]
        for pr in pairs:
            for oc in range(ws.bw // 128):
                ocg = ob * (ws.bw // 128) + oc
                pss = [ctx.psum_f() for _ in pr]
                for k in range(ws.nk):
                    for j, tt in enumerate(pr):
                        tt0 = t0 + tt * tw
                        p.t_matmul(pss[j][0][:, :tw], ws.tile(ob, k, oc),
                                   A.chunks[k][:, tt0:tt0 + tw],
                                   start=(k == 0), stop=(k == ws.nk - 1),
                                   reads=[wkey, A.key(k)],
                                   writes=[pss[j][1]],
                                   tag=f"mm:{tag}:{ob}:{tt}:{oc}")
                for j, tt in enumerate(pr):
                    tt0 = t0 + tt * tw
                    epilogue(ctx, pss[j][0], pss[j][1], ocg, toff + tt0, tw)
    ctx._psf_n = old_psf_n


def ep_to_fm(out_fm, func=None, scale=1.0, bias_tile=None, bias_key=None,
             engine="scalar"):
    def ep(ctx, ps, pskey, ocg, tt0, tw):
        p = ctx.p
        dst = out_fm.chunks[ocg][:, tt0:tt0 + tw]
        reads = [pskey] + ([bias_key] if bias_key else [])
        writes = [out_fm.key(ocg)]
        if func is None and bias_tile is None and scale == 1.0:
            p.s("copy", dst, ps[:, :tw], reads=reads, writes=writes, tag="ep_copy")
        else:
            kw = {}
            if bias_tile is not None:
                kw["bias"] = bias_tile[:, ocg:ocg + 1]
            if scale != 1.0:
                kw["scale"] = float(scale)
            p.s("activation", dst, ps[:, :tw], func if func else AF.Copy,
                reads=reads, writes=writes, tag="ep_act", **kw)
    return ep


def ep_mult_fm(gate_fm, out_fm):
    def ep(ctx, ps, pskey, ocg, tt0, tw):
        p = ctx.p
        dst = out_fm.chunks[ocg][:, tt0:tt0 + tw]
        p.v("tensor_tensor", dst, ps[:, :tw], gate_fm.chunks[ocg][:, tt0:tt0 + tw],
            ALU.mult, reads=[pskey, gate_fm.key(ocg)], writes=[out_fm.key(ocg)],
            tag="ep_mult")
    return ep



def ln_pass(ctx, x_dram, out_bf, out_f32=None, eps=1e-5, t0=0, nt=T, tag="ln",
            gamma=None, beta=None, out_dram=None, src_key="xres", dst_key=None):
    p, sb = ctx.p, ctx.sb
    if out_bf is not None:
        identb, identb_k = ctx.ident_bf()
    if out_f32 is not None:
        identf, identf_k = ctx.ident_f32()
    ntile = nt // 128
    for ii in range(ntile):
        i = t0 // 128 + ii
        r = i % 2
        xt = sb.get(f"lnx{r}", 128, D, F32)
        xk = f"lnx{r}"
        st = sb.get(f"lnst{r}", 128, 8, F32)
        stk = f"lnst{r}"
        lt = sb.get(f"lnt{r}", 128, D, BF16)
        ltk = f"lnt{r}"
        if out_f32 is not None or out_dram is not None:
            lf = sb.get(f"resx{r}", 128, D, F32)
            lfk = f"resx{r}"
        p.load(xt[:, :], x_dram[i * 128:(i + 1) * 128, :], engine="sync",
               reads=[f"{src_key}:{i}"], writes=[xk], tag=f"{tag}:load{i}")
        p.v("tensor_reduce", st[:, 0:1], xt[:, :], AX.X, ALU.add,
            reads=[xk], writes=[stk], tag=f"{tag}:sum")
        p.v("tensor_scalar", st[:, 0:1], st[:, 0:1], 1.0 / D, None, ALU.mult,
            reads=[stk], writes=[stk], tag=f"{tag}:mean")
        p.v("tensor_scalar", xt[:, :], xt[:, :], st[:, 0:1], None, ALU.subtract,
            reads=[xk, stk], writes=[xk], tag=f"{tag}:sub")
        p.s("activation", lt[:, :], xt[:, :], AF.Square,
            reads=[xk], writes=[ltk, stk], accum_out=st[:, 1:2], tag=f"{tag}:sq")
        p.v("tensor_scalar", st[:, 2:3], st[:, 1:2], 1.0 / D, float(eps),
            ALU.mult, ALU.add, reads=[stk], writes=[stk], tag=f"{tag}:var")
        p.s("activation", st[:, 3:4], st[:, 2:3], AF.Sqrt,
            reads=[stk], writes=[stk], tag=f"{tag}:sd")
        p.v("reciprocal", st[:, 4:5], st[:, 3:4],
            reads=[stk], writes=[stk], tag=f"{tag}:rs")
        if out_bf is not None:
            p.v("tensor_scalar", lt[:, :], xt[:, :], st[:, 4:5], None, ALU.mult,
                reads=[xk, stk], writes=[ltk], tag=f"{tag}:nrm")
        if out_f32 is not None or out_dram is not None:
            p.v("tensor_scalar", lf[:, :], xt[:, :], st[:, 4:5], None, ALU.mult,
                reads=[xk, stk], writes=[lfk], tag=f"{tag}:nrmf")
        if out_dram is not None:
            p.dma("sync", lambda e, lf=lf, i=i: e.dma_start(
                out=out_dram[i * 128:(i + 1) * 128, :], in_=lf[:, :]),
                reads=[lfk], writes=[f"{dst_key}:{i}"], group=lfk,
                tag=f"{tag}:zst")
        for j in range(0 if out_bf is not None or out_f32 is not None else D // 128,
                       D // 128):
            if out_bf is not None:
                pb, pbk = ctx.psum_b()
                p.t_transpose(pb[:, :128], lt[:, j * 128:(j + 1) * 128],
                              identb[:, :], reads=[ltk, identb_k], writes=[pbk],
                              tag=f"{tag}:tr")
                dst = out_bf.chunks[j][:, ii * 128:(ii + 1) * 128]
                if gamma is not None:
                    p.v("tensor_scalar", dst, pb[:, :128], gamma[0][:, j:j + 1],
                        beta[0][:, j:j + 1], ALU.mult, ALU.add,
                        reads=[pbk, gamma[1], beta[1]], writes=[out_bf.key(j)],
                        tag=f"{tag}:trgb")
                else:
                    p.v("tensor_copy", dst, pb[:, :128], reads=[pbk],
                        writes=[out_bf.key(j)], tag=f"{tag}:trc")
            if out_f32 is not None:
                pf, pfk = ctx.psum_f()
                p.t_transpose(pf[:, :128], lf[:, j * 128:(j + 1) * 128],
                              identf[:, :], reads=[lfk, identf_k], writes=[pfk],
                              tag=f"{tag}:trf")
                p.v("tensor_copy", out_f32.chunks[j][:, ii * 128:(ii + 1) * 128],
                    pf[:, :128], reads=[pfk], writes=[out_f32.key(j)],
                    tag=f"{tag}:trfc")



def residual_add(ctx, x_src, x_dst, y_fm, t0, tw, scale=1.0, y_t0=0, tag="res",
                 res_key="xres"):
    p, sb = ctx.p, ctx.sb
    if y_fm.dtype == BF16:
        ident, identk = ctx.ident_bf()
    else:
        ident, identk = ctx.ident_f32()
    ntile = tw // 128
    for i in range(ntile):
        r = i % 2
        xt = sb.get(f"resx{r}", 128, D, F32)
        xk = f"resx{r}"
        g0 = t0 + i * 128
        p.load(xt[:, :], x_src[g0:g0 + 128, :], engine="sync",
               reads=[f"{res_key}:{g0 // 128}"], writes=[xk], tag=f"{tag}:load")
        for j in range(y_fm.nchunk):
            if y_fm.dtype == BF16:
                pb, pbk = ctx.psum_b()
                pbv = pb[:, :128]
            else:
                pb, pbk = ctx.psum_f()
                pbv = pb[:, :128]
            p.t_transpose(pbv, y_fm.chunks[j][:, y_t0 + i * 128:y_t0 + (i + 1) * 128],
                          ident[:, :], reads=[y_fm.key(j), identk], writes=[pbk],
                          tag=f"{tag}:tr")
            p.v("scalar_tensor_tensor", xt[:, j * 128:(j + 1) * 128], pbv,
                float(scale), xt[:, j * 128:(j + 1) * 128], ALU.mult, ALU.add,
                reads=[pbk, xk], writes=[xk], tag=f"{tag}:acc")
        p.dma("sync", lambda e, xt=xt, g0=g0: e.dma_start(
            out=x_dst[g0:g0 + 128, :], in_=xt[:, :]),
            reads=[xk], writes=[f"{res_key}:{g0 // 128}"], group=xk,
            tag=f"{tag}:store")



def emit_l0_ffn(ctx, x_src, x_dst, weights, tag):
    p, sb = ctx.p, ctx.sb
    HID = 3072
    ctx.new_phase()

    ws_g = WStream(ctx, "l0wg", weights["wgT"], 512)
    ws_u = WStream(ctx, "l0wu", weights["wuT"], 512)
    ws_d = WStream(ctx, "l0wd", weights["wdT"], 128)

    ln_bf = ctx.pool.alloc_fm("l0f_ln", D, 512, BF16)
    gate = ctx.pool.alloc_fm("l0f_gate", HID, 512, BF16)
    h = ctx.pool.alloc_fm("l0f_h", HID, 512, BF16)
    y = ctx.pool.alloc_fm("ytt", D, 512, BF16)

    SC = 1.0 / 127.0

    def ep_gate(ctx, ps, pskey, ocg, gt0, tw):
        p.s("activation", gate.chunks[ocg][:, 0:tw], ps[:, :tw], AF.Silu,
            scale=SC, reads=[pskey], writes=[gate.key(ocg)], tag="l0:gsilu")

    def ep_up(ctx, ps, pskey, ocg, gt0, tw):
        p.v("scalar_tensor_tensor", h.chunks[ocg][:, 0:tw], ps[:, :tw], SC,
            gate.chunks[ocg][:, 0:tw], ALU.mult, ALU.mult,
            reads=[pskey, gate.key(ocg)], writes=[h.key(ocg)], tag="l0:hup")

    def ep_down(ctx, ps, pskey, ocg, gt0, tw):
        p.s("activation", y.chunks[ocg][:, 0:tw], ps[:, :tw], AF.Copy,
            scale=SC, reads=[pskey], writes=[y.key(ocg)], tag="l0:down")

    for tt in range(T // 512):
        tt0 = tt * 512
        ln_pass(ctx, x_src, ln_bf, t0=tt0, nt=512, tag=f"{tag}:ln")
        gemm_fm(ctx, ln_bf, ws_g, ep_gate, t0=0, tn=512, toff=tt0,
                tag=f"{tag}:g")
        gemm_fm(ctx, ln_bf, ws_u, ep_up, t0=0, tn=512, toff=tt0,
                tag=f"{tag}:u")
        gemm_fm(ctx, h, ws_d, ep_down, t0=0, tn=512, toff=tt0, tag=f"{tag}:d")
        residual_add(ctx, x_src, x_dst, y, tt0, 512, scale=0.5, tag=f"{tag}:res")



def load_fm_from_dram(ctx, fm, dram, dname, t0, tw, tag="lfm"):
    p = ctx.p
    for c in range(fm.nchunk):
        p.load(fm.chunks[c][:, 0:tw], dram[c, :, t0:t0 + tw], engine="sync",
               reads=[f"{dname}:{c}"], writes=[fm.key(c)], tag=f"{tag}:{c}")


def ep_store_dram(ctx_, dram, dname, group_unused=None, func=None, scale_tile=None,
                  scale_key=None, mult_fm=None, dst_dtype=BF16, t_base=0,
                  bias_tile=None, bias_key=None):
    def ep(ctx, ps, pskey, ocg, tt0, tw):
        p, sb = ctx.p, ctx.sb
        r = (ocg + tt0 // 512) % 2 if dst_dtype == BF16 else 0
        nm = "spill" if dst_dtype == BF16 else "spillf"
        st = sb.get(f"{nm}{r}", 128, 512, dst_dtype)
        stk = st.name
        kw = {}
        if scale_tile is not None:
            kw["scale"] = scale_tile[:, ocg:ocg + 1]
        if bias_tile is not None:
            p.v("tensor_scalar", st[:, :tw], ps[:, :tw],
                bias_tile[:, ocg:ocg + 1], None, ALU.add,
                reads=[pskey, bias_key], writes=[stk], tag="eps_bias")
        elif mult_fm is None:
            p.s("activation", st[:, :tw], ps[:, :tw], func if func else AF.Copy,
                reads=[pskey] + ([scale_key] if scale_key else []),
                writes=[stk], tag="eps_act", **kw)
        else:
            p.v("scalar_tensor_tensor", st[:, :tw], ps[:, :tw],
                scale_tile[:, ocg:ocg + 1] if scale_tile is not None else 1.0,
                mult_fm.chunks[ocg][:, tt0 - t_base:tt0 - t_base + tw],
                ALU.mult, ALU.mult,
                reads=[pskey, mult_fm.key(ocg)] + ([scale_key] if scale_key else []),
                writes=[stk], tag="eps_stt")
        p.dma("sync", lambda e, st=st, ocg=ocg, tt0=tt0, tw=tw: e.dma_start(
            out=dram[ocg, :, tt0:tt0 + tw], in_=st[:, :tw]),
            reads=[stk], writes=[f"{dname}:{ocg}"], group=stk, tag="eps_dma")
    return ep


def col_tile(ctx, name, vec_2d):
    if name in ctx.const_sbuf:
        return ctx.const_sbuf[name]
    ctx.const_log.append(("col", name, vec_2d, None))
    ap = ctx.din("t_" + name, vec_2d)
    t = ctx.sb.get("ts_" + name, 128, vec_2d.shape[1],
                   F32 if vec_2d.dtype == np.float32 else BF16)
    ctx.p.load(t[:, :], ap, engine="sync", writes=[t.name], tag=f"ct:{name}",
               group=ctx.const_group)
    ctx.const_sbuf[name] = (t, t.name)
    return t, t.name



def emit_l3_ffn(ctx, x_src, x_dst, weights, tag):
    p, sb = ctx.p, ctx.sb
    HID = 4096
    NCH = HID // 128
    ctx.new_phase()

    s1t, s1k = col_tile(ctx, "l3s1", weights["s1c"])
    s2t, s2k = col_tile(ctx, "l3s2", weights["s2c"])
    sh1t, sh1k = col_tile(ctx, "l3sh1", weights["sh1c"])
    sh2t, sh2k = col_tile(ctx, "l3sh2", weights["sh2c"])
    s3t, s3k = col_tile(ctx, "l3s3", weights["s3c"])

    a_dram = ctx.dscratch("l3_a", [NCH, 128, T], BF16)
    b_dram = ctx.dscratch("l3_b", [NCH, 128, T], BF16)

    ln_bf = ctx.pool.alloc_fm("l3f_ln", D, T, BF16)
    ln_pass(ctx, x_src, ln_bf, tag=f"{tag}:ln")
    ws_1 = WStream(ctx, "l3w1", weights["w1T"], 512)
    ws_2 = WStream(ctx, "l3w2", weights["w2T"], 512)
    gate = ctx.pool.alloc_fm("l3f_gate", 512, T, BF16)

    for ob in range(HID // 512):
        def ep_gate(ctx, ps, pskey, ocg, gt0, tw, ob=ob):
            loc = ocg - ob * 4
            p.s("activation", gate.chunks[loc][:, gt0:gt0 + tw], ps[:, :tw],
                AF.Silu, scale=s1t[:, ocg:ocg + 1], reads=[pskey, s1k],
                writes=[gate.key(loc)], tag="l3:gsilu")

        def ep_a(ctx, ps, pskey, ocg, gt0, tw, ob=ob):
            loc = ocg - ob * 4
            r = (ocg + gt0 // 512) % 2
            st = sb.get(f"spill{r}", 128, 512, BF16)
            p.v("scalar_tensor_tensor", st[:, :tw], ps[:, :tw],
                s2t[:, ocg:ocg + 1], gate.chunks[loc][:, gt0:gt0 + tw],
                ALU.mult, ALU.mult, reads=[pskey, gate.key(loc), s2k],
                writes=[st.name], tag="l3:amul")
            p.dma("sync", lambda e, st=st, ocg=ocg, gt0=gt0, tw=tw: e.dma_start(
                out=a_dram[ocg, :, gt0:gt0 + tw], in_=st[:, :tw]),
                reads=[st.name], writes=[f"l3_a:{ocg}"], group=st.name,
                tag="l3:adma")

        gemm_fm(ctx, ln_bf, ws_1, ep_gate, blocks=[ob], tag=f"{tag}:w1")
        gemm_fm(ctx, ln_bf, ws_2, ep_a, blocks=[ob], tag=f"{tag}:w2")

    ctx.pool.reset()
    ws_h1 = WStream(ctx, "l3h1", weights["h1T"], 128)
    ws_h2 = WStream(ctx, "l3h2", weights["h2T"], 128)
    ws_3 = WStream(ctx, "l3w3", weights["w3T"], 128)
    y = ctx.pool.alloc_fm("ytt", D, 512, BF16)
    pm = ctx.pool.mark()

    for q in range(4):
        th0 = q * 1024
        ctx.pool.release(pm)
        ah = ctx.pool.alloc_fm("l3ah", HID, 1024, BF16)
        load_fm_from_dram(ctx, ah, a_dram, "l3_a", th0, 1024, tag=f"{tag}:lda")
        gemm_fm(ctx, ah, ws_h1,
                ep_store_dram(ctx, b_dram, "l3_b", f"{tag}:bst{q}",
                              func=AF.Gelu, scale_tile=sh1t, scale_key=sh1k),
                t0=0, tn=1024, toff=th0, tag=f"{tag}:h1")
        ctx.pool.release(pm)
        bh = ctx.pool.alloc_fm("l3bh", HID, 1024, BF16)
        load_fm_from_dram(ctx, bh, b_dram, "l3_b", th0, 1024, tag=f"{tag}:ldb")
        gemm_fm(ctx, bh, ws_h2,
                ep_store_dram(ctx, a_dram, "l3_a", f"{tag}:cst{q}",
                              func=AF.Gelu, scale_tile=sh2t, scale_key=sh2k),
                t0=0, tn=1024, toff=th0, tag=f"{tag}:h2")
        ctx.pool.release(pm)
        ch = ctx.pool.alloc_fm("l3ch", HID, 1024, BF16)
        load_fm_from_dram(ctx, ch, a_dram, "l3_a", th0, 1024, tag=f"{tag}:ldc")

        def ep_y(ctx, ps, pskey, ocg, gt0, tw):
            p.s("activation", y.chunks[ocg][:, 0:tw], ps[:, :tw], AF.Copy,
                scale=s3t[:, ocg:ocg + 1], reads=[pskey, s3k],
                writes=[y.key(ocg)], tag="l3:y")

        for tt in range(2):
            lt0 = tt * 512
            gemm_fm(ctx, ch, ws_3, ep_y, t0=lt0, tn=lt0 + 512, tag=f"{tag}:w3")
            residual_add(ctx, x_src, x_dst, y, th0 + lt0, 512, scale=0.5,
                         tag=f"{tag}:res")


def prep_l3_ffn(inp):
    def colpack(v):
        return np.ascontiguousarray(v.reshape(-1, 128).T.astype(np.float32))
    return {
        "w1T": np.ascontiguousarray(inp["L3_ffn_w1_w"].T),
        "w2T": np.ascontiguousarray(inp["L3_ffn_w2_w"].T),
        "h1T": np.ascontiguousarray(inp["L3_ffn_h1_w"].T),
        "h2T": np.ascontiguousarray(inp["L3_ffn_h2_w"].T),
        "w3T": np.ascontiguousarray(inp["L3_ffn_w3_w"].T),
        "s1c": colpack(inp["L3_ffn_w1_s"]),
        "s2c": colpack(inp["L3_ffn_w2_s"]),
        "sh1c": colpack(inp["L3_ffn_h1_s"]),
        "sh2c": colpack(inp["L3_ffn_h2_s"]),
        "s3c": colpack(inp["L3_ffn_w3_s"]),
    }



def emit_l1_ffn(ctx, x_src, x_dst, weights, tag):
    p, sb = ctx.p, ctx.sb
    E = 20
    ctx.new_phase()
    ln_bf = ctx.pool.alloc_fm("l1f_ln", D, T, BF16)
    ln_pass(ctx, x_src, ln_bf, tag=f"{tag}:ln")
    pm0 = ctx.pool.mark()

    hfm = ctx.pool.alloc_fm("l1f_h", 256, T, BF16)
    logit = ctx.pool.alloc_fm("l1f_lg", 128, T, BF16)
    ws_g1 = WStream(ctx, "l1wg1", weights["wg1"], 256)
    ws_g2 = WStream(ctx, "l1wg2", weights["wg2p"], 128)
    bg1t, bg1k = col_tile(ctx, "l1bg1", weights["bg1c"])
    bg2t, bg2k = col_tile(ctx, "l1bg2", weights["bg2c"])
    identf, identf_k = ctx.ident_f32()

    def ep_h(ctx, ps, pskey, ocg, tt0, tw):
        p.s("activation", hfm.chunks[ocg][:, tt0:tt0 + tw], ps[:, :tw], AF.Gelu,
            bias=bg1t[:, ocg:ocg + 1], reads=[pskey, bg1k],
            writes=[hfm.key(ocg)], tag="l1:h")

    def ep_lg(ctx, ps, pskey, ocg, tt0, tw):
        p.v("tensor_scalar", logit.chunks[0][:, tt0:tt0 + tw], ps[:, :tw],
            bg2t[:, 0:1], None, ALU.add, reads=[pskey, bg2k],
            writes=[logit.key(0)], tag="l1:lg")

    gemm_fm(ctx, ln_bf, ws_g1, ep_h, tag=f"{tag}:g1")
    gemm_fm(ctx, hfm, ws_g2, ep_lg, tag=f"{tag}:g2")

    gates = sb.get("l2mb", 128, 32 * 32, BF16)
    gk = gates.name
    p.v("memset", gates[:, :], 0.0, writes=[gk], tag="l1:gz")
    identb, identb_k = ctx.ident_bf()
    for i in range(T // 128):
        pb, pbk = ctx.psum_b()
        lgT = sb.get(f"l1lgT{i % 2}", 128, 32, F32)
        lk = lgT.name
        st = sb.get(f"l1st{i % 2}", 128, 8, F32)
        sk = st.name
        p.t_transpose(pb[:, :128], logit.chunks[0][:, i * 128:(i + 1) * 128],
                      identb[:, :], reads=[logit.key(0), identb_k],
                      writes=[pbk], tag="l1:lgT")
        p.v("tensor_copy", lgT[:, 0:20], pb[:, 0:20], reads=[pbk], writes=[lk],
            tag="l1:lgTc")
        p.v("reduce_max", st[:, 0:1], lgT[:, 0:20], AX.X, reads=[lk],
            writes=[sk], tag="l1:max")
        p.v("tensor_scalar", lgT[:, 0:20], lgT[:, 0:20], st[:, 0:1], None,
            ALU.subtract, reads=[lk, sk], writes=[lk], tag="l1:sub")
        p.s("activation", lgT[:, 0:20], lgT[:, 0:20], AF.Exp,
            accum_out=st[:, 1:2], reads=[lk], writes=[lk, sk], tag="l1:exp")
        p.v("reciprocal", st[:, 2:3], st[:, 1:2], reads=[sk], writes=[sk],
            tag="l1:rcp")
        p.v("tensor_scalar", lgT[:, 0:20], lgT[:, 0:20], st[:, 2:3], None,
            ALU.mult, reads=[lk, sk], writes=[lk], tag="l1:prob")
        p.v("reduce_max", st[:, 3:4], lgT[:, 0:20], AX.X, reads=[lk],
            writes=[sk], tag="l1:m1")
        m1col = st[:, 3:4]
        mask = sb.get(f"l1msk{i % 2}", 128, 20, F32)
        mk = mask.name
        p.v("tensor_scalar", mask[:, 0:20], lgT[:, 0:20], m1col, None,
            ALU.is_equal, reads=[lk, sk], writes=[mk], tag="l1:eq1")
        pm_t = sb.get(f"l1pm{i % 2}", 128, 20, F32)
        pmk = pm_t.name
        p.v("tensor_scalar", pm_t[:, 0:20], mask[:, 0:20], -1.0, 1.0, ALU.mult,
            ALU.add, reads=[mk], writes=[pmk], tag="l1:inv")
        p.v("tensor_mul", pm_t[:, 0:20], pm_t[:, 0:20], lgT[:, 0:20],
            reads=[pmk, lk], writes=[pmk], tag="l1:pm")
        p.v("reduce_max", st[:, 4:5], pm_t[:, 0:20], AX.X, reads=[pmk],
            writes=[sk], tag="l1:m2")
        p.v("tensor_scalar", pm_t[:, 0:20], pm_t[:, 0:20], st[:, 4:5], None,
            ALU.is_equal, reads=[pmk, sk], writes=[pmk], tag="l1:eq2")
        p.v("tensor_add", mask[:, 0:20], mask[:, 0:20], pm_t[:, 0:20],
            reads=[mk, pmk], writes=[mk], tag="l1:msum")
        p.v("tensor_mul", gates[:, i * 32:i * 32 + 20], mask[:, 0:20],
            lgT[:, 0:20], reads=[mk, lk], writes=[gk], tag="l1:gates")

    ctx.pool.release(pm0)
    web = [ctx.pool.alloc_cols(4096) for _ in range(4)]
    we_dram = ctx.din("w_l1we", weights["wep"])
    be_sb = sb.get("l1be", 20, 1024, BF16)
    p.load(be_sb[:, :], ctx.din("w_l1be", weights["be"]), engine="gpsimd",
           writes=[be_sb.name], tag="l1:be")
    acc = [ctx.pool.alloc_cols(1024) for j in range(8)]
    gT = sb.get("l1gT", 32, 128, BF16)

    for q in range(4):
        q0 = q * 1024
        for j in range(8):
            i = q * 8 + j
            pg, pgk = ctx.psum_b()
            p.t_transpose(pg[0:32, 0:128], gates[:, i * 32:(i + 1) * 32],
                          identb[:, :], reads=[gk, identb_k], writes=[pgk],
                          tag="l1:gT")
            p.v("tensor_copy", gT[:, :], pg[0:32, 0:128], reads=[pgk],
                writes=[gT.name], tag="l1:gTc")
            pf, pfk = ctx.psum_f()
            p.t_matmul(pf[:, 0:512], gT[0:20, :], be_sb[:, 0:512], True, True,
                       reads=[gT.name, be_sb.name], writes=[pfk], tag="l1:be1")
            p.v("tensor_copy", acc[j][0][:, 0:512], pf[:, 0:512], reads=[pfk],
                writes=[acc[j][1]], tag="l1:bec1", lowp=True)
            pf2, pfk2 = ctx.psum_f()
            p.t_matmul(pf2[:, 0:512], gT[0:20, :], be_sb[:, 512:1024], True,
                       True, reads=[gT.name, be_sb.name], writes=[pfk2],
                       tag="l1:be2")
            p.v("tensor_copy", acc[j][0][:, 512:1024], pf2[:, 0:512],
                reads=[pfk2], writes=[acc[j][1]], tag="l1:bec2", lowp=True)
        for e in range(E):
            b0, b0k = web[(e % 2) * 2]
            b1, b1k = web[(e % 2) * 2 + 1]
            p.load(b0[:, :], we_dram[e, :, 0:4096], engine="gpsimd",
                   writes=[b0k], tag=f"l1:we{e}a")
            p.load(b1[:, :], we_dram[e, :, 4096:8192], engine="gpsimd",
                   writes=[b1k], tag=f"l1:we{e}b")

            def wetile(k, db):
                c = k * 1024 + db * 512
                if c < 4096:
                    return b0[:, c:c + 512], b0k
                return b1[:, c - 4096:c - 4096 + 512], b1k

            for j in range(8):
                i = q * 8 + j
                for db in range(2):
                    pf, pfk = ctx.psum_f()
                    for k in range(8):
                        wt, wkk = wetile(k, db)
                        p.t_matmul(pf[:, 0:512], ln_bf.chunks[k][:, i * 128:(i + 1) * 128],
                                   wt, start=(k == 0), stop=(k == 7),
                                   reads=[ln_bf.key(k), wkk], writes=[pfk],
                                   tag=f"l1:mm{e}")
                    p.v("scalar_tensor_tensor", acc[j][0][:, db * 512:(db + 1) * 512],
                        pf[:, 0:512], gates[:, i * 32 + e:i * 32 + e + 1],
                        acc[j][0][:, db * 512:(db + 1) * 512], ALU.mult, ALU.add,
                        reads=[pfk, gk, acc[j][1]], writes=[acc[j][1]],
                        tag=f"l1:acc{e}", lowp=True)
        for j in range(8):
            i = q * 8 + j
            r = i % 2
            xt = sb.get(f"resx{r}", 128, D, F32)
            xk = xt.name
            p.load(xt[:, :], x_src[i * 128:(i + 1) * 128, :], engine="sync",
                   reads=[f"xres:{i}"], writes=[xk], tag="l1:resl")
            p.v("scalar_tensor_tensor", xt[:, :], acc[j][0][:, :], 0.5, xt[:, :],
                ALU.mult, ALU.add, reads=[acc[j][1], xk], writes=[xk],
                tag="l1:resa", lowp=True)
            p.dma("sync", lambda e, xt=xt, i=i: e.dma_start(
                out=x_dst[i * 128:(i + 1) * 128, :], in_=xt[:, :]),
                reads=[xk], writes=[f"xres:{i}"], group=xk, tag="l1:ress")


def prep_l1_ffn(inp):
    wg2 = np.zeros((256, 128), np.float32)
    wg2[:, :40] = inp["L1_ffn_wg2"]
    bg2 = np.zeros((128,), np.float32)
    bg2[:40] = inp["L1_ffn_bg2"]
    We = inp["L1_ffn_We"]
    wep = np.stack([pack_w(We[e], 1024)[0] for e in range(20)], 0)
    return {
        "wg1": np.ascontiguousarray(inp["L1_ffn_wg1"]),
        "wg2p": np.ascontiguousarray(wg2),
        "bg1c": np.ascontiguousarray(inp["L1_ffn_bg1"].reshape(-1, 128).T),
        "bg2c": np.ascontiguousarray(bg2.reshape(-1, 128).T),
        "wep": np.ascontiguousarray(wep),
        "be": np.ascontiguousarray(inp["L1_ffn_be"]),
    }



def emit_l2_ffn(ctx, x_src, x_dst, weights, tag):
    p, sb = ctx.p, ctx.sb
    NH, DH = 16, 64
    ctx.new_phase()
    identb, identb_k = ctx.ident_bf()
    identf, identf_k = ctx.ident_f32()
    onescol, onescol_k = ctx.const("onescol", np.ones((128, 1), BF16NP))
    onesrow, onesrow_k = ctx.ones_row_bf()

    z_dram = ctx.dscratch("l2_z", [T, D], F32)
    zk = f"l2z{tag}"
    qkv_dram = ctx.dscratch("l2_qkv", [24, 128, T], BF16)

    ln1t, ln1k = col_tile(ctx, "l2ln1w", weights["ln1wc"])
    ln1bt, ln1bk = col_tile(ctx, "l2ln1b", weights["ln1bc"])
    ln2t, ln2k = col_tile(ctx, "l2ln2w", weights["ln2wc"])
    ln2bt, ln2bk = col_tile(ctx, "l2ln2b", weights["ln2bc"])
    cw0, cw0k = col_tile(ctx, "l2cw0", weights["cw0c"])
    cw1, cw1k = col_tile(ctx, "l2cw1", weights["cw1c"])
    cw2, cw2k = col_tile(ctx, "l2cw2", weights["cw2c"])

    hc = ctx.pool.alloc_fm("l2hc", D, T, BF16)
    pm1 = ctx.pool.mark()
    h = ctx.pool.alloc_fm("l2h", D, T, BF16)
    ln_pass(ctx, x_src, None, out_dram=z_dram, dst_key=zk, tag=f"{tag}:lnz")
    ln_pass(ctx, z_dram, h, gamma=(ln1t, ln1k), beta=(ln1bt, ln1bk),
            src_key=zk, tag=f"{tag}:ln1")
    for j in range(D // 128):
        hck, hk = hc.key(j), h.key(j)
        hcj, hj = hc.chunks[j], h.chunks[j]
        p.v("tensor_scalar", hcj[:, 2:T], hj[:, 0:T - 2], cw0[:, j:j + 1], None,
            ALU.mult, reads=[hk, cw0k], writes=[hck], tag="l2:c0")
        p.v("scalar_tensor_tensor", hcj[:, 2:T], hj[:, 1:T - 1],
            cw1[:, j:j + 1], hcj[:, 2:T], ALU.mult, ALU.add,
            reads=[hk, cw1k, hck], writes=[hck], tag="l2:c1", lowp=True)
        p.v("scalar_tensor_tensor", hcj[:, 2:T], hj[:, 2:T],
            cw2[:, j:j + 1], hcj[:, 2:T], ALU.mult, ALU.add,
            reads=[hk, cw2k, hck], writes=[hck], tag="l2:c2", lowp=True)
        p.v("tensor_scalar", hcj[:, 0:1], hj[:, 0:1], cw2[:, j:j + 1], None,
            ALU.mult, reads=[hk, cw2k], writes=[hck], tag="l2:cb0")
        p.v("tensor_scalar", hcj[:, 1:2], hj[:, 0:1], cw1[:, j:j + 1], None,
            ALU.mult, reads=[hk, cw1k], writes=[hck], tag="l2:cb1")
        p.v("scalar_tensor_tensor", hcj[:, 1:2], hj[:, 1:2], cw2[:, j:j + 1],
            hcj[:, 1:2], ALU.mult, ALU.add, reads=[hk, cw2k, hck],
            writes=[hck], tag="l2:cb2", lowp=True)

    ctx.pool.release(pm1)
    ws_qkv = WStream(ctx, "l2aw", weights["attn_w"], 512)
    qsc, qsck = col_tile(ctx, "l2qsc", weights["qscale"])
    gemm_fm(ctx, hc, ws_qkv, ep_store_dram(ctx, qkv_dram, "l2qkv",
                                           scale_tile=qsc, scale_key=qsck),
            tag=f"{tag}:qkv")

    ctx.pool.reset()
    attn_fm = ctx.pool.alloc_fm("l2attn", D, T, BF16)
    pm_attn = ctx.pool.mark()
    qp_t, qp_k = ctx.pool.alloc_cols(T)
    kp_t, kp_k = ctx.pool.alloc_cols(T)
    vp_t, vp_k = ctx.pool.alloc_cols(T)
    vTs = [ctx.pool.alloc_cols(16 * 130) for _ in range(2)]
    cos_dram = ctx.din("w_l2cos", weights["cosp"])
    sin_dram = ctx.din("w_l2sin", weights["sinp"])
    ropwc = sb.get("ropwc", 128, 512, BF16)
    ropws = sb.get("ropws", 128, 512, BF16)
    rotP, rotP_k = ctx.const("l2rotP", weights["rotP"])
    mext, mext_k = ctx.const("l2mske", weights["maskext"])
    sqt, sqt_key = ctx.pool.alloc_cols(T)
    kst = sb.get("l2kst", 1, 8, F32)
    chb = sb.get("l2chb", 1, 2, BF16)
    chcol = [sb.get(f"l2chc{i}", 128, 1, F32) for i in range(2)]
    normr = sb.get("l2normr", 1, 512, BF16)
    rbcs = sb.get("l2rbcs", 128, 512, BF16)
    pv_ps, pv_psk = ctx.psf[4], "psf4"
    nm_ps, nm_psk = ctx.psf[5], "psf5"

    def rope_pair(dst_t, dst_k):
        for b in range(T // 512):
            c0 = b * 512
            p.load(ropwc[:, :], cos_dram[:, c0:c0 + 512], engine="sync",
                   writes=["ropwc"], tag="l2:cw")
            p.load(ropws[:, :], sin_dram[:, c0:c0 + 512], engine="sync",
                   writes=["ropws"], tag="l2:sw")
            pf, pfk = ctx.psum_f()
            p.t_matmul(pf[:, 0:512], rotP[:, :], dst_t[:, c0:c0 + 512],
                       True, True, reads=[rotP_k, dst_k], writes=[pfk],
                       tag="l2:rot")
            qc = sb.get("l2ropetmp", 128, 512, BF16)
            p.v("tensor_mul", qc[:, :], dst_t[:, c0:c0 + 512],
                ropwc[:, :], reads=[dst_k, "ropwc"], writes=[qc.name],
                tag="l2:ropec")
            p.v("tensor_mul", pf[:, 0:512], pf[:, 0:512], ropws[:, :],
                reads=[pfk, "ropws"], writes=[pfk], tag="l2:ropes")
            p.v("tensor_add", dst_t[:, c0:c0 + 512], qc[:, :], pf[:, 0:512],
                reads=[qc.name, pfk], writes=[dst_k], tag="l2:ropea", lowp=True)

    for pr in range(8):
        p.load(qp_t[:, :], qkv_dram[pr, :, :], engine="sync",
               reads=["l2qkv:%d" % pr], writes=[qp_k], tag="l2:ldq")
        p.load(kp_t[:, :], qkv_dram[8 + pr, :, :], engine="sync",
               reads=["l2qkv:%d" % (8 + pr)], writes=[kp_k], tag="l2:ldk")
        p.load(vp_t[:, :], qkv_dram[16 + pr, :, :], engine="sync",
               reads=["l2qkv:%d" % (16 + pr)], writes=[vp_k], tag="l2:ldv")
        rope_pair(qp_t, qp_k)
        rope_pair(kp_t, kp_k)
        for half in range(2):
            p.v("memset", vTs[half][0][:, :], 1.0, writes=[vTs[half][1]],
                tag="l2:vT1")
        for i in range(T // 128):
            vb, vbk = vTs[i // 16]
            base = (i % 16) * 130
            pb, pbk = ctx.psum_b()
            p.t_transpose(pb[:, 0:128], vp_t[:, i * 128:(i + 1) * 128],
                          identb[:, :], reads=[vp_k, identb_k], writes=[pbk],
                          tag="l2:vT")
            p.v("tensor_copy", vb[:, base:base + 64], pb[:, 0:64],
                reads=[pbk], writes=[vbk], tag="l2:vTc")
            p.v("tensor_copy", vb[:, base + 65:base + 129], pb[:, 64:128],
                reads=[pbk], writes=[vbk], tag="l2:vTc2")
        for hh in range(2):
            rb = hh * 64
            rb2 = hh * 65
            head = pr * 2 + hh
            kmax_parts = sb.get("l2kmx", 1, 16, F32)
            p.s("activation", sqt[rb:rb + 64, :], qp_t[rb:rb + 64, :],
                AF.Square, reads=[qp_k], writes=[sqt_key], tag="l2:sqq")
            for b in range(T // 512):
                pf, pfk = ctx.psum_f()
                p.t_matmul(pf[0:1, 0:512], onescol[rb:rb + 64, 0:1],
                           sqt[rb:rb + 64, b * 512:(b + 1) * 512], True, True,
                           reads=[onescol_k, sqt_key], writes=[pfk],
                           tag="l2:qn")
                p.v("reduce_max", kmax_parts[0:1, 8 + b:9 + b], pf[0:1, 0:512],
                    AX.X, reads=[pfk], writes=[kmax_parts.name], tag="l2:qnm")
            p.v("reduce_max", kst[0:1, 3:4], kmax_parts[0:1, 8:16], AX.X,
                reads=[kmax_parts.name], writes=[kst.name], tag="l2:qnm2")
            p.s("activation", kst[0:1, 4:5], kst[0:1, 3:4], AF.Sqrt,
                reads=[kst.name], writes=[kst.name], tag="l2:qns")
            p.s("activation", sqt[rb:rb + 64, :], kp_t[rb:rb + 64, :],
                AF.Square, reads=[kp_k], writes=[sqt_key], tag="l2:sqk")
            for b in range(T // 512):
                pf2, pfk2 = ctx.psum_f()
                p.t_matmul(pf2[0:1, 0:512], onescol[rb:rb + 64, 0:1],
                           sqt[rb:rb + 64, b * 512:(b + 1) * 512], True, True,
                           reads=[onescol_k, sqt_key], writes=[pfk2],
                           tag="l2:kn")
                p.v("reduce_max", kmax_parts[0:1, b:b + 1], pf2[0:1, 0:512],
                    AX.X, reads=[pfk2], writes=[kmax_parts.name], tag="l2:knm")
            p.v("reduce_max", kst[0:1, 0:1], kmax_parts[0:1, 0:8], AX.X,
                reads=[kmax_parts.name], writes=[kst.name], tag="l2:knm2")
            p.s("activation", kst[0:1, 1:2], kst[0:1, 0:1], AF.Sqrt,
                reads=[kst.name], writes=[kst.name], tag="l2:kns")
            p.v("tensor_scalar", kst[0:1, 5:6], kst[0:1, 4:5], kst[0:1, 1:2],
                -1.0, ALU.mult, ALU.mult, reads=[kst.name],
                writes=[kst.name], tag="l2:chm")
            p.v("tensor_copy", chb[0:1, 0:1], kst[0:1, 5:6],
                reads=[kst.name], writes=[chb.name], tag="l2:chb", lowp=True)
            ch = chcol[hh]
            pf3, pfk3 = ctx.psum_f()
            p.t_matmul(pf3[:, 0:1], onesrow[0:1, :], chb[0:1, 0:1], True,
                       True, reads=[onesrow_k, chb.name], writes=[pfk3],
                       tag="l2:chbc")
            p.v("tensor_copy", ch[:, 0:1], pf3[:, 0:1], reads=[pfk3],
                writes=[ch.name], tag="l2:chc")


        pvb = [(pv_ps, pv_psk), (nm_ps, nm_psk)]
        for qb in range(8):
            q0 = qb * 512
            nkt = 4 * (qb + 1)
            pend = []

            def flush_pv(pend=None, nkt=nkt):
                for kt_, hh_, pt_ in pend:
                    rb2 = hh_ * 65
                    pvp, pvk = pvb[hh_]
                    vb, vbk = vTs[kt_ // 16]
                    base = (kt_ % 16) * 130
                    p.t_matmul(pvp[0:65, 0:512],
                               vb[:, base + rb2:base + rb2 + 65],
                               pt_[:, :], kt_ == 0, kt_ == nkt - 1,
                               reads=[vbk, pt_.name], writes=[pvk],
                               tag="l2:pv")
                pend.clear()

            for kt in range(nkt):
                new = []
                for hh in range(2):
                    rb = hh * 64
                    ch = chcol[hh]
                    ps, psk = ctx.psum_f()
                    p.t_matmul(ps[:, 0:512],
                               kp_t[rb:rb + 64, kt * 128:(kt + 1) * 128],
                               qp_t[rb:rb + 64, q0:q0 + 512], True, True,
                               reads=[kp_k, qp_k], writes=[psk], tag="l2:sc")
                    pt = sb.get(f"l2pt{(kt % 2) * 2 + hh}", 128, 512, BF16)
                    p.s("activation", pt[:, :], ps[:, 0:512], AF.Exp,
                        bias=ch[:, 0:1], reads=[psk, ch.name],
                        writes=[pt.name], tag="l2:exp")
                    di = kt - 4 * qb
                    if di >= 0:
                        off = 384 - 128 * di
                        p.v("tensor_mul", pt[:, :], pt[:, :],
                            mext[:, off:off + 512],
                            reads=[pt.name, mext_k], writes=[pt.name],
                            tag="l2:msk")
                    new.append((kt, hh, pt))
                flush_pv(pend)
                pend.extend(new)
            flush_pv(pend)
            for hh in range(2):
                rb = hh * 64
                pvp, pvk = pvb[hh]
                nrh = sb.get("l2ropetmp", 128, 512, BF16) if hh == 0 \
                    else sb.get("l3qk", 128, 512, BF16)
                nr = nrh
                rbc = sb.get(f"l2gt{hh}", 128, 512, BF16)
                p.v("reciprocal", nr[0:1, 0:512], pvp[64:65, 0:512],
                    reads=[pvk], writes=[nr.name], tag="l2:rcp", lowp=True)
                pf, pfk = ctx.psum_f()
                p.t_matmul(pf[0:64, 0:512], onesrow[0:1, 0:64],
                           nr[0:1, 0:512], True, True,
                           reads=[onesrow_k, nr.name], writes=[pfk],
                           tag="l2:rbc")
                p.v("tensor_copy", rbc[0:64, 0:512], pf[0:64, 0:512],
                    reads=[pfk], writes=[rbc.name], tag="l2:rbcc")
                p.v("tensor_mul", attn_fm.chunks[pr][rb:rb + 64, q0:q0 + 512],
                    pvp[0:64, 0:512], rbc[0:64, 0:512],
                    reads=[pvk, rbc.name], writes=[attn_fm.key(pr)],
                    tag="l2:onrm", lowp=True)

    ctx.pool.release(pm_attn)
    ws_proj = WStream(ctx, "l2pw", weights["proj_w"], 512)
    y = ctx.pool.alloc_fm("ytt", D, 512, BF16)

    def ep_proj(ctx, ps, pskey, ocg, tt0, tw):
        p.s("copy", y.chunks[ocg][:, 0:tw], ps[:, :tw], reads=[pskey],
            writes=[y.key(ocg)], tag="l2:proj")

    for tt in range(T // 512):
        tt0 = tt * 512
        gemm_fm(ctx, attn_fm, ws_proj, ep_proj, t0=tt0, tn=tt0 + 512,
                tag=f"{tag}:proj")
        residual_add(ctx, z_dram, z_dram, y, tt0, 512, scale=1.0,
                     tag=f"{tag}:zres", res_key=zk)

    ctx.pool.reset()
    ws_rt = WStream(ctx, "l2rt", weights["router_p"], 128, nbuf=1)
    mr_t, mr_key = ctx.pool.alloc_cols(4096)
    maskb = sb.get("l2mb", 128, 1024, BF16)
    wv_dram = ctx.din("w_l2wv", weights["wvp"])
    proj_dram = ctx.din("w_l2proj", weights["projp"])
    h2 = ctx.pool.alloc_fm("l2h2", D, 1024, BF16)
    accfm = ctx.pool.alloc_fm("l2accfm", D, 1024, BF16)
    hid = ctx.pool.alloc_fm("l2hid", 4096, 1024, BF16)
    wvb = [ctx.pool.alloc_cols(2048) for _ in range(1)]
    prb = [ctx.pool.alloc_cols(4096) for _ in range(1)]
    rl_t, rl_key = ctx.pool.alloc_cols(1024)

    for q in range(4):
        q0 = q * 1024
        ln_pass(ctx, z_dram, h2, gamma=(ln2t, ln2k), beta=(ln2bt, ln2bk),
                src_key=zk, t0=q0, nt=1024, tag=f"{tag}:ln2")

        def ep_rl(ctx, ps, pskey, ocg, tt0, tw):
            p.s("copy", rl_t[:, tt0:tt0 + tw], ps[:, :tw],
                reads=[pskey], writes=[rl_key], tag="l2:rl")

        gemm_fm(ctx, h2, ws_rt, ep_rl, t0=0, tn=1024, tag=f"{tag}:rt")
        for jj in range(8):
            pb, pbk = ctx.psum_b()
            p.t_transpose(pb[:, 0:128], rl_t[:, jj * 128:(jj + 1) * 128],
                          identb[:, :], reads=[rl_key, identb_k],
                          writes=[pbk], tag="l2:rlT")
            lgT = sb.get("l2rlT", 128, 8, F32)
            p.v("tensor_copy", lgT[:, 0:4], pb[:, 0:4], reads=[pbk],
                writes=[lgT.name], tag="l2:rlTc")
            p.v("reduce_max", lgT[:, 4:5], lgT[:, 0:4], AX.X,
                reads=[lgT.name], writes=[lgT.name], tag="l2:rmx")
            p.v("tensor_scalar", lgT[:, 0:4], lgT[:, 0:4], lgT[:, 4:5], None,
                ALU.is_equal, reads=[lgT.name], writes=[lgT.name],
                tag="l2:req")
            for e in range(4):
                pg, pgk = ctx.psum_f()
                p.t_transpose(pg[0:1, 0:128], lgT[:, e:e + 1], identf[:, :],
                              reads=[lgT.name, identf_k], writes=[pgk],
                              tag="l2:mT")
                p.v("tensor_copy", mr_t[0:1, e * 1024 + jj * 128:e * 1024 + (jj + 1) * 128],
                    pg[0:1, 0:128], reads=[pgk], writes=[mr_key],
                    tag="l2:mr")

        for e in range(4):
            for hb in range(2):
                pf, pfk = ctx.psum_f()
                p.t_matmul(pf[:, 0:512], onesrow[0:1, :],
                           mr_t[0:1, e * 1024 + hb * 512:e * 1024 + (hb + 1) * 512],
                           True, True, reads=[onesrow_k, mr_key],
                           writes=[pfk], tag="l2:mb")
                p.v("tensor_copy", maskb[:, hb * 512:(hb + 1) * 512],
                    pf[:, 0:512], reads=[pfk], writes=[maskb.name],
                    tag="l2:mbc")
            for gc in range(32):
                bufap, bufkey = wvb[0]
                p.load(bufap[:, :], wv_dram[e, gc, :, :], engine="gpsimd",
                       writes=[bufkey], tag=f"l2:wv{e}")
                for th in range(2):
                    t0l = th * 512
                    psg, psgk = ctx.psum_f()
                    for k in range(8):
                        p.t_matmul(psg[:, 0:512], bufap[:, k * 256:k * 256 + 128],
                                   h2.chunks[k][:, t0l:t0l + 512],
                                   k == 0, k == 7, reads=[bufkey, h2.key(k)],
                                   writes=[psgk], tag="l2:hg")
                    psv, psvk = ctx.psum_f()
                    for k in range(8):
                        p.t_matmul(psv[:, 0:512], bufap[:, k * 256 + 128:k * 256 + 256],
                                   h2.chunks[k][:, t0l:t0l + 512],
                                   k == 0, k == 7, reads=[bufkey, h2.key(k)],
                                   writes=[psvk], tag="l2:hv")
                    gt = sb.get(f"l2gt{th}", 128, 512, BF16)
                    p.s("activation", gt[:, :], psg[:, 0:512], AF.Silu,
                        reads=[psgk], writes=[gt.name], tag="l2:hsilu")
                    p.v("tensor_mul", gt[:, :], gt[:, :], psv[:, 0:512],
                        reads=[gt.name, psvk], writes=[gt.name], tag="l2:hmul", lowp=True)
                    p.v("tensor_mul", hid.chunks[gc][:, t0l:t0l + 512],
                        gt[:, :], maskb[:, t0l:t0l + 512],
                        reads=[gt.name, maskb.name], writes=[hid.key(gc)],
                        tag="l2:hmsk", lowp=True)
            for ob in range(8):
                prab, prk = prb[0]
                p.load(prab[:, :], proj_dram[e, ob, :, :], engine="gpsimd",
                       writes=[prk], tag=f"l2:pr{e}")
                for th in range(2):
                    t0l = th * 512
                    pf, pfk = ctx.psum_f()
                    for k in range(32):
                        p.t_matmul(pf[:, 0:512], prab[:, k * 128:(k + 1) * 128],
                                   hid.chunks[k][:, t0l:t0l + 512],
                                   k == 0, k == 31, reads=[prk, hid.key(k)],
                                   writes=[pfk], tag="l2:prm")
                    if e == 0:
                        p.v("tensor_copy", accfm.chunks[ob][:, t0l:t0l + 512],
                            pf[:, 0:512], reads=[pfk], writes=[accfm.key(ob)],
                            tag="l2:acc0")
                    else:
                        p.v("tensor_add", accfm.chunks[ob][:, t0l:t0l + 512],
                            accfm.chunks[ob][:, t0l:t0l + 512], pf[:, 0:512],
                            reads=[accfm.key(ob), pfk], writes=[accfm.key(ob)],
                            tag="l2:acca", lowp=True)

        for jj in range(8):
            i = q * 8 + jj
            r = i % 2
            xt = sb.get(f"resx{r}", 128, D, F32)
            zt = sb.get(f"lnx{r}", 128, D, F32)
            p.load(xt[:, :], x_src[i * 128:(i + 1) * 128, :], engine="sync",
                   reads=[f"xres:{i}"], writes=[xt.name], tag="l2:cx")
            p.load(zt[:, :], z_dram[i * 128:(i + 1) * 128, :], engine="sync",
                   reads=[f"{zk}:{i}"], writes=[zt.name], tag="l2:cz")
            p.v("scalar_tensor_tensor", xt[:, :], zt[:, :], 0.5, xt[:, :],
                ALU.mult, ALU.add, reads=[zt.name, xt.name], writes=[xt.name],
                tag="l2:cza")
            for j in range(8):
                pf, pfk = ctx.psum_b()
                p.t_transpose(pf[:, 0:128],
                              accfm.chunks[j][:, jj * 128:(jj + 1) * 128],
                              identb[:, :], reads=[accfm.key(j), identb_k],
                              writes=[pfk], tag="l2:accT")
                p.v("scalar_tensor_tensor", xt[:, j * 128:(j + 1) * 128],
                    pf[:, 0:128], 0.5, xt[:, j * 128:(j + 1) * 128],
                    ALU.mult, ALU.add, reads=[pfk, xt.name], writes=[xt.name],
                    tag="l2:cacc")
            p.dma("sync", lambda e2, xt=xt, i=i: e2.dma_start(
                out=x_dst[i * 128:(i + 1) * 128, :], in_=xt[:, :]),
                reads=[xt.name], writes=[f"xres:{i}"], group=xt.name,
                tag="l2:cst")


def weights_negrow(ctx):
    t, k = ctx.const("negrow", np.full((1, 128), -1.0, BF16NP))
    return t[0:1, :]


def prep_l2_ffn(inp):
    C, NH, DH = 1024, 16, 64
    cw = inp["L2_ffn_conv_w"][:, 0, :]
    inv_freq = 1.0 / 10000.0 ** (np.arange(0, DH, 2) / DH)
    tpos = np.arange(T)
    fr = tpos[:, None] * inv_freq
    emb = np.concatenate([fr, fr], -1)
    cos = np.cos(emb).T.astype(np.float32)
    sin = np.sin(emb).T.astype(np.float32)
    cosp = np.concatenate([cos, cos], 0)
    sinp = np.concatenate([sin, sin], 0)
    sc = 1.0 / np.sqrt(DH)
    P = np.zeros((128, 128), np.float32)
    for base in (0, 64):
        for j in range(32):
            P[base + 32 + j, base + j] = -1.0
            P[base + j, base + 32 + j] = 1.0
    kl = np.arange(128)[:, None]
    cl = np.arange(896)[None, :]
    d = {"maskext": (kl <= cl - 384).astype(BF16NP)}
    wv = inp["L2_ffn_expert_wv"]
    wvp = np.zeros((4, 32, 128, 2048), np.float32)
    for e in range(4):
        for gc in range(32):
            pair = np.concatenate([wv[e][:, gc * 128:(gc + 1) * 128],
                                   wv[e][:, 4096 + gc * 128:4096 + (gc + 1) * 128]],
                                  1)
            wvp[e, gc] = pack_w(pair, 256)[0]
    router = np.zeros((1024, 128), np.float32)
    router[:, :4] = inp["L2_ffn_router_w"]
    out = {
        "ln1wc": np.ascontiguousarray(inp["L2_ffn_ln1_w"].reshape(-1, 128).T),
        "ln1bc": np.ascontiguousarray(inp["L2_ffn_ln1_b"].reshape(-1, 128).T),
        "ln2wc": np.ascontiguousarray(inp["L2_ffn_ln2_w"].reshape(-1, 128).T),
        "ln2bc": np.ascontiguousarray(inp["L2_ffn_ln2_b"].reshape(-1, 128).T),
        "cw0c": np.ascontiguousarray(cw[:, 0].reshape(-1, 128).T),
        "cw1c": np.ascontiguousarray(cw[:, 1].reshape(-1, 128).T),
        "cw2c": np.ascontiguousarray(cw[:, 2].reshape(-1, 128).T),
        "attn_w": np.ascontiguousarray(inp["L2_ffn_attn_w"]),
        "proj_w": np.ascontiguousarray(inp["L2_ffn_proj_w"]),
        "router_p": router,
        "cosp": cosp.astype(BF16NP),
        "sinp": sinp.astype(BF16NP),
        "rotP": P.astype(BF16NP),
        "wvp": wvp,
        "qscale": np.ascontiguousarray(
            np.concatenate([np.full(1024, sc), np.ones(2048)])
            .astype(np.float32).reshape(-1, 128).T),
        "projp": np.stack([pack_w(inp["L2_ffn_expert_proj"][e], 128)
                           for e in range(4)], 0),
    }
    out.update(d)
    return out



def emit_l3_attn(ctx, x_src, x_dst, weights, tag):
    p, sb = ctx.p, ctx.sb
    ctx.new_phase()
    identb, identb_k = ctx.ident_bf()
    identf, identf_k = ctx.ident_f32()
    onesrow, onesrow_k = ctx.ones_row_bf()
    TH = T // 2

    qs_dram = ctx.dscratch("l3a_qs", [4, 128, T], BF16)

    ln_bf = ctx.pool.alloc_fm("l3a_ln", D, T, BF16)
    ln_pass(ctx, x_src, ln_bf, tag=f"{tag}:ln")

    ws_q = WStream(ctx, "l3aq", weights["WqT"], 512)
    ws_k = WStream(ctx, "l3ak", weights["WkT"], 512, share=ws_q)
    ws_v = WStream(ctx, "l3av", weights["WvT"], 512, share=ws_q)
    bqt, bqk = col_tile(ctx, "l3abq", weights["bqc"])
    bkt, bkk = col_tile(ctx, "l3abk", weights["bkc"])
    bvt, bvk = col_tile(ctx, "l3abv", weights["bvc"])
    alphat, alphak = col_tile(ctx, "l3aal", weights["alphac"])
    augt, augk = col_tile(ctx, "l3aau", weights["augc"])
    ind = [ctx.const(f"l3hind{c}", weights[f"hind{c}"]) for c in range(4)]
    ind2 = [ctx.const(f"l3hbc{c}", weights[f"hbc{c}"]) for c in range(4)]

    dec = sb.get("l3dec", 128, 4, F32)
    dm1 = sb.get("l3dm1", 128, 4, F32)
    p.s("activation", dec[:, :], alphat[:, :], AF.Sigmoid, reads=[alphak],
        writes=[dec.name], tag="l3a:sig")
    p.v("tensor_scalar", dm1[:, :], dec[:, :], -1.0, 1.0, ALU.mult, ALU.add,
        reads=[dec.name], writes=[dm1.name], tag="l3a:dm1")

    attnT = ctx.pool.alloc_cols(T)
    attnT_t, attnT_k = attnT
    scanh = sb.get("l3shand", 128, 8, F32)
    p.v("memset", scanh[:, :], 0.0, writes=[scanh.name], tag="l3a:sh0")
    pmh = ctx.pool.mark()

    qk = sb.get("l3qk", 128, 512, BF16)
    _sf = sb.get("spillf0", 128, 512, F32)
    class _SfmView:
        name = _sf.name
        def __getitem__(self, idx):
            r, c = idx
            assert r == slice(None)
            return _sf[0:16, c]
    sfm = _SfmView()
    stt = sb.get("l3stok", 128, 16, F32)
    stt2 = sb.get("l3stok2", 128, 8, F32)

    def mk_ep(fm, bt, bk, loc0):
        def ep(ctx, ps, pskey, ocg, tt0, tw):
            p.v("tensor_scalar", fm.chunks[ocg - loc0][:, tt0:tt0 + tw],
                ps[:, :tw], bt[:, ocg:ocg + 1], None, ALU.add,
                reads=[pskey, bk], writes=[fm.key(ocg - loc0)], tag="l3a:qkv")
        return ep

    for hf in range(2):
        h0 = hf * TH
        ctx.pool.release(pmh)
        qfm = ctx.pool.alloc_fm("l3a_q", 512, TH, BF16)
        kfm = ctx.pool.alloc_fm("l3a_k", 512, TH, BF16)

        gemm_fm(ctx, ln_bf, ws_q, mk_ep(qfm, bqt, bqk, 0), t0=h0,
                tn=h0 + TH, toff=-h0, tag=f"{tag}:q")
        gemm_fm(ctx, ln_bf, ws_k, mk_ep(kfm, bkt, bkk, 0), t0=h0,
                tn=h0 + TH, toff=-h0, tag=f"{tag}:k")

        for tb in range(TH // 512):
            b0 = tb * 512
            ps, psk = ctx.psum_f()
            for c in range(4):
                p.v("tensor_mul", qk[:, :], qfm.chunks[c][:, b0:b0 + 512],
                    kfm.chunks[c][:, b0:b0 + 512],
                    reads=[qfm.key(c), kfm.key(c)], writes=[qk.name],
                    tag="l3a:qk")
                p.t_matmul(ps[0:16, 0:512], ind[c][0][:, :], qk[:, :],
                           c == 0, c == 3, reads=[ind[c][1], qk.name],
                           writes=[psk], tag="l3a:ssum")
            p.v("tensor_scalar", sfm[:, 0:512], ps[0:16, 0:512], 0.125, None,
                ALU.mult, reads=[psk], writes=[sfm.name], tag="l3a:ssc")
            for j in range(4):
                pf, pfk = ctx.psum_f()
                p.t_transpose(pf[0:128, 0:16], sfm[:, j * 128:(j + 1) * 128],
                              identf[0:16, 0:16], reads=[sfm.name, identf_k],
                              writes=[pfk], tag="l3a:sT")
                p.v("tensor_copy", stt[:, 0:16], pf[0:128, 0:16], reads=[pfk],
                    writes=[stt.name], tag="l3a:sTc")
                p.v("reduce_max", stt2[:, 0:1], stt[:, 0:8], AX.X,
                    reads=[stt.name], writes=[stt2.name], tag="l3a:smx")
                p.v("tensor_scalar", stt[:, 0:8], stt[:, 0:8], stt2[:, 0:1],
                    None, ALU.subtract, reads=[stt.name, stt2.name],
                    writes=[stt.name], tag="l3a:ssub")
                p.s("activation", stt[:, 0:8], stt[:, 0:8], AF.Exp,
                    accum_out=stt2[:, 1:2], reads=[stt.name],
                    writes=[stt.name, stt2.name], tag="l3a:sexp")
                p.v("reciprocal", stt2[:, 2:3], stt2[:, 1:2],
                    reads=[stt2.name], writes=[stt2.name], tag="l3a:srcp")
                p.v("tensor_scalar", stt[:, 0:8], stt[:, 0:8], stt2[:, 2:3],
                    None, ALU.mult, reads=[stt.name, stt2.name],
                    writes=[stt.name], tag="l3a:snrm", lowp=True)
                pb, pbk = ctx.psum_f()
                p.t_transpose(pb[0:16, 0:128], stt[:, 0:16], identf[:, :],
                              reads=[stt.name, identf_k], writes=[pbk],
                              tag="l3a:sTb")
                p.v("tensor_copy",
                    attnT_t[0:16, h0 + b0 + j * 128:h0 + b0 + (j + 1) * 128],
                    pb[0:16, 0:128], reads=[pbk], writes=[attnT_k],
                    tag="l3a:aTc")

        ctx.pool.release(pmh)
        qfm2 = ctx.pool.alloc_fm("l3a_q", 512, TH, BF16)
        vfm = ctx.pool.alloc_fm("l3a_v", 512, TH, BF16)
        gemm_fm(ctx, ln_bf, ws_v, mk_ep(vfm, bvt, bvk, 0), t0=h0,
                tn=h0 + TH, toff=-h0, tag=f"{tag}:v")
        for c in range(4):
            for tb in range(TH // 512):
                b0 = tb * 512
                pf, pfk = ctx.psum_f()
                p.t_matmul(pf[:, 0:512], ind2[c][0][:, :],
                           attnT_t[0:16, h0 + b0:h0 + b0 + 512], True, True,
                           reads=[ind2[c][1], attnT_k], writes=[pfk],
                           tag="l3a:abc")
                p.v("scalar_tensor_tensor", vfm.chunks[c][:, b0:b0 + 512],
                    pf[:, 0:512], dm1[:, c:c + 1], vfm.chunks[c][:, b0:b0 + 512],
                    ALU.mult, ALU.mult, reads=[pfk, dm1.name, vfm.key(c)],
                    writes=[vfm.key(c)], tag="l3a:u", lowp=True)
            init = 0.0 if hf == 0 else scanh[:, c:c + 1]
            rds = [dec.name, vfm.key(c)] + ([] if hf == 0 else [scanh.name])
            p.v("tensor_tensor_scan", vfm.chunks[c][:, 0:TH],
                dec[:, c:c + 1].broadcast_to((128, TH)),
                vfm.chunks[c][:, 0:TH], init, ALU.mult, ALU.add, reads=rds,
                writes=[vfm.key(c)], tag="l3a:scan", lowp=True)
            if hf == 0:
                p.v("tensor_copy", scanh[:, c:c + 1],
                    vfm.chunks[c][:, TH - 1:TH], reads=[vfm.key(c)],
                    writes=[scanh.name], tag="l3a:shc")
            for tb in range(TH // 512):
                b0 = tb * 512
                p.s("activation", qk[:, :], qfm2.chunks[c][:, b0:b0 + 512],
                    AF.Sigmoid, scale=augt[:, c:c + 1],
                    reads=[qfm2.key(c), augk], writes=[qk.name], tag="l3a:aug")
                p.v("tensor_mul", qfm2.chunks[c][:, b0:b0 + 512],
                    qfm2.chunks[c][:, b0:b0 + 512],
                    vfm.chunks[c][:, b0:b0 + 512],
                    reads=[qfm2.key(c), vfm.key(c)], writes=[qfm2.key(c)],
                    tag="l3a:qs", lowp=True)
                p.v("tensor_add", qfm2.chunks[c][:, b0:b0 + 512],
                    qfm2.chunks[c][:, b0:b0 + 512], qk[:, :],
                    reads=[qfm2.key(c), qk.name], writes=[qfm2.key(c)],
                    tag="l3a:qsa", lowp=True)
            p.dma("sync", lambda e, c=c, hf=hf, ap=qfm2.chunks[c]: e.dma_start(
                out=qs_dram[c, :, hf * TH:(hf + 1) * TH], in_=ap[:, 0:TH]),
                reads=[qfm2.key(c)], writes=[f"l3a_qs:{c}:{hf}"],
                group=qfm2.key(c), tag="l3a:qst")

    ctx.pool.release(pmh)
    gbp, gbp_k = ctx.pool.alloc_cols(2 * D)
    lrow = sb.get("spill0", 128, 512, BF16)
    lnsc = ctx.din("w_l3alns", weights["ln_scale_row"])
    lnbi = ctx.din("w_l3alnb", weights["ln_bias_row"])
    for src, off in ((lnsc, 0), (lnbi, D)):
        for hb in range(2):
            p.load(lrow[0:1, :], src[0:1, hb * 512:(hb + 1) * 512],
                   engine="gpsimd", writes=[lrow.name], tag="l3a:lr")
            pf, pfk = ctx.psum_f()
            p.t_matmul(pf[:, 0:512], onesrow[0:1, :], lrow[0:1, :], True,
                       True, reads=[onesrow_k, lrow.name], writes=[pfk],
                       tag="l3a:gb")
            p.v("tensor_copy", gbp[:, off + hb * 512:off + (hb + 1) * 512],
                pf[:, 0:512], reads=[pfk], writes=[gbp_k], tag="l3a:gbc",
                lowp=True)
    gbc = gbp[:, 0:D]
    bbc = gbp[:, D:2 * D]

    ws_o = WStream(ctx, "l3ao", weights["WoT"], 512, share=ws_q)
    bot, bok = col_tile(ctx, "l3abo", weights["boc"])
    y = ctx.pool.alloc_fm("ytt", D, 512, BF16)
    qb2 = [ctx.pool.alloc_fm(f"l3a_qb{i}", 512, 512, BF16) for i in range(2)]

    def ep_o(ctx, ps, pskey, ocg, tt0, tw):
        p.v("tensor_scalar", y.chunks[ocg][:, 0:tw], ps[:, :tw],
            bot[:, ocg:ocg + 1], None, ALU.add, reads=[pskey, bok],
            writes=[y.key(ocg)], tag="l3a:o")

    for tt in range(T // 512):
        tt0 = tt * 512
        qb = qb2[tt % 2]
        for c in range(4):
            p.load(qb.chunks[c][:, 0:512], qs_dram[c, :, tt0:tt0 + 512],
                   engine="sync", reads=[f"l3a_qs:{c}:{tt0 // TH}"],
                   writes=[qb.key(c)], tag="l3a:qbl")
        gemm_fm(ctx, qb, ws_o, ep_o, t0=0, tn=512, toff=tt0, tag=f"{tag}:wo")
        for ii in range(4):
            i = tt * 4 + ii
            r = i % 2
            yt = sb.get(f"resx{r}", 128, D, F32)
            st = sb.get(f"lnst{r}", 128, 8, F32)
            xt = sb.get(f"lnx{r}", 128, D, F32)
            for j in range(8):
                pb, pbk = ctx.psum_b()
                p.t_transpose(pb[:, 0:128],
                              y.chunks[j][:, ii * 128:(ii + 1) * 128],
                              identb[:, :], reads=[y.key(j), identb_k],
                              writes=[pbk], tag="l3a:yT")
                p.v("tensor_copy", yt[:, j * 128:(j + 1) * 128], pb[:, 0:128],
                    reads=[pbk], writes=[yt.name], tag="l3a:yTc")
            p.v("tensor_reduce", st[:, 0:1], yt[:, :], AX.X, ALU.add,
                reads=[yt.name], writes=[st.name], tag="l3a:lsum")
            p.v("tensor_scalar", st[:, 0:1], st[:, 0:1], 1.0 / D, None,
                ALU.mult, reads=[st.name], writes=[st.name], tag="l3a:lm")
            p.v("tensor_scalar", yt[:, :], yt[:, :], st[:, 0:1], None,
                ALU.subtract, reads=[yt.name, st.name], writes=[yt.name],
                tag="l3a:lsub")
            sq = sb.get(f"lnt{r}", 128, D, BF16)
            p.s("activation", sq[:, :], yt[:, :], AF.Square,
                accum_out=st[:, 1:2], reads=[yt.name],
                writes=[sq.name, st.name], tag="l3a:lsq")
            p.v("tensor_scalar", st[:, 2:3], st[:, 1:2], 1.0 / D, 1e-6,
                ALU.mult, ALU.add, reads=[st.name], writes=[st.name],
                tag="l3a:lvar")
            p.s("activation", st[:, 3:4], st[:, 2:3], AF.Sqrt,
                reads=[st.name], writes=[st.name], tag="l3a:lsd")
            p.v("reciprocal", st[:, 4:5], st[:, 3:4], reads=[st.name],
                writes=[st.name], tag="l3a:lrs")
            p.v("tensor_scalar", yt[:, :], yt[:, :], st[:, 4:5], None,
                ALU.mult, reads=[yt.name, st.name], writes=[yt.name],
                tag="l3a:lnorm")
            p.v("tensor_mul", yt[:, :], yt[:, :], gbc[:, :],
                reads=[yt.name, gbp_k], writes=[yt.name], tag="l3a:lgam")
            p.v("tensor_add", yt[:, :], yt[:, :], bbc[:, :],
                reads=[yt.name, gbp_k], writes=[yt.name], tag="l3a:lbet")
            p.load(xt[:, :], x_src[i * 128:(i + 1) * 128, :], engine="sync",
                   reads=[f"xres:{i}"], writes=[xt.name], tag="l3a:rx")
            p.v("tensor_add", xt[:, :], xt[:, :], yt[:, :],
                reads=[xt.name, yt.name], writes=[xt.name], tag="l3a:racc")
            p.dma("sync", lambda e, xt=xt, i=i: e.dma_start(
                out=x_dst[i * 128:(i + 1) * 128, :], in_=xt[:, :]),
                reads=[xt.name], writes=[f"xres:{i}"], group=xt.name,
                tag="l3a:rs")


def prep_l3_attn(inp):
    hind = {}
    for c in range(4):
        m = np.zeros((128, 16), BF16NP)
        for j in range(2):
            m[j * 64:(j + 1) * 64, c * 2 + j] = 1
        hind[f"hind{c}"] = m
        m2 = np.zeros((16, 128), BF16NP)
        for j in range(2):
            m2[c * 2 + j, j * 64:(j + 1) * 64] = 1
        hind[f"hbc{c}"] = m2
    out = {
        "WqT": np.ascontiguousarray(inp["L3_attn_Wq"]),
        "WkT": np.ascontiguousarray(inp["L3_attn_Wk"]),
        "WvT": np.ascontiguousarray(inp["L3_attn_Wv"]),
        "WoT": np.ascontiguousarray(inp["L3_attn_Wo"]),
        "bqc": np.ascontiguousarray(inp["L3_attn_bq"].reshape(-1, 128).T),
        "bkc": np.ascontiguousarray(inp["L3_attn_bk"].reshape(-1, 128).T),
        "bvc": np.ascontiguousarray(inp["L3_attn_bv"].reshape(-1, 128).T),
        "boc": np.ascontiguousarray(inp["L3_attn_bo"].reshape(-1, 128).T),
        "alphac": np.ascontiguousarray(
            inp["L3_attn_alpha"].reshape(-1).reshape(-1, 128).T),
        "augc": np.ascontiguousarray(
            inp["L3_attn_aug_param"].reshape(-1).reshape(-1, 128).T),
        "ln_scale_row": inp["L3_attn_ln_scale"][None, :].astype(BF16NP),
        "ln_bias_row": inp["L3_attn_ln_bias"][None, :].astype(BF16NP),
    }
    out.update(hind)
    return out



def ep_store_tok(ctx, asm, asm_keys, identb, identb_k, nchunk):
    p, sb = ctx.p, ctx.sb

    def ep(ctx, ps, pskey, ocg, tt0, tw):
        r = ocg % 2
        st = sb.get(f"spill{r}", 128, 512, BF16)
        p.s("copy", st[:, :tw], ps[:, :tw], reads=[pskey], writes=[st.name],
            tag="est:cp")
        for u in range(tw // 128):
            pb, pbk = ctx.psum_b()
            p.t_transpose(pb[:, 0:128], st[:, u * 128:(u + 1) * 128],
                          identb[:, :], reads=[st.name, identb_k],
                          writes=[pbk], tag="est:tr")
            p.v("tensor_copy", asm[u][:, ocg * 128:(ocg + 1) * 128],
                pb[:, 0:128], reads=[pbk], writes=[asm_keys[u]], tag="est:ac")
    return ep


def emit_l1_attn(ctx, x_src, x_dst, weights, tag):
    p, sb = ctx.p, ctx.sb
    ctx.new_phase()
    identb, identb_k = ctx.ident_bf()
    onescol, onescol_k = ctx.const("onescol", np.ones((128, 1), BF16NP))
    onesrow, onesrow_k = ctx.ones_row_bf()
    negrow = weights_negrow(ctx)

    q_dram = ctx.dscratch("l1a_q", [T, D], BF16)
    k_dram = ctx.dscratch("l1a_k", [T, D], BF16)
    v_dram = ctx.dscratch("l1a_v", [T, D], BF16)
    b1_dram = ctx.dscratch("l1a_b1", [D, T], BF16)
    b2_dram = ctx.dscratch("l1a_b2", [T, D], BF16)

    ln_bf = ctx.pool.alloc_fm("l1a_ln", D, T, BF16)
    ln_pass(ctx, x_src, ln_bf, tag=f"{tag}:ln")
    ws_q = WStream(ctx, "l1aq", weights["WqT"], 512)
    ws_k = WStream(ctx, "l1ak", weights["WkT"], 512, share=ws_q)
    ws_v = WStream(ctx, "l1av", weights["WvT"], 512, share=ws_q)
    asm = [sb.get(f"l1asm{u}", 128, D, BF16) for u in range(4)]
    asm_keys = [a.name for a in asm]

    for tt in range(T // 512):
        tt0 = tt * 512
        for ws, dram, dname in ((ws_q, q_dram, "l1aq"), (ws_k, k_dram, "l1ak"),
                                (ws_v, v_dram, "l1av")):
            gemm_fm(ctx, ln_bf, ws, ep_store_tok(ctx, asm, asm_keys, identb,
                                                 identb_k, 8),
                    t0=tt0, tn=tt0 + 512, tag=f"{tag}:{dname}")
            for u in range(4):
                i = tt * 4 + u
                p.dma("sync", lambda e, a=asm[u], i=i, dram=dram: e.dma_start(
                    out=dram[i * 128:(i + 1) * 128, :], in_=a[:, :]),
                    reads=[asm_keys[u]], writes=[f"{dname}:{i}"],
                    group=asm_keys[u], tag=f"{tag}:st{dname}")

    ctx.pool.reset()
    attn_fm = ctx.pool.alloc_fm("l1a_attn", D, T, BF16)
    qtok, qtok_k = ctx.pool.alloc_cols(T)
    ktok, ktok_k = ctx.pool.alloc_cols(T)
    vTs = [ctx.pool.alloc_cols(16 * 130) for _ in range(2)]
    qp_t, qp_k = ctx.pool.alloc_cols(T)
    kp_t, kp_k = ctx.pool.alloc_cols(T)
    sqt, sqt_key = ctx.pool.alloc_cols(T)
    chb = sb.get("l2chb", 1, 2, BF16)
    chcol = [sb.get(f"l2chc{i}", 128, 1, F32) for i in range(2)]
    cos_dram = ctx.din("w_l1cos", weights["cosp"])
    sin_dram = ctx.din("w_l1sin", weights["sinp"])
    ropwc = sb.get("ropwc", 128, 512, BF16)
    ropws = sb.get("ropws", 128, 512, BF16)
    rotP, rotP_k = ctx.const("l1rotP", weights["rotP"])
    kst = sb.get("l2kst", 1, 8, F32)
    normr = sb.get("l2normr", 1, 512, BF16)
    rbcs = sb.get("l2rbcs", 128, 512, BF16)
    pv_ps, pv_psk = ctx.psf[4], "psf4"
    nm_ps, nm_psk = ctx.psf[5], "psf5"

    def load_head_pair(dst, dst_key, dram, dname, pr):
        for hh in range(2):
            h = pr * 2 + hh
            src = dram[h * 256:(h + 1) * 256, :].rearrange(
                "r (m d) -> (r m) d", d=64).rearrange("(i p) d -> p i d", p=128)
            dst_ap = dst.rearrange("p (i d) -> p i d", d=128)[
                :, :, hh * 64:(hh + 1) * 64]
            p.dma("sync", lambda e, dst_ap=dst_ap, src=src: e.dma_start(
                out=dst_ap, in_=src),
                reads=[f"{dname}:{2 * h}", f"{dname}:{2 * h + 1}"],
                writes=[dst_key], group=dst_key, tag=f"{tag}:ldh")

    def load_v_pair(dram, dname, pr):
        for half in range(2):
            p.v("memset", vTs[half][0][:, :], 1.0, writes=[vTs[half][1]],
                tag="l1:v1")
        for hh in range(2):
            h = pr * 2 + hh
            srcf = dram[h * 256:(h + 1) * 256, :].rearrange(
                "r (m d) -> (r m) d", d=64).rearrange("(i p) d -> p i d", p=128)
            for half in range(2):
                vb, vbk = vTs[half]
                src = srcf[:, half * 16:(half + 1) * 16, :]
                dst_ap = vb.rearrange("p (i d) -> p i d", d=130)[
                    :, :, hh * 65:hh * 65 + 64]
                p.dma("sync", lambda e, dst_ap=dst_ap, src=src: e.dma_start(
                    out=dst_ap, in_=src),
                    reads=[f"{dname}:{2 * h}", f"{dname}:{2 * h + 1}"],
                    writes=[vbk], group=vbk, tag=f"{tag}:ldv")

    def rope_pair(dst_t, dst_k):
        for b in range(T // 512):
            c0 = b * 512
            p.load(ropwc[:, :], cos_dram[:, c0:c0 + 512], engine="sync",
                   writes=["ropwc"], tag="l1:cw")
            p.load(ropws[:, :], sin_dram[:, c0:c0 + 512], engine="sync",
                   writes=["ropws"], tag="l1:sw")
            pf, pfk = ctx.psum_f()
            p.t_matmul(pf[:, 0:512], rotP[:, :], dst_t[:, c0:c0 + 512],
                       True, True, reads=[rotP_k, dst_k], writes=[pfk],
                       tag="l1:rot")
            qc = sb.get("l2ropetmp", 128, 512, BF16)
            p.v("tensor_mul", qc[:, :], dst_t[:, c0:c0 + 512],
                ropwc[:, :], reads=[dst_k, "ropwc"], writes=[qc.name],
                tag="l1:ropec")
            p.v("tensor_mul", pf[:, 0:512], pf[:, 0:512], ropws[:, :],
                reads=[pfk, "ropws"], writes=[pfk], tag="l1:ropes")
            p.v("tensor_add", dst_t[:, c0:c0 + 512], qc[:, :], pf[:, 0:512],
                reads=[qc.name, pfk], writes=[dst_k], tag="l1:ropea",
                lowp=True)

    for pr in range(8):
        load_head_pair(qtok, qtok_k, q_dram, "l1aq", pr)
        load_head_pair(ktok, ktok_k, k_dram, "l1ak", pr)
        load_v_pair(v_dram, "l1av", pr)
        for i in range(T // 128):
            for src_t, src_k, dst_t, dst_k in ((qtok, qtok_k, qp_t, qp_k),
                                               (ktok, ktok_k, kp_t, kp_k)):
                pb, pbk = ctx.psum_b()
                p.t_transpose(pb[:, 0:128], src_t[:, i * 128:(i + 1) * 128],
                              identb[:, :], reads=[src_k, identb_k],
                              writes=[pbk], tag="l1:qT")
                p.v("tensor_copy", dst_t[:, i * 128:(i + 1) * 128],
                    pb[:, 0:128], reads=[pbk], writes=[dst_k], tag="l1:qTc")
        rope_pair(qp_t, qp_k)
        rope_pair(kp_t, kp_k)
        p.v("tensor_scalar", qp_t[:, :], qp_t[:, :], 0.25, None, ALU.mult,
            reads=[qp_k], writes=[qp_k], tag="l1:qsc", lowp=True)
        for hh in range(2):
            rb = hh * 64
            rb2 = hh * 65
            kmax_parts = sb.get("l2kmx", 1, 16, F32)
            p.s("activation", sqt[rb:rb + 64, :], qp_t[rb:rb + 64, :],
                AF.Square, reads=[qp_k], writes=[sqt_key], tag="l1:sqq")
            for b in range(T // 512):
                pf, pfk = ctx.psum_f()
                p.t_matmul(pf[0:1, 0:512], onescol[rb:rb + 64, 0:1],
                           sqt[rb:rb + 64, b * 512:(b + 1) * 512], True, True,
                           reads=[onescol_k, sqt_key], writes=[pfk],
                           tag="l1:qn")
                p.v("reduce_max", kmax_parts[0:1, 8 + b:9 + b], pf[0:1, 0:512],
                    AX.X, reads=[pfk], writes=[kmax_parts.name], tag="l1:qnm")
            p.v("reduce_max", kst[0:1, 3:4], kmax_parts[0:1, 8:16], AX.X,
                reads=[kmax_parts.name], writes=[kst.name], tag="l1:qnm2")
            p.s("activation", kst[0:1, 4:5], kst[0:1, 3:4], AF.Sqrt,
                reads=[kst.name], writes=[kst.name], tag="l1:qns")
            p.s("activation", sqt[rb:rb + 64, :], kp_t[rb:rb + 64, :],
                AF.Square, reads=[kp_k], writes=[sqt_key], tag="l1:sqk")
            for b in range(T // 512):
                pf2, pfk2 = ctx.psum_f()
                p.t_matmul(pf2[0:1, 0:512], onescol[rb:rb + 64, 0:1],
                           sqt[rb:rb + 64, b * 512:(b + 1) * 512], True, True,
                           reads=[onescol_k, sqt_key], writes=[pfk2],
                           tag="l1:kn")
                p.v("reduce_max", kmax_parts[0:1, b:b + 1], pf2[0:1, 0:512],
                    AX.X, reads=[pfk2], writes=[kmax_parts.name], tag="l1:knm")
            p.v("reduce_max", kst[0:1, 0:1], kmax_parts[0:1, 0:8], AX.X,
                reads=[kmax_parts.name], writes=[kst.name], tag="l1:knm2")
            p.s("activation", kst[0:1, 1:2], kst[0:1, 0:1], AF.Sqrt,
                reads=[kst.name], writes=[kst.name], tag="l1:kns")
            p.v("tensor_scalar", kst[0:1, 5:6], kst[0:1, 4:5], kst[0:1, 1:2],
                -1.0, ALU.mult, ALU.mult, reads=[kst.name],
                writes=[kst.name], tag="l1:chm")
            p.v("tensor_copy", chb[0:1, 0:1], kst[0:1, 5:6],
                reads=[kst.name], writes=[chb.name], tag="l1:chb", lowp=True)
            ch = chcol[hh]
            pf3, pfk3 = ctx.psum_f()
            p.t_matmul(pf3[:, 0:1], onesrow[0:1, :], chb[0:1, 0:1], True,
                       True, reads=[onesrow_k, chb.name], writes=[pfk3],
                       tag="l1:chbc")
            p.v("tensor_copy", ch[:, 0:1], pf3[:, 0:1], reads=[pfk3],
                writes=[ch.name], tag="l1:chc")


        pvb = [(pv_ps, pv_psk), (nm_ps, nm_psk)]
        for qb in range(8):
            q0 = qb * 512
            pend = []

            def flush_pv(pend=None):
                for kt_, hh_, pt_ in pend:
                    rb2 = hh_ * 65
                    pvp, pvk = pvb[hh_]
                    vb, vbk = vTs[kt_ // 16]
                    base = (kt_ % 16) * 130
                    p.t_matmul(pvp[0:65, 0:512],
                               vb[:, base + rb2:base + rb2 + 65],
                               pt_[:, :], kt_ == 0, kt_ == 31,
                               reads=[vbk, pt_.name], writes=[pvk],
                               tag="l1:pv")
                pend.clear()

            for kt in range(32):
                new = []
                for hh in range(2):
                    rb = hh * 64
                    ch = chcol[hh]
                    ps, psk = ctx.psum_f()
                    p.t_matmul(ps[:, 0:512],
                               kp_t[rb:rb + 64, kt * 128:(kt + 1) * 128],
                               qp_t[rb:rb + 64, q0:q0 + 512], True, True,
                               reads=[kp_k, qp_k], writes=[psk], tag="l1:sc")
                    pt = sb.get(f"l2pt{(kt % 2) * 2 + hh}", 128, 512, BF16)
                    p.s("activation", pt[:, :], ps[:, 0:512], AF.Exp,
                        bias=ch[:, 0:1], reads=[psk, ch.name],
                        writes=[pt.name], tag="l1:exp")
                    new.append((kt, hh, pt))
                flush_pv(pend)
                pend.extend(new)
            flush_pv(pend)
            for hh in range(2):
                rb = hh * 64
                pvp, pvk = pvb[hh]
                nrh = sb.get("l2ropetmp", 128, 512, BF16) if hh == 0 \
                    else sb.get("l3qk", 128, 512, BF16)
                nr = nrh
                rbc = sb.get(f"l2gt{hh}", 128, 512, BF16)
                p.v("reciprocal", nr[0:1, 0:512], pvp[64:65, 0:512],
                    reads=[pvk], writes=[nr.name], tag="l1:rcp", lowp=True)
                pf, pfk = ctx.psum_f()
                p.t_matmul(pf[0:64, 0:512], onesrow[0:1, 0:64],
                           nr[0:1, 0:512], True, True,
                           reads=[onesrow_k, nr.name], writes=[pfk],
                           tag="l1:rbc")
                p.v("tensor_copy", rbc[0:64, 0:512], pf[0:64, 0:512],
                    reads=[pfk], writes=[rbc.name], tag="l1:rbcc")
                p.v("tensor_mul", attn_fm.chunks[pr][rb:rb + 64, q0:q0 + 512],
                    pvp[0:64, 0:512], rbc[0:64, 0:512],
                    reads=[pvk, rbc.name], writes=[attn_fm.key(pr)],
                    tag="l1:onrm", lowp=True)

    for c in range(8):
        p.dma("sync", lambda e, c=c: e.dma_start(
            out=b1_dram[c * 128:(c + 1) * 128, :], in_=attn_fm.chunks[c][:, :]),
            reads=[attn_fm.key(c)], writes=[f"l1b1:{c}"],
            group="l1b1st", tag=f"{tag}:b1")

    ctx.pool.reset()
    newfm = ctx.pool.alloc_fm("l1a_new", D, T, BF16)
    b1v = b1_dram.rearrange("a b -> (a b)").rearrange("(t d) -> t d", d=D)
    for i in range(T // 128):
        r = i % 2
        nt = sb.get(f"lnt{r}", 128, D, BF16)
        p.load(nt[:, :], b1v[i * 128:(i + 1) * 128, :], engine="sync",
               reads=[f"l1b1:{i // 4}"], writes=[nt.name],
               tag=f"{tag}:ldnew")
        for j in range(8):
            pb, pbk = ctx.psum_b()
            p.t_transpose(pb[:, 0:128], nt[:, j * 128:(j + 1) * 128],
                          identb[:, :], reads=[nt.name, identb_k],
                          writes=[pbk], tag="l1:nT")
            p.v("tensor_copy", newfm.chunks[j][:, i * 128:(i + 1) * 128],
                pb[:, 0:128], reads=[pbk], writes=[newfm.key(j)],
                tag="l1:nTc")

    ws_r = WStream(ctx, "l1ar", weights["WrT"], 512)
    for tt in range(T // 512):
        tt0 = tt * 512
        gemm_fm(ctx, newfm, ws_r, ep_store_tok(ctx, asm, asm_keys, identb,
                                               identb_k, 8),
                t0=tt0, tn=tt0 + 512, tag=f"{tag}:wr")
        for u in range(4):
            i = tt * 4 + u
            p.dma("sync", lambda e, a=asm[u], i=i: e.dma_start(
                out=b2_dram[i * 128:(i + 1) * 128, :], in_=a[:, :]),
                reads=[asm_keys[u]], writes=[f"l1b2:{i}"],
                group=asm_keys[u], tag=f"{tag}:stb2")

    b2v = b2_dram.rearrange("a b -> (a b)").rearrange("(d t) -> d t", t=T)
    y = ctx.pool.alloc_fm("ytt", D, 512, BF16)
    for tt in range(T // 512):
        tt0 = tt * 512
        for c in range(8):
            p.load(y.chunks[c][:, 0:512], b2v[c * 128:(c + 1) * 128, tt0:tt0 + 512],
                   engine="sync",
                   reads=[f"l1b2:{c * 4 + j}" for j in range(4)],
                   writes=[y.key(c)], tag=f"{tag}:ldy")
        residual_add(ctx, x_src, x_dst, y, tt0, 512, scale=1.0,
                     tag=f"{tag}:res")


def prep_l1_attn(inp):
    DH, ROT = 64, 32
    inv_freq = 1.0 / 10000.0 ** (np.arange(0, ROT, 2) / ROT)
    tpos = np.arange(T)
    fr = tpos[:, None] * inv_freq
    fr2 = np.repeat(fr, 2, axis=-1)
    cos = np.ones((T, DH))
    sin = np.zeros((T, DH))
    cos[:, :ROT] = np.cos(fr2)
    sin[:, :ROT] = np.sin(fr2)
    cosp = np.concatenate([cos.T, cos.T], 0).astype(BF16NP)
    sinp = np.concatenate([sin.T, sin.T], 0).astype(BF16NP)
    P = np.zeros((128, 128), np.float32)
    for base in (0, 64):
        for i2 in range(16):
            P[base + 2 * i2 + 1, base + 2 * i2] = -1.0
            P[base + 2 * i2, base + 2 * i2 + 1] = 1.0
    return {
        "WqT": np.ascontiguousarray(inp["L1_attn_Wq"].T),
        "WkT": np.ascontiguousarray(inp["L1_attn_Wk"].T),
        "WvT": np.ascontiguousarray(inp["L1_attn_Wv"].T),
        "WrT": np.ascontiguousarray(inp["L1_attn_Wr"].T),
        "cosp": cosp,
        "sinp": sinp,
        "rotP": P.astype(BF16NP),
    }



def emit_l0_attn(ctx, x_src, x_dst, weights, tag):
    p, sb = ctx.p, ctx.sb
    NH, DK, DV, C = 4, 128, 256, 32
    NCHK = T // C
    ctx.new_phase()
    identb, identb_k = ctx.ident_bf()
    identf, identf_k = ctx.ident_f32()
    onesrow, onesrow_k = ctx.ones_row_bf()

    r_dram = ctx.dscratch("l0_r", [4, 128, T], BF16)
    k_dram = ctx.dscratch("l0_k", [4, 128, T], BF16)
    v_dram = ctx.dscratch("l0_v", [8, 128, T], BF16)
    w_dram = ctx.dscratch("l0_w", [4, 128, T], F32)
    g_dram = ctx.dscratch("l0_g", [T, D], BF16)
    o_dram = ctx.dscratch("l0_o", [8, 128, T], BF16)
    o2_dram = ctx.dscratch("l0_o2", [8, 128, T], BF16)

    mux, muxk = col_tile(ctx, "l0mux", weights["muxc"])
    xbias, xbiask = col_tile(ctx, "l0xb", weights["xbiasc"])
    bw2t, bw2k = col_tile(ctx, "l0bw2", weights["bw2c"])
    bonust, bonusk = col_tile(ctx, "l0bonus", weights["bonusc"])

    ws_x1 = WStream(ctx, "l0wx1", weights["Wx1T"], 128)
    ws_mus = WStream(ctx, "l0wmus", weights["Wx2T"], 512)
    ws_r = WStream(ctx, "l0wr", weights["WrT"], 512)
    ws_w1 = WStream(ctx, "l0ww1", weights["Ww1T"], 128, share=ws_r)
    ws_w2 = WStream(ctx, "l0ww2", weights["Ww2T"], 512, share=ws_r)
    ws_k = WStream(ctx, "l0wk", weights["WkT"], 512, share=ws_r)
    ws_v = WStream(ctx, "l0wv", weights["WvT"], 512, share=ws_r)
    ws_g = WStream(ctx, "l0wg2", weights["WgT"], 512, share=ws_r)

    pm0 = ctx.pool.mark()
    TB = 1024
    prev = sb.get("l0prev", 128, 8, F32)
    p.v("memset", prev[:, :], 0.0, writes=[prev.name], tag="l0a:pz")
    asm = [sb.get(f"l1asm{u}", 128, D, BF16) for u in range(4)]
    asm_keys = [a.name for a in asm]

    for tb in range(T // TB):
        tb0 = tb * TB
        ctx.pool.release(pm0)
        lnf = ctx.pool.alloc_fm("l0a_ln", D, TB, F32)
        ln_pass(ctx, x_src, None, out_f32=lnf, t0=tb0, nt=TB, tag=f"{tag}:ln")
        delta = ctx.pool.alloc_fm("l0a_dl", D, TB, F32)
        for j in range(8):
            p.v("tensor_sub", delta.chunks[j][:, 1:TB], lnf.chunks[j][:, 0:TB - 1],
                lnf.chunks[j][:, 1:TB], reads=[lnf.key(j)],
                writes=[delta.key(j)], tag="l0a:dl")
            p.v("tensor_sub", delta.chunks[j][:, 0:1], prev[:, j:j + 1],
                lnf.chunks[j][:, 0:1], reads=[prev.name, lnf.key(j)],
                writes=[delta.key(j)], tag="l0a:dl0")
            p.v("tensor_copy", prev[:, j:j + 1], lnf.chunks[j][:, TB - 1:TB],
                reads=[lnf.key(j)], writes=[prev.name], tag="l0a:pcp")
        xp = ctx.pool.alloc_fm("l0a_xp", 256, TB, BF16)
        pmxa = ctx.pool.mark()
        xa = ctx.pool.alloc_fm("l0a_xa", D, TB, BF16)
        for j in range(8):
            p.v("scalar_tensor_tensor", xa.chunks[j][:, :], delta.chunks[j][:, :],
                mux[:, j:j + 1], lnf.chunks[j][:, :], ALU.mult, ALU.add,
                reads=[delta.key(j), muxk, lnf.key(j)], writes=[xa.key(j)],
                tag="l0a:xax", lowp=True)

        def ep_xp(ctx, ps, pskey, ocg, tt0l, tw):
            p.s("activation", xp.chunks[ocg][:, tt0l - tb0:tt0l - tb0 + tw],
                ps[:, :tw], AF.Tanh, reads=[pskey], writes=[xp.key(ocg)],
                tag="l0a:tanh")
        gemm_fm(ctx, xa, ws_x1, ep_xp, t0=0, tn=TB, toff=tb0, tag=f"{tag}:x1")

        ctx.pool.release(pmxa)
        pmn = ctx.pool.mark()
        for n, (wsn, dram, dname, nch, store_tok) in enumerate((
                (ws_r, r_dram, "l0_r", 4, False),
                (ws_w1, None, None, 1, False),
                (ws_k, k_dram, "l0_k", 4, False),
                (ws_v, v_dram, "l0_v", 8, False),
                (ws_g, g_dram, "l0_g", 8, True))):
            ctx.pool.release(pmn)
            mu = ctx.pool.alloc_fm("l0a_mu", D, TB, BF16)

            def ep_mu(ctx, ps, pskey, ocg, tt0l, tw, n=n):
                loc = ocg - n * 8
                p.v("tensor_scalar", mu.chunks[loc][:, tt0l - tb0:tt0l - tb0 + tw],
                    ps[:, :tw], xbias[:, n * 8 + loc:n * 8 + loc + 1], None,
                    ALU.add, reads=[pskey, xbiask], writes=[mu.key(loc)],
                    tag="l0a:mu")
            gemm_fm(ctx, xp, ws_mus, ep_mu, t0=0, tn=TB, toff=tb0,
                    blocks=list(range(n * 2, n * 2 + 2)), tag=f"{tag}:mus")
            xan = ctx.pool.alloc_fm("l0a_xan", D, TB, BF16)
            for j in range(8):
                p.v("tensor_mul", xan.chunks[j][:, :], delta.chunks[j][:, :],
                    mu.chunks[j][:, :], reads=[delta.key(j), mu.key(j)],
                    writes=[xan.key(j)], tag="l0a:xm", lowp=True)
                p.v("tensor_add", xan.chunks[j][:, :], xan.chunks[j][:, :],
                    lnf.chunks[j][:, :], reads=[xan.key(j), lnf.key(j)],
                    writes=[xan.key(j)], tag="l0a:xa2", lowp=True)
            if n == 1:
                w64 = ctx.pool.alloc_fm("l0a_w64", 128, TB, BF16)

                def ep_w64(ctx, ps, pskey, ocg, tt0l, tw):
                    p.s("activation", w64.chunks[0][:, tt0l - tb0:tt0l - tb0 + tw],
                        ps[:, :tw], AF.Tanh, reads=[pskey],
                        writes=[w64.key(0)], tag="l0a:w64")
                gemm_fm(ctx, xan, ws_w1, ep_w64, t0=0, tn=TB, toff=tb0,
                        tag=f"{tag}:w1")
                gemm_fm(ctx, w64, ws_w2,
                        ep_store_dram(ctx, w_dram, "l0_w",
                                      scale_tile=None, func=None,
                                      dst_dtype=F32, bias_tile=bw2t,
                                      bias_key=bw2k),
                        t0=0, tn=TB, toff=tb0, tag=f"{tag}:w2")
            elif store_tok:
                for sub in range(TB // 512):
                    gemm_fm(ctx, xan, wsn, ep_store_tok(ctx, asm, asm_keys,
                                                        identb, identb_k, 8),
                            t0=sub * 512, tn=(sub + 1) * 512, toff=tb0,
                            tag=f"{tag}:g")
                    for u in range(4):
                        i = (tb0 + sub * 512) // 128 + u
                        p.dma("sync", lambda e, a=asm[u], i=i: e.dma_start(
                            out=g_dram[i * 128:(i + 1) * 128, :], in_=a[:, :]),
                            reads=[asm_keys[u]], writes=[f"l0_g:{i}"],
                            group=asm_keys[u], tag=f"{tag}:gst")
            else:
                gemm_fm(ctx, xan, wsn,
                        ep_store_dram(ctx, dram, dname),
                        t0=0, tn=TB, toff=tb0, tag=f"{tag}:p{n}")

    ctx.pool.reset()
    ohead = [ctx.pool.alloc_cols(T) for _ in range(2)]
    pm2 = ctx.pool.mark()
    S_r = [sb.get("l0S", 128, DV, BF16), sb.get("l0S2", 128, DV, BF16)]
    NR = 3
    Am_r = [sb.get(f"l0Am{i}", 32, 32, BF16) for i in range(NR)]
    Amt_r = [sb.get(f"l0Amt{i}", 32, 32, BF16) for i in range(NR)]
    vt_r = [sb.get(f"l0vt{i}", 32, DV, BF16) for i in range(NR)]
    khat_r = [sb.get(f"l0khat{i}", 128, C, BF16) for i in range(NR)]
    dcol_r = [sb.get(f"l0dcol{i}", 32, 2, F32) for i in range(NR)]
    otok_r = [sb.get(f"l0otok{i}", 32, DV, BF16) for i in range(NR)]
    rk_r = [sb.get(f"l0rk{i}", 128, C, BF16) for i in range(NR)]
    kht_r = [sb.get(f"l0khT{i}", 32, 128, BF16) for i in range(NR)]
    emb_blk = sb.get("l0em", 128, 512, F32)
    maskt, maskt_k = ctx.const("l0mask", weights["trimask"])

    for h in range(NH):
        ctx.pool.release(pm2)
        rt, rt_k = ctx.pool.alloc_cols(T)
        kt, kt_k = ctx.pool.alloc_cols(T)
        cumh = [ctx.pool.alloc_cols(T) for _ in range(2)]
        e3h = [ctx.pool.alloc_cols(T) for _ in range(2)]
        rtl_t, rtl_k = ctx.pool.alloc_cols(T)
        ktl_t, ktl_k = ctx.pool.alloc_cols(T)
        vt_fm0, vt_fm0k = ctx.pool.alloc_cols(T)
        vt_fm1, vt_fm1k = ctx.pool.alloc_cols(T)
        TH = T // 2
        NCH2 = NCHK // 2
        p.load(rt[:, :], r_dram[h, :, :], engine="sync", reads=[f"l0_r:{h}"],
               writes=[rt_k], tag="l0a:ldr")
        p.load(kt[:, :], k_dram[h, :, :], engine="sync", reads=[f"l0_k:{h}"],
               writes=[kt_k], tag="l0a:ldk")
        p.load(vt_fm0[:, :], v_dram[2 * h, :, :], engine="sync",
               reads=[f"l0_v:{2 * h}"], writes=[vt_fm0k], tag="l0a:ldv")
        p.load(vt_fm1[:, :], v_dram[2 * h + 1, :, :], engine="sync",
               reads=[f"l0_v:{2 * h + 1}"], writes=[vt_fm1k], tag="l0a:ldv2")
        hand = sb.get("l0hand", 128, 2, F32)
        for hf in range(2):
            cum = cumh[hf][0].bitcast(F32)
            cum_k = cumh[hf][1]
            p.load(cum[:, :], w_dram[h, :, hf * TH:(hf + 1) * TH],
                   engine="sync", reads=[f"l0_w:{h}"], writes=[cum_k],
                   tag="l0a:ldw")
            p.s("activation", cum[:, :], cum[:, :], AF.Exp, reads=[cum_k],
                writes=[cum_k], tag="l0a:wexp")
            p.v("tensor_scalar", cum[:, :], cum[:, :], -1.0, None, ALU.mult,
                reads=[cum_k], writes=[cum_k], tag="l0a:wneg")
            init = 0.0 if hf == 0 else hand[:, 0:1]
            rds = [cum_k, "cs_onescol_f"] + ([] if hf == 0 else [hand.name])
            p.v("tensor_tensor_scan", cum[:, :], _ones_bcast(ctx, TH),
                cum[:, :], init, ALU.mult, ALU.add, reads=rds,
                writes=[cum_k], tag="l0a:cums")
            p.v("tensor_copy", hand[:, hf:hf + 1], cum[:, TH - 1:TH],
                reads=[cum_k], writes=[hand.name], tag="l0a:hand")
            cview = cum.rearrange("p (c j) -> p c j", j=C)
            p.v("tensor_tensor", cview[:, 1:NCH2, :], cview[:, 1:NCH2, :],
                cview[:, 0:NCH2 - 1, C - 1:C].broadcast_to((128, NCH2 - 1, C)),
                ALU.subtract, reads=[cum_k], writes=[cum_k], tag="l0a:seg")
            if hf == 1:
                p.v("tensor_tensor", cview[:, 0:1, :], cview[:, 0:1, :],
                    hand[:, 0:1].unsqueeze(1).broadcast_to((128, 1, C)),
                    ALU.subtract,
                    reads=[cum_k, hand.name], writes=[cum_k], tag="l0a:seg0")
            e3 = e3h[hf][0].bitcast(F32)
            e3_k = e3h[hf][1]
            p.s("activation", e3[:, :], cum[:, :], AF.Exp, reads=[cum_k],
                writes=[e3_k], tag="l0a:e3")
            rview = rt[:, hf * TH:(hf + 1) * TH].rearrange(
                "p (c j) -> p c j", j=C)
            rtlview = rtl_t[:, hf * TH:(hf + 1) * TH].rearrange(
                "p (c j) -> p c j", j=C)
            e3view = e3.rearrange("p (c j) -> p c j", j=C)
            p.v("tensor_tensor", rtlview[:, :, 1:C], rview[:, :, 1:C],
                e3view[:, :, 0:C - 1], ALU.mult, reads=[rt_k, e3_k],
                writes=[rtl_k], tag="l0a:rtl", lowp=True)
            p.v("tensor_copy", rtlview[:, :, 0:1], rview[:, :, 0:1],
                reads=[rt_k], writes=[rtl_k], tag="l0a:rtl0")
            for b in range(TH // 512):
                p.s("activation", emb_blk[:, :], cum[:, b * 512:(b + 1) * 512],
                    AF.Exp, scale=-1.0, reads=[cum_k], writes=[emb_blk.name],
                    tag="l0a:em")
                p.v("tensor_mul", ktl_t[:, hf * TH + b * 512:hf * TH + (b + 1) * 512],
                    kt[:, hf * TH + b * 512:hf * TH + (b + 1) * 512],
                    emb_blk[:, :], reads=[kt_k, emb_blk.name],
                    writes=[ktl_k], tag="l0a:ktl", lowp=True)
        p.v("memset", S_r[0][:, :], 0.0, writes=[S_r[0].name], tag="l0a:Sz")
        p.v("memset", S_r[1][:, :], 0.0, writes=[S_r[1].name], tag="l0a:Sz2")
        for c in range(NCHK):
            c0 = c * C
            Am = Am_r[c % NR]
            Amt = Amt_r[c % NR]
            vt_t = vt_r[c % NR]
            khat = khat_r[c % NR]
            dcol = dcol_r[c % NR]
            otok = otok_r[c % NR]
            rk = rk_r[c % NR]
            kht = kht_r[c % NR]
            Scur, Sprev = S_r[c % 2], S_r[(c + 1) % 2]
            e3c = e3h[c // NCH2][0].bitcast(F32)
            e3c_k = e3h[c // NCH2][1]
            cl0 = (c % NCH2) * C
            p.v("tensor_mul", rk[:, :], rt[:, c0:c0 + C], kt[:, c0:c0 + C],
                reads=[rt_k, kt_k], writes=[rk.name], tag="l0a:rk", lowp=True)
            pD, pDk = ctx.psum_f()
            p.t_matmul(pD[0:C, 0:1], rk[:, :], bonust[:, h:h + 1], True, True,
                       reads=[rk.name, bonusk], writes=[pDk], tag="l0a:Dmm")
            p.s("copy", dcol[:, 0:1], pD[0:C, 0:1], reads=[pDk],
                writes=[dcol.name], tag="l0a:Dcp")
            pA, pAk = ctx.psum_f()
            p.t_matmul(pA[0:C, 0:C], rtl_t[:, c0:c0 + C], ktl_t[:, c0:c0 + C],
                       True, True, reads=[rtl_k, ktl_k], writes=[pAk],
                       tag="l0a:Amm")
            p.v("tensor_mul", Am[:, :], pA[0:C, 0:C], maskt[:, :],
                reads=[pAk, maskt_k], writes=[Am.name], tag="l0a:Amsk",
                lowp=True)
            p.v("transpose", Amt[:, :], Am[:, :], reads=[Am.name],
                writes=[Amt.name], tag="l0a:AT")
            for half in range(2):
                vv = vt_fm0 if half == 0 else vt_fm1
                vvk = vt_fm0k if half == 0 else vt_fm1k
                pb, pbk = ctx.psum_b()
                p.t_transpose(pb[0:C, 0:128], vv[:, c0:c0 + C], identb[:, :],
                              reads=[vvk, identb_k], writes=[pbk],
                              tag="l0a:vT")
                p.s("copy", vt_t[:, half * 128:(half + 1) * 128],
                    pb[0:C, 0:128], reads=[pbk], writes=[vt_t.name],
                    tag="l0a:vTc")
            p.v("tensor_tensor", khat[:, :], ktl_t[:, c0:c0 + C],
                e3c[:, cl0 + C - 1:cl0 + C].broadcast_to((128, C)), ALU.mult,
                reads=[ktl_k, e3c_k], writes=[khat.name], tag="l0a:khat",
                lowp=True)
            pb2, pb2k = ctx.psum_b()
            p.t_transpose(pb2[0:C, 0:128], khat[:, :], identb[:, :],
                          reads=[khat.name, identb_k], writes=[pb2k],
                          tag="l0a:khT")
            p.s("copy", kht[:, :], pb2[0:C, 0:128], reads=[pb2k],
                writes=[kht.name], tag="l0a:khTc")
            pU, pUk = ctx.psum_f()
            p.t_matmul(pU[:, 0:DV], kht[:, :], vt_t[:, :], True, True,
                       reads=[kht.name, vt_t.name], writes=[pUk],
                       tag="l0a:Umm")
            pO, pOk = ctx.psum_f()
            p.t_matmul(pO[0:C, 0:DV], Amt[:, :], vt_t[:, :], True, False,
                       reads=[Amt.name, vt_t.name], writes=[pOk],
                       tag="l0a:intra")
            p.t_matmul(pO[0:C, 0:DV], rtl_t[:, c0:c0 + C], Sprev[:, :], False,
                       True, reads=[rtl_k, Sprev.name], writes=[pOk],
                       tag="l0a:inter")
            p.v("scalar_tensor_tensor", otok[:, :], vt_t[:, :], dcol[:, 0:1],
                pO[0:C, 0:DV], ALU.mult, ALU.add,
                reads=[vt_t.name, dcol.name, pOk], writes=[otok.name],
                tag="l0a:oeps", lowp=True)
            p.v("scalar_tensor_tensor", Scur[:, :], Sprev[:, :],
                e3c[:, cl0 + C - 1:cl0 + C], pU[:, 0:DV], ALU.mult, ALU.add,
                reads=[Sprev.name, e3c_k, pUk], writes=[Scur.name],
                tag="l0a:Supd", lowp=True)
            for half in range(2):
                pb3, pb3k = ctx.psum_b()
                p.t_transpose(pb3[0:128, 0:C],
                              otok[:, half * 128:(half + 1) * 128],
                              identb[0:C, 0:C], reads=[otok.name, identb_k],
                              writes=[pb3k], tag="l0a:oT")
                p.s("copy",
                    ohead[half][0][:, c0:c0 + C], pb3[0:128, 0:C],
                    reads=[pb3k], writes=[ohead[half][1]],
                    tag="l0a:oTc")
        for half in range(2):
            p.dma("sync", lambda e, h=h, half=half, ap=ohead[half][0]:
                  e.dma_start(out=o_dram[2 * h + half, :, :], in_=ap[:, :]),
                  reads=[ohead[half][1]], writes=[f"l0_oA:{2 * h + half}"],
                  group=ohead[half][1], tag="l0a:ost")

    lnw_bc = sb.get("l3gbc", 128, D, F32)
    lnb_bc = sb.get("l3bbc", 128, D, F32)
    p.load(asm[0][0:1, :], ctx.din("w_l0lnw", weights["lnw_row"]),
           engine="gpsimd", writes=[asm_keys[0]], tag="l0a:lnw")
    p.load(asm[1][0:1, :], ctx.din("w_l0lnb", weights["lnb_row"]),
           engine="gpsimd", writes=[asm_keys[1]], tag="l0a:lnb")
    for hb in range(2):
        for dst, rowt, rowk in ((lnw_bc, asm[0], asm_keys[0]),
                                (lnb_bc, asm[1], asm_keys[1])):
            pf, pfk = ctx.psum_f()
            p.t_matmul(pf[:, 0:512], onesrow[0:1, :],
                       rowt[0:1, hb * 512:(hb + 1) * 512], True,
                       True, reads=[onesrow_k, rowk], writes=[pfk],
                       tag="l0a:lbc")
            p.v("tensor_copy", dst[:, hb * 512:(hb + 1) * 512], pf[:, 0:512],
                reads=[pfk], writes=[dst.name], tag="l0a:lbcc")

    ock = [ctx.pool.alloc_cols(1024) for _ in range(2)]
    for i in range(T // 128):
        r = i % 2
        ot = sb.get(f"resx{r}", 128, D, F32)
        gt = sb.get(f"lnx{r}", 128, D, F32)
        st = sb.get(f"lnst{r}", 128, 8, F32)
        sg = sb.get(f"l0sg{r}", 128, 16, F32)
        oca, oca_k = ock[i % 2]
        srco = o_dram[0:8, :, i * 128:(i + 1) * 128].rearrange("j p t -> p j t")
        p.dma("sync", lambda e, oca=oca, srco=srco: e.dma_start(
            out=oca.rearrange("p (j t) -> p j t", t=128), in_=srco),
            reads=[f"l0_oA:{j}" for j in range(8)], writes=[oca_k],
            group=oca_k, tag="l0a:old")
        for j in range(8):
            pb, pbk = ctx.psum_b()
            p.t_transpose(pb[:, 0:128], oca[:, j * 128:(j + 1) * 128],
                          identb[:, :], reads=[oca_k, identb_k],
                          writes=[pbk], tag="l0a:gT")
            p.v("tensor_copy", ot[:, j * 128:(j + 1) * 128], pb[:, 0:128],
                reads=[pbk], writes=[ot.name], tag="l0a:gTc")
        otv = ot.rearrange("p (h v) -> p h v", v=DV)
        p.v("tensor_reduce", sg[:, 0:4].rearrange("p (h o) -> p h o", o=1),
            otv, AX.X, ALU.add, reads=[ot.name], writes=[sg.name],
            tag="l0a:gm")
        p.v("tensor_scalar", sg[:, 0:4], sg[:, 0:4], 1.0 / DV, None, ALU.mult,
            reads=[sg.name], writes=[sg.name], tag="l0a:gms")
        p.v("tensor_tensor", otv, otv,
            sg[:, 0:4].rearrange("p (h o) -> p h o", o=1).broadcast_to((128, 4, DV)),
            ALU.subtract, reads=[ot.name, sg.name], writes=[ot.name],
            tag="l0a:gsub")
        p.s("activation", gt[:, :], ot[:, :], AF.Square, reads=[ot.name],
            writes=[gt.name], tag="l0a:gsq")
        p.v("tensor_reduce", sg[:, 4:8].rearrange("p (h o) -> p h o", o=1),
            gt.rearrange("p (h v) -> p h v", v=DV), AX.X, ALU.add,
            reads=[gt.name], writes=[sg.name], tag="l0a:gv")
        p.v("tensor_scalar", sg[:, 4:8], sg[:, 4:8], 1.0 / DV, 1e-5,
            ALU.mult, ALU.add, reads=[sg.name], writes=[sg.name],
            tag="l0a:gvs")
        p.s("activation", sg[:, 8:12], sg[:, 4:8], AF.Sqrt, reads=[sg.name],
            writes=[sg.name], tag="l0a:gsd")
        p.v("reciprocal", sg[:, 12:16], sg[:, 8:12], reads=[sg.name],
            writes=[sg.name], tag="l0a:grs")
        p.v("tensor_tensor", otv, otv,
            sg[:, 12:16].rearrange("p (h o) -> p h o", o=1).broadcast_to((128, 4, DV)),
            ALU.mult, reads=[ot.name, sg.name], writes=[ot.name],
            tag="l0a:gnm")
        p.v("tensor_mul", ot[:, :], ot[:, :], lnw_bc[:, :],
            reads=[ot.name, lnw_bc.name], writes=[ot.name], tag="l0a:glw")
        p.v("tensor_add", ot[:, :], ot[:, :], lnb_bc[:, :],
            reads=[ot.name, lnb_bc.name], writes=[ot.name], tag="l0a:glb")
        p.load(gt[:, :].bitcast(BF16)[:, 0:D], g_dram[i * 128:(i + 1) * 128, :],
               engine="sync", reads=[f"l0_g:{i}"], writes=[gt.name],
               tag="l0a:gld")
        p.s("activation", gt[:, :].bitcast(BF16)[:, D:2 * D],
            gt[:, :].bitcast(BF16)[:, 0:D], AF.Silu, reads=[gt.name],
            writes=[gt.name], tag="l0a:gsilu")
        p.v("tensor_mul", ot[:, :], ot[:, :], gt[:, :].bitcast(BF16)[:, D:2 * D],
            reads=[ot.name, gt.name], writes=[ot.name], tag="l0a:gmul")
        lt = sb.get(f"lnt{r}", 128, D, BF16)
        for j in range(8):
            p.v("tensor_copy", lt[:, j * 128:(j + 1) * 128],
                ot[:, j * 128:(j + 1) * 128], reads=[ot.name],
                writes=[lt.name], tag="l0a:ocast")
            pb, pbk = ctx.psum_b()
            p.t_transpose(pb[:, 0:128], lt[:, j * 128:(j + 1) * 128],
                          identb[:, :], reads=[lt.name, identb_k],
                          writes=[pbk], tag="l0a:obT")
            p.v("tensor_copy", oca[:, j * 128:(j + 1) * 128],
                pb[:, 0:128], reads=[pbk], writes=[oca_k],
                tag="l0a:obTc")
        dsto = o2_dram[0:8, :, i * 128:(i + 1) * 128].rearrange("j p t -> p j t")
        p.dma("sync", lambda e, oca=oca, dsto=dsto: e.dma_start(
            out=dsto, in_=oca.rearrange("p (j t) -> p j t", t=128)),
            reads=[oca_k], writes=[f"l0_oB:{i}"], group=oca_k,
            tag="l0a:o2st")

    ctx.pool.release(pm2)
    ws_o = WStream(ctx, "l0wo", weights["WoT"], 512)
    y = ctx.pool.alloc_fm("ytt", D, 512, BF16)
    ob2 = [ctx.pool.alloc_fm(f"l0a_ob{i}", D, 512, BF16) for i in range(2)]

    def ep_o(ctx, ps, pskey, ocg, tt0, tw):
        p.s("copy", y.chunks[ocg][:, 0:tw], ps[:, :tw], reads=[pskey],
            writes=[y.key(ocg)], tag="l0a:wo")

    for tt in range(T // 512):
        tt0 = tt * 512
        ob = ob2[tt % 2]
        for j in range(8):
            p.load(ob.chunks[j][:, 0:512], o2_dram[j, :, tt0:tt0 + 512],
                   engine="sync",
                   reads=[f"l0_oB:{ii}" for ii in range(tt * 4, tt * 4 + 4)],
                   writes=[ob.key(j)], tag="l0a:obl")
        gemm_fm(ctx, ob, ws_o, ep_o, t0=0, tn=512, toff=tt0, tag=f"{tag}:wo")
        residual_add(ctx, x_src, x_dst, y, tt0, 512, scale=1.0,
                     tag=f"{tag}:res")


def _ones_bcast(ctx, n, parts=128):
    t, k = ctx.const("onescol_f", np.ones((128, 1), np.float32))
    return t[0:parts, 0:1].broadcast_to((parts, n))


def prep_l0_attn(inp):
    H = 1024
    Wx1 = inp["L0_attn_Wx1"]
    Wx1p = np.zeros((256, H), np.float32)
    Wx1p[:160] = Wx1
    Wx2 = inp["L0_attn_Wx2"].reshape(H, 5, 32)
    W2 = np.zeros((256, 5 * H), np.float32)
    for n in range(5):
        W2[n * 32:(n + 1) * 32, n * H:(n + 1) * H] = Wx2[:, n, :].T
    Ww1 = inp["L0_attn_Ww1"]
    Ww1p = np.zeros((128, H), np.float32)
    Ww1p[:64] = Ww1
    Ww2 = inp["L0_attn_Ww2"]
    Ww2p = np.zeros((128, 512), np.float32)
    Ww2p[:64] = Ww2.T
    trimask = np.tril(np.ones((32, 32)), -1).astype(BF16NP)
    xbias = inp["L0_attn_x_bias"]
    xb = np.zeros((128, 5 * 8), np.float32)
    for n in range(5):
        xb[:, n * 8:(n + 1) * 8] = xbias[n].reshape(8, 128).T
    return {
        "muxc": np.ascontiguousarray(inp["L0_attn_mu_x"].reshape(-1, 128).T),
        "xbiasc": np.ascontiguousarray(xb),
        "bw2c": np.ascontiguousarray(inp["L0_attn_bw2"].reshape(-1, 128).T),
        "bonusc": np.ascontiguousarray(inp["L0_attn_bonus"].T.astype(BF16NP)),
        "Wx1T": np.ascontiguousarray(Wx1p.T),
        "Wx2T": np.ascontiguousarray(W2),
        "WrT": np.ascontiguousarray(inp["L0_attn_Wr"].T),
        "Ww1T": np.ascontiguousarray(Ww1p.T),
        "Ww2T": np.ascontiguousarray(Ww2p),
        "WkT": np.ascontiguousarray(inp["L0_attn_Wk"].T),
        "WvT": np.ascontiguousarray(inp["L0_attn_Wv"].T),
        "WgT": np.ascontiguousarray(inp["L0_attn_Wg"].T),
        "WoT": np.ascontiguousarray(inp["L0_attn_Wo"].T),
        "trimask": trimask,
        "lnw_row": np.tile(inp["L0_attn_ln_w"], 4)[None, :].astype(BF16NP),
        "lnb_row": np.tile(inp["L0_attn_ln_b"], 4)[None, :].astype(BF16NP),
    }



def emit_l2_attn(ctx, x_src, x_dst, weights, tag):
    p, sb = ctx.p, ctx.sb
    NH, HD, DS, Q = 32, 64, 128, 128
    NC = T // Q
    DIN = 2048
    ctx.new_phase()
    identb, identb_k = ctx.ident_bf()
    identf, identf_k = ctx.ident_f32()
    onesrow, onesrow_k = ctx.ones_row_bf()

    z_dram = ctx.dscratch("l2a_z", [16, 128, T], BF16)
    xbc_dram = ctx.dscratch("l2a_xbc", [18, 128, T], BF16)
    xc_dram = ctx.dscratch("l2a_xc", [18, 128, T], BF16)
    dt_dram = ctx.dscratch("l2a_dt", [128, T], F32)
    y_dram = ctx.dscratch("l2a_y", [16, 128, T], BF16)
    ends_dram = ctx.dscratch("l2a_ends", [32, 32], F32)

    ln_bf = ctx.pool.alloc_fm("l2a_ln", D, T, BF16)
    ln_pass(ctx, x_src, ln_bf, tag=f"{tag}:ln")
    ws_ip = WStream(ctx, "l2aip", weights["ipT"], 512)

    def ep_ip(ctx, ps, pskey, ocg, tt0, tw):
        r = (ocg + tt0 // 512) % 2
        if ocg < 16:
            st = sb.get(f"spill{r}", 128, 512, BF16)
            p.s("copy", st[:, :tw], ps[:, :tw], reads=[pskey],
                writes=[st.name], tag="l2a:zc")
            p.dma("sync", lambda e, st=st, ocg=ocg, tt0=tt0, tw=tw:
                  e.dma_start(out=z_dram[ocg, :, tt0:tt0 + tw],
                              in_=st[:, :tw]),
                  reads=[st.name], writes=[f"l2a_z:{ocg}"], group=st.name,
                  tag="l2a:zd")
        elif ocg < 34:
            st = sb.get(f"spill{r}", 128, 512, BF16)
            p.s("copy", st[:, :tw], ps[:, :tw], reads=[pskey],
                writes=[st.name], tag="l2a:xc")
            p.dma("sync", lambda e, st=st, ocg=ocg, tt0=tt0, tw=tw:
                  e.dma_start(out=xbc_dram[ocg - 16, :, tt0:tt0 + tw],
                              in_=st[:, :tw]),
                  reads=[st.name], writes=[f"l2a_xbc:{ocg - 16}"],
                  group=st.name, tag="l2a:xd")
        elif ocg == 34:
            st = sb.get("spillf0", 128, 512, F32)
            p.v("tensor_copy", st[:, :tw], ps[:, :tw], reads=[pskey],
                writes=[st.name], tag="l2a:dc")
            p.dma("sync", lambda e, st=st, tt0=tt0, tw=tw:
                  e.dma_start(out=dt_dram[:, tt0:tt0 + tw], in_=st[:, :tw]),
                  reads=[st.name], writes=["l2a_dt:0"], group=st.name,
                  tag="l2a:dd")

    gemm_fm(ctx, ln_bf, ws_ip, ep_ip, tag=f"{tag}:ip")

    ctx.pool.reset()
    cw = [col_tile(ctx, f"l2acw{k}", weights[f"cwc{k}"]) for k in range(4)]
    cbt, cbk = col_tile(ctx, "l2acb", weights["cbc"])
    for j in range(18):
        xt, xt_k = ctx.pool.alloc_cols(T) if j == 0 else (xt, xt_k)
        p.load(xt[:, :], xbc_dram[j, :, :], engine="sync",
               reads=[f"l2a_xbc:{j}"], writes=[xt_k], tag="l2a:cl")
        ct, ct_k = ctx.pool.alloc_cols(T) if j == 0 else (ct, ct_k)
        p.v("tensor_scalar", ct[:, :], xt[:, :], cw[3][0][:, j:j + 1],
            cbt[:, j:j + 1], ALU.mult, ALU.add,
            reads=[xt_k, cw[3][1], cbk], writes=[ct_k], tag="l2a:c3")
        for k, sh in ((2, 1), (1, 2), (0, 3)):
            p.v("scalar_tensor_tensor", ct[:, sh:T], xt[:, 0:T - sh],
                cw[k][0][:, j:j + 1], ct[:, sh:T], ALU.mult, ALU.add,
                reads=[xt_k, cw[k][1], ct_k], writes=[ct_k],
                tag=f"l2a:c{k}", lowp=True)
        p.s("activation", ct[:, :], ct[:, :], AF.Silu, reads=[ct_k],
            writes=[ct_k], tag="l2a:csilu")
        p.dma("sync", lambda e, ct=ct, j=j: e.dma_start(
            out=xc_dram[j, :, :], in_=ct[:, :]),
            reads=[ct_k], writes=[f"l2a_xc:{j}"], group="l2a_xcst",
            tag="l2a:cst")

    ctx.pool.reset()
    dtb, dtbk = col_tile(ctx, "l2adtb", weights["dtbc"])
    alog, alogk = col_tile(ctx, "l2aal", weights["alogc"])
    acol = sb.get("l2aacol", 32, 2, F32)
    p.s("activation", acol[:, 0:1], alog[0:32, 0:1], AF.Exp, reads=[alogk],
        writes=[acol.name], tag="l2a:aexp")
    p.v("tensor_scalar", acol[:, 0:1], acol[:, 0:1], -1.0, None, ALU.mult,
        reads=[acol.name], writes=[acol.name], tag="l2a:aneg")
    cumh = [ctx.pool.alloc_cols(T) for _ in range(2)]
    ldth = [ctx.pool.alloc_cols(T) for _ in range(2)]
    Rh = [ctx.pool.alloc_cols(T) for _ in range(2)]
    TH = T // 2
    hand = sb.get("l0hand", 128, 2, F32)
    ends = sb.get("l2aends", 32, 32, F32)
    for hf in range(2):
        cum = cumh[hf][0].bitcast(F32)[0:32, :]
        cum_k = cumh[hf][1]
        ldt = ldth[hf][0].bitcast(F32)[0:32, :]
        ldt_k = ldth[hf][1]
        R = Rh[hf][0].bitcast(F32)[0:32, :]
        R_k = Rh[hf][1]
        p.load(cum[:, :], dt_dram[0:32, hf * TH:(hf + 1) * TH], engine="sync",
               reads=["l2a_dt:0"], writes=[cum_k], tag="l2a:dtl")
        p.v("tensor_scalar", cum[:, :], cum[:, :], dtb[0:32, 0:1], None,
            ALU.add, reads=[cum_k, dtbk], writes=[cum_k], tag="l2a:dtb")
        p.s("activation", R[:, :], cum[:, :], AF.Abs,
            reads=[cum_k], writes=[R_k], tag="l2a:dabs")
        p.s("activation", R[:, :], R[:, :], AF.Exp, scale=-1.0, reads=[R_k],
            writes=[R_k], tag="l2a:dexp")
        p.s("activation", R[:, :], R[:, :], AF.Ln, bias=1.0, reads=[R_k],
            writes=[R_k], tag="l2a:dln")
        p.v("tensor_scalar", cum[:, :], cum[:, :], 0.0, None, ALU.max,
            reads=[cum_k], writes=[cum_k], tag="l2a:dmax")
        p.v("tensor_add", cum[:, :], cum[:, :], R[:, :], reads=[cum_k, R_k],
            writes=[cum_k], tag="l2a:dtf")
        p.s("activation", ldt[:, :], cum[:, :], AF.Ln, reads=[cum_k],
            writes=[ldt_k], tag="l2a:ldt")
        p.v("tensor_scalar", cum[:, :], cum[:, :], acol[:, 0:1], None,
            ALU.mult, reads=[cum_k, acol.name], writes=[cum_k], tag="l2a:la")
        init = 0.0 if hf == 0 else hand[0:32, 0:1]
        rds = [cum_k, "cs_onescol_f"] + ([] if hf == 0 else [hand.name])
        p.v("tensor_tensor_scan", cum[:, :], _ones_bcast(ctx, TH, 32),
            cum[:, :], init, ALU.mult, ALU.add, reads=rds, writes=[cum_k],
            tag="l2a:cums")
        p.v("tensor_copy", hand[0:32, hf:hf + 1], cum[:, TH - 1:TH],
            reads=[cum_k], writes=[hand.name], tag="l2a:hand")
        NC2 = NC // 2
        cview = cum.rearrange("p (c j) -> p c j", j=Q)
        p.v("tensor_tensor", cview[:, 1:NC2, :], cview[:, 1:NC2, :],
            cview[:, 0:NC2 - 1, Q - 1:Q].broadcast_to((32, NC2 - 1, Q)),
            ALU.subtract, reads=[cum_k], writes=[cum_k], tag="l2a:seg")
        if hf == 1:
            p.v("tensor_tensor", cview[:, 0:1, :], cview[:, 0:1, :],
                hand[0:32, 0:1].unsqueeze(1).broadcast_to((32, 1, Q)),
                ALU.subtract, reads=[cum_k, hand.name], writes=[cum_k],
                tag="l2a:seg0")
        p.v("tensor_tensor", R.rearrange("p (c j) -> p c j", j=Q),
            cview[:, :, Q - 1:Q].broadcast_to((32, NC2, Q)),
            cum.rearrange("p (c j) -> p c j", j=Q),
            ALU.subtract, reads=[cum_k], writes=[R_k], tag="l2a:R")
        p.v("tensor_add", R[:, :], R[:, :], ldt[:, :], reads=[R_k, ldt_k],
            writes=[R_k], tag="l2a:Rldt")
        p.v("tensor_copy", ends[:, hf * NC2:(hf + 1) * NC2],
            cum[:, Q - 1::Q] if False else
            cum.rearrange("p (c j) -> p c j", j=Q)[:, :, Q - 1:Q]
            .rearrange("p c j -> p (c j)"),
            reads=[cum_k], writes=[ends.name], tag="l2a:ends")
    p.dma("sync", lambda e: e.dma_start(
        out=ends_dram.rearrange("c h -> h c"), in_=ends[:, :],
        allow_slow_non_contiguous=True),
        reads=[ends.name], writes=["l2a_ends"], group="l2a_endst",
        tag="l2a:endst")
    erowb = sb.get("l2aerowb", 1, 1024, BF16)
    for hb in range(2):
        spf = sb.get("spillf0", 128, 512, F32)
        p.load(spf[0:1, 0:512],
               ends_dram.rearrange("c h -> (c h)")[None, hb * 512:(hb + 1) * 512],
               engine="sync", reads=["l2a_ends"], writes=[spf.name],
               tag="l2a:endl")
        p.s("activation", erowb[0:1, hb * 512:(hb + 1) * 512], spf[0:1, 0:512],
            AF.Exp, reads=[spf.name], writes=[erowb.name], tag="l2a:ee")
    EQbc, EQbc_k = ctx.pool.alloc_cols(1024)
    for hb in range(2):
        pf, pfk = ctx.psum_f()
        p.t_matmul(pf[:, 0:512], onesrow[0:1, :],
                   erowb[0:1, hb * 512:(hb + 1) * 512], True, True,
                   reads=[onesrow_k, erowb.name], writes=[pfk], tag="l2a:eb")
        p.v("tensor_copy", EQbc[:, hb * 512:(hb + 1) * 512], pf[:, 0:512],
            reads=[pfk], writes=[EQbc_k], tag="l2a:ebc", lowp=True)

    Dbc = sb.get("l2aDbc", 128, 32, F32)
    spb = sb.get("spill0", 128, 512, BF16)
    p.load(spb[0:1, 0:32], ctx.din("w_l2aD", weights["D_row"]),
           engine="gpsimd", writes=[spb.name], tag="l2a:Dl")
    pf, pfk = ctx.psum_f()
    p.t_matmul(pf[:, 0:32], onesrow[0:1, :], spb[0:1, 0:32], True, True,
               reads=[onesrow_k, spb.name], writes=[pfk], tag="l2a:Db")
    p.v("tensor_copy", Dbc[:, :], pf[:, 0:32], reads=[pfk], writes=[Dbc.name],
        tag="l2a:Dbc")
    nwbc, nwbc_k = ctx.pool.alloc_cols(DIN)
    nw_dram = ctx.din("w_l2anw", weights["nw_row"])
    for hb in range(4):
        p.load(spb[0:1, 0:512], nw_dram[0:1, hb * 512:(hb + 1) * 512],
               engine="gpsimd", writes=[spb.name], tag="l2a:nwl")
        pf, pfk = ctx.psum_f()
        p.t_matmul(pf[:, 0:512], onesrow[0:1, :], spb[0:1, 0:512], True, True,
                   reads=[onesrow_k, spb.name], writes=[pfk], tag="l2a:nb")
        p.v("tensor_copy", nwbc[:, hb * 512:(hb + 1) * 512], pf[:, 0:512],
            reads=[pfk], writes=[nwbc_k], tag="l2a:nbc")

    Bfm, Bfm_k = ctx.pool.alloc_cols(T)
    Cfm, Cfm_k = ctx.pool.alloc_cols(T)
    p.load(Bfm[:, :], xc_dram[16, :, :], engine="sync", reads=["l2a_xc:16"],
           writes=[Bfm_k], tag="l2a:Bl")
    p.load(Cfm[:, :], xc_dram[17, :, :], engine="sync", reads=["l2a_xc:17"],
           writes=[Cfm_k], tag="l2a:Cl")
    xck = [ctx.pool.alloc_cols(DIN) for _ in range(2)]
    ybuf, ybuf_k = ctx.pool.alloc_cols(DIN)
    Sst, Sst_k = ctx.pool.alloc_cols(DIN)
    p.v("memset", Sst[:, :], 0.0,
        writes=[Sst_k] + [f"l2aS:{h}" for h in range(NH)], tag="l2a:S0")
    mask_l, mask_lk = ctx.const("l2amask",
                                np.triu(np.ones((128, 128))).astype(BF16NP))
    xT, xT_k = ctx.pool.alloc_cols(DIN)
    zT, zT_k = xT, xT_k
    ctok = sb.get("l2actok", 128, 96, F32)
    GT = sb.get("l2aGT", 128, 128, BF16)
    BT = sb.get("l2aBT", 128, 128, BF16)
    NR2 = 3
    Mt_r = [sb.get(f"l2aMt{i}", 128, 128, BF16) for i in range(NR2)]
    Ct_r = [sb.get(f"l2aCt{i}", 128, 128, BF16) for i in range(NR2)]
    Bh_r = [sb.get(f"l2aBh{i}", 128, 128, BF16) for i in range(NR2)]
    ec_r = [sb.get(f"l2aec{i}", 128, 4, F32) for i in range(NR2)]
    onesrow_f, onesrow_f_k = ctx.const("l2aonesrf",
                                       np.ones((1, 128), np.float32))
    sp0f = sb.get("spill0", 128, 512, BF16).bitcast(F32)
    sp1f = sb.get("spill1", 128, 512, BF16).bitcast(F32)
    crow_slots = [(sp0f[0:1, 0:128], "l2acr0", "spill0"),
                  (sp0f[0:1, 128:256], "l2acr1", "spill0"),
                  (sp1f[0:1, 0:128], "l2acr2", "spill1"),
                  (sp1f[0:1, 128:256], "l2acr3", "spill1")]
    Skeys = [f"l2aS:{h}" for h in range(NH)]
    ctx._psf_n = 6
    ytok_t, ytok_k = ctx.pool.alloc_cols(2 * DIN)
    ytok = ytok_t.bitcast(F32)

    for c in range(NC):
        c0 = c * Q
        hf = c // (NC // 2)
        cl0 = (c % (NC // 2)) * Q
        cum = cumh[hf][0].bitcast(F32)[0:32, :]
        cum_k = cumh[hf][1]
        Rl = Rh[hf][0].bitcast(F32)[0:32, :]
        Rl_k = Rh[hf][1]
        ldt = ldth[hf][0].bitcast(F32)[0:32, :]
        ldt_k = ldth[hf][1]
        pf, pfk = ctx.psum_f()
        p.t_transpose(pf[0:128, 0:32], cum[:, cl0:cl0 + Q], identf[0:32, 0:32],
                      reads=[cum_k, identf_k], writes=[pfk], tag="l2a:cT")
        p.v("tensor_copy", ctok[:, 0:32], pf[0:128, 0:32], reads=[pfk],
            writes=[ctok.name], tag="l2a:cTc")
        pf2, pfk2 = ctx.psum_f()
        p.t_transpose(pf2[0:128, 0:32], ldt[:, cl0:cl0 + Q],
                      identf[0:32, 0:32], reads=[ldt_k, identf_k],
                      writes=[pfk2], tag="l2a:lT")
        p.v("tensor_sub", ctok[:, 32:64], ctok[:, 0:32], pf2[0:128, 0:32],
            reads=[ctok.name, pfk2], writes=[ctok.name], tag="l2a:clT")
        pf3, pfk3 = ctx.psum_f()
        p.t_transpose(pf3[0:128, 0:32], Rl[:, cl0:cl0 + Q],
                      identf[0:32, 0:32], reads=[Rl_k, identf_k],
                      writes=[pfk3], tag="l2a:rT")
        p.v("tensor_copy", ctok[:, 64:96], pf3[0:128, 0:32], reads=[pfk3],
            writes=[ctok.name], tag="l2a:rTc")
        pG, pGk = ctx.psum_f()
        p.t_matmul(pG[:, 0:128], Bfm[:, c0:c0 + Q], Cfm[:, c0:c0 + Q], True,
                   True, reads=[Bfm_k, Cfm_k], writes=[pGk], tag="l2a:G")
        p.v("tensor_mul", GT[:, :], pG[:, 0:128], mask_l[:, :],
            reads=[pGk, mask_lk], writes=[GT.name], tag="l2a:Gc", lowp=True)
        pB, pBk = ctx.psum_b()
        p.t_transpose(pB[:, 0:128], Bfm[:, c0:c0 + Q], identb[:, :],
                      reads=[Bfm_k, identb_k], writes=[pBk], tag="l2a:BT")
        p.v("tensor_copy", BT[:, :], pB[:, 0:128], reads=[pBk],
            writes=[BT.name], tag="l2a:BTc")
        xca, xck_key = xck[c % 2]
        src = xc_dram[0:16, :, c0:c0 + Q].rearrange("j p t -> p j t")
        p.dma("sync", lambda e, xca=xca, src=src: e.dma_start(
            out=xca.rearrange("p (j t) -> p j t", t=Q), in_=src),
            reads=[f"l2a_xc:{j}" for j in range(16)], writes=[xck_key],
            group=xck_key, tag="l2a:xl")
        for j in range(16):
            pb, pbk = ctx.psum_b()
            p.t_transpose(pb[:, 0:128], xca[:, j * 128:(j + 1) * 128],
                          identb[:, :], reads=[xck_key, identb_k],
                          writes=[pbk], tag="l2a:xT")
            p.s("copy", xT[:, j * 128:(j + 1) * 128], pb[:, 0:128],
                reads=[pbk], writes=[xT_k], tag="l2a:xTc")
        for h in range(NH):
            Mt = Mt_r[h % NR2]
            Ct_s = Ct_r[h % NR2]
            Bh_s = Bh_r[h % NR2]
            ecol = ec_r[h % NR2]
            crow, crk, crtens = crow_slots[h % 4]
            wl = [crk, crtens] if (c == 0 and h < 4) else [crk]
            p.dma("gpsimd", lambda e, crow=crow, h=h, cl0=cl0, cum=cum:
                  e.dma_start(out=crow, in_=cum[h:h + 1, cl0:cl0 + Q]),
                  reads=[cum_k], writes=wl, group=crk, tag="l2a:crd")
            pb2, pb2k = ctx.psum_f()
            p.t_matmul(pb2[:, 0:128], onesrow_f[0:1, :], crow, True, True,
                       reads=[onesrow_f_k, crk],
                       writes=[pb2k], tag="l2a:hb")
            p.v("tensor_scalar", Mt[:, :], pb2[:, 0:128],
                ctok[:, 32 + h:32 + h + 1], 0.0, ALU.subtract, ALU.min,
                reads=[pb2k, ctok.name], writes=[Mt.name], tag="l2a:Me")
            p.s("activation", Mt[:, :], Mt[:, :], AF.Exp, reads=[Mt.name],
                writes=[Mt.name], tag="l2a:Mx")
            p.v("tensor_mul", Mt[:, :], Mt[:, :], GT[:, :],
                reads=[Mt.name, GT.name], writes=[Mt.name], tag="l2a:MG",
                lowp=True)
            p.s("activation", Ct_s[:, :], pb2[:, 0:128], AF.Exp,
                reads=[pb2k], writes=[Ct_s.name], tag="l2a:Ce")
            p.v("tensor_mul", Ct_s[:, :], Ct_s[:, :], Cfm[:, c0:c0 + Q],
                reads=[Ct_s.name, Cfm_k], writes=[Ct_s.name], tag="l2a:Cm",
                lowp=True)
            p.s("activation", ecol[:, 0:1], ctok[:, 64 + h:64 + h + 1],
                AF.Exp, reads=[ctok.name], writes=[ecol.name], tag="l2a:be")
            p.v("tensor_scalar", Bh_s[:, :], BT[:, :], ecol[:, 0:1], None,
                ALU.mult, reads=[BT.name, ecol.name], writes=[Bh_s.name],
                tag="l2a:bm", lowp=True)
            pY, pYk = ctx.psum_f()
            p.t_matmul(pY[:, 0:HD], Mt[:, :], xT[:, h * 64:(h + 1) * 64],
                       True, False, reads=[Mt.name, xT_k], writes=[pYk],
                       tag="l2a:Yi")
            p.t_matmul(pY[:, 0:HD], Ct_s[:, :], Sst[:, h * 64:(h + 1) * 64],
                       False, True, reads=[Ct_s.name, Skeys[h]],
                       writes=[pYk], tag="l2a:Ys")
            p.v("scalar_tensor_tensor", ytok[:, h * 64:(h + 1) * 64],
                xT[:, h * 64:(h + 1) * 64], Dbc[:, h:h + 1], pY[:, 0:HD],
                ALU.mult, ALU.add, reads=[xT_k, Dbc.name, pYk],
                writes=[ytok_k], tag="l2a:Yd")
            pU, pUk = ctx.psum_f()
            p.t_matmul(pU[:, 0:HD], Bh_s[:, :], xT[:, h * 64:(h + 1) * 64],
                       True, True, reads=[Bh_s.name, xT_k], writes=[pUk],
                       tag="l2a:U")
            p.v("scalar_tensor_tensor", Sst[:, h * 64:(h + 1) * 64],
                Sst[:, h * 64:(h + 1) * 64], EQbc[:, c * 32 + h:c * 32 + h + 1],
                pU[:, 0:HD], ALU.mult, ALU.add,
                reads=[Skeys[h], EQbc_k, pUk], writes=[Skeys[h]],
                tag="l2a:Su", lowp=True)
        for j in range(16):
            zc = sb.get("l2azc", 128, 128, BF16)
            p.load(zc[:, :], z_dram[j, :, c0:c0 + Q], engine="gpsimd",
                   reads=[f"l2a_z:{j}"], writes=[zc.name], tag="l2a:zl")
            pb, pbk = ctx.psum_b()
            p.t_transpose(pb[:, 0:128], zc[:, :], identb[:, :],
                          reads=[zc.name, identb_k], writes=[pbk],
                          tag="l2a:zT")
            p.s("copy", zT[:, j * 128:(j + 1) * 128], pb[:, 0:128],
                reads=[pbk], writes=[zT_k], tag="l2a:zTc")
        p.s("activation", zT[:, :], zT[:, :], AF.Silu, reads=[zT_k],
            writes=[zT_k], tag="l2a:zs")
        p.v("tensor_mul", ytok[:, :], ytok[:, :], zT[:, :],
            reads=[ytok_k, zT_k], writes=[ytok_k], tag="l2a:yz")
        st = sb.get("lnst0", 128, 8, F32)
        p.s("activation", zT[:, :], ytok[:, :], AF.Square,
            accum_out=st[:, 0:1], reads=[ytok_k],
            writes=[zT_k, st.name], tag="l2a:ysq")
        p.v("tensor_scalar", st[:, 1:2], st[:, 0:1], 1.0 / DIN, 1e-5,
            ALU.mult, ALU.add, reads=[st.name], writes=[st.name],
            tag="l2a:yv")
        p.s("activation", st[:, 2:3], st[:, 1:2], AF.Sqrt, reads=[st.name],
            writes=[st.name], tag="l2a:ysd")
        p.v("reciprocal", st[:, 3:4], st[:, 2:3], reads=[st.name],
            writes=[st.name], tag="l2a:yrs")
        p.v("tensor_scalar", ytok[:, :], ytok[:, :], st[:, 3:4], None,
            ALU.mult, reads=[ytok_k, st.name], writes=[ytok_k],
            tag="l2a:yn")
        p.v("tensor_mul", ytok[:, :], ytok[:, :], nwbc[:, :],
            reads=[ytok_k, nwbc_k], writes=[ytok_k], tag="l2a:ynw")
        for j in range(16):
            zc = sb.get("l2azc2", 128, 128, BF16)
            p.v("tensor_copy", zc[:, :], ytok[:, j * 128:(j + 1) * 128],
                reads=[ytok_k], writes=[zc.name], tag="l2a:ycst")
            pb, pbk = ctx.psum_b()
            p.t_transpose(pb[:, 0:128], zc[:, :], identb[:, :],
                          reads=[zc.name, identb_k], writes=[pbk],
                          tag="l2a:yT")
            p.s("copy", ybuf[:, j * 128:(j + 1) * 128], pb[:, 0:128],
                reads=[pbk], writes=[ybuf_k], tag="l2a:yTc")
        dsty = y_dram[0:16, :, c0:c0 + Q].rearrange("j p t -> p j t")
        p.dma("sync", lambda e, dsty=dsty, ybuf=ybuf: e.dma_start(
            out=dsty, in_=ybuf.rearrange("p (j t) -> p j t", t=Q)),
            reads=[ybuf_k], writes=[f"l2a_yd:{c}"], group=ybuf_k,
            tag="l2a:yst")

    p.v("tensor_copy", Sst[0:1, 0:1], Sst[0:1, 0:1], reads=Skeys,
        writes=[Sst_k], tag="l2a:Sbar")
    p.v("tensor_copy", sp0f[0:1, 0:1], sp0f[0:1, 0:1],
        reads=["l2acr0", "l2acr1", "l2acr2", "l2acr3"],
        writes=["spill0", "spill1"], tag="l2a:crbar")
    ctx._psf_n = 4

    ctx.pool.reset()
    ws_o = WStream(ctx, "l2aop", weights["opT"], 256)
    y = ctx.pool.alloc_fm("ytt", D, 512, BF16)
    ybk = [ctx.pool.alloc_fm(f"l2a_yb{i}", DIN, 512, BF16) for i in range(2)]

    def ep_o(ctx, ps, pskey, ocg, tt0, tw):
        p.s("copy", y.chunks[ocg][:, 0:tw], ps[:, :tw], reads=[pskey],
            writes=[y.key(ocg)], tag="l2a:op")

    for tt in range(T // 512):
        tt0 = tt * 512
        yb = ybk[tt % 2]
        for j in range(16):
            p.load(yb.chunks[j][:, 0:512], y_dram[j, :, tt0:tt0 + 512],
                   engine="sync",
                   reads=[f"l2a_yd:{cc}" for cc in range(tt * 4, tt * 4 + 4)],
                   writes=[yb.key(j)], tag="l2a:ybl")
        gemm_fm(ctx, yb, ws_o, ep_o, t0=0, tn=512, toff=tt0, tag=f"{tag}:op")
        residual_add(ctx, x_src, x_dst, y, tt0, 512, scale=1.0,
                     tag=f"{tag}:res")


def prep_l2_attn(inp):
    ip = inp["L2_attn_in_proj_w"]
    ipp = np.zeros((4608, 1024), np.float32)
    ipp[:4384] = ip
    cw = inp["L2_attn_conv_w"]
    out = {
        "ipT": np.ascontiguousarray(ipp.T),
        "opT": np.ascontiguousarray(inp["L2_attn_out_proj_w"].T),
        "dtbc": np.ascontiguousarray(
            np.pad(inp["L2_attn_dt_bias"], (0, 96)).reshape(-1, 128).T),
        "alogc": np.ascontiguousarray(
            np.pad(inp["L2_attn_A_log"], (0, 96)).reshape(-1, 128).T),
        "cbc": np.ascontiguousarray(
            np.pad(inp["L2_attn_conv_b"], (0, 0)).reshape(-1, 128).T),
        "D_row": inp["L2_attn_D"][None, :].astype(BF16NP),
        "nw_row": inp["L2_attn_norm_w"][None, :].astype(BF16NP),
    }
    for k in range(4):
        out[f"cwc{k}"] = np.ascontiguousarray(cw[:, k].reshape(-1, 128).T)
    return out



def prep_l0_ffn(inp):
    return {
        "wgT": np.ascontiguousarray(inp["L0_ffn_w_gate"].T),
        "wuT": np.ascontiguousarray(inp["L0_ffn_w_up"].T),
        "wdT": np.ascontiguousarray(inp["L0_ffn_w_down"].T),
    }


SUBLAYER_DEFS = {
    "l0_ffn": (emit_l0_ffn, prep_l0_ffn),
    "l1_ffn": (emit_l1_ffn, prep_l1_ffn),
    "l0_attn": (emit_l0_attn, prep_l0_attn),
    "l1_attn": (emit_l1_attn, prep_l1_attn),
    "l2_attn": (emit_l2_attn, prep_l2_attn),
    "l2_ffn": (emit_l2_ffn, prep_l2_ffn),
    "l3_attn": (emit_l3_attn, prep_l3_attn),
    "l3_ffn": (emit_l3_ffn, prep_l3_ffn),
}

MODULE_STEPS = [
    ("l0_ffn", "s0"), ("l0_attn", "s1"), ("l0_ffn", "s2"),
    ("l1_ffn", "s3"), ("l1_attn", "s4"), ("l1_ffn", "s5"),
    ("l2_ffn", "s6"), ("l2_attn", "s7"), ("l2_ffn", "s8"),
    ("l3_ffn", "s9"), ("l3_attn", "s10"), ("l3_ffn", "s11"),
]


def build_sublayer_program(step_names, weights_all, serial=False):
    nc = bass.Bass("TRN2", target_bir_lowering=False, debug=False)
    ctx = Ctx(nc, serial=serial)
    x_in = nc.dram_tensor("x_in", [T, D], F32, kind="ExternalInput").ap()
    x_out = nc.dram_tensor("x_out", [T, D], F32, kind="ExternalOutput").ap()
    n = len(step_names)
    if n > 1:
        x_mid = ctx.dscratch("x_mid", [T, D], F32)
    for si, name in enumerate(step_names):
        emit, _ = SUBLAYER_DEFS[name]
        src = x_in if si == 0 else x_mid
        dst = x_out if si == n - 1 else x_mid
        emit(ctx, src, dst, weights_all[name], tag=f"s{si}")
    ctx.p.op("sync", lambda e: e.nop(),
             reads=[f"xres:{i}" for i in range(T // 128)], tag="finish")
    ctx.p.build()
    return nc, ctx


_prog_cache = {}


def _get_program(key, builder):
    if key not in _prog_cache:
        _prog_cache[key] = builder()
    return _prog_cache[key]


def _emit_module(ctx, nc):
    x_in = nc.dram_tensor("x_in", [T, D], F32, kind="ExternalInput").ap()
    x_out = nc.dram_tensor("x_out", [T, D], F32, kind="ExternalOutput").ap()
    x_mid = ctx.dscratch("x_mid", [T, D], F32)
    n = len(MODULE_STEPS)
    for si, (name, tag) in enumerate(MODULE_STEPS):
        emit, _ = SUBLAYER_DEFS[name]
        src = x_in if si == 0 else x_mid
        dst = x_out if si == n - 1 else x_mid
        emit(ctx, src, dst, ctx._weights_all[name], tag=tag)
    ctx.p.op("sync", lambda e: e.nop(),
             reads=[f"xres:{i}" for i in range(T // 128)], tag="finish")


def build_fused_program(weights_all, serial=False):
    nc1 = bass.Bass("TRN2", target_bir_lowering=False, debug=False)
    ctx1 = Ctx(nc1)
    ctx1._weights_all = weights_all
    _emit_module(ctx1, nc1)
    nc = bass.Bass("TRN2", target_bir_lowering=False, debug=False)
    ctx = Ctx(nc, serial=serial)
    ctx._weights_all = weights_all
    ctx.const_group = "consts"
    for kind, name, arr, dtype in ctx1.const_log:
        if kind == "const":
            ctx.const(name, arr, dtype)
        else:
            col_tile(ctx, name, arr)
    ctx.pool.reset()
    _emit_module(ctx, nc)
    for kind, name, arr, dtype in ctx.const_log[len(ctx1.const_log):]:
        raise AssertionError(f"straggler const {name}")
    ctx.p.build()
    return nc, ctx


def kernel(**inputs):
    B = 8
    x_full = np.ascontiguousarray(inputs["L0_attn_x"], dtype=np.float32)

    weights_all = {}
    for name, (_, prep) in SUBLAYER_DEFS.items():
        weights_all[name] = prep({k: np.asarray(v) for k, v in inputs.items()
                                  if hasattr(v, 'shape')})

    nc, ctx = _get_program(("fused",), lambda: build_fused_program(weights_all))
    in_maps = []
    for c in range(B):
        m = dict(ctx.host_inputs)
        m["x_in"] = np.ascontiguousarray(x_full[c])
        in_maps.append(m)
    res = run_bass_kernel_spmd(nc, in_maps, core_ids=list(range(B)))
    return np.stack([np.asarray(res.results[c]["x_out"], dtype=np.float32)
                     for c in range(B)], axis=0)




